# version 08 (final)
# speedup vs baseline: 1.0962x; 1.0258x over previous
import numpy as np
from contextlib import ExitStack
import concourse.bass as bass
import concourse.mybir as mybir
from concourse.bass_utils import run_bass_kernel_spmd

F32 = mybir.dt.float32
BF16 = mybir.dt.bfloat16
AF = mybir.ActivationFunctionType
ALU = mybir.AluOpType
AX = mybir.AxisListType

NCORES = 8
D = 1024
NC_D = D // 128

BIG = 1.0e30
IBIG = 100.0



class SemW:
    __slots__ = ("h", "count")

    def __init__(self, h):
        self.h = h
        self.count = 0


class Prog:
    ENGINES = ("sync", "scalar", "vector", "tensor", "gpsimd")

    def __init__(self, nc):
        self.nc = nc
        self.es = ExitStack()
        self.q = {e: [] for e in self.ENGINES}
        self._semn = {}
        self._blk_es = None

    def sem(self, name):
        h = self.es.enter_context(self.nc.semaphore(name))
        return SemW(h)

    def n(self, s):
        return s.count

    def sb(self, name, shape, dt=F32):
        tgt = self._blk_es if self._blk_es is not None else self.es
        return tgt.enter_context(self.nc.sbuf_tensor(name, shape, dt,
                                                     allow_name_mangling=True))

    def ps(self, name, shape, dt=F32):
        tgt = self._blk_es if self._blk_es is not None else self.es
        return tgt.enter_context(self.nc.psum_tensor(name, shape, dt,
                                                     allow_name_mangling=True))

    def block_begin(self):
        self._blk_es = ExitStack()

    def block_end(self, sems):
        self.fence(sems)
        self._blk_es.close()
        self._blk_es = None

    def emit(self, eng, fn, waits=(), inc=None, inc_by=1):
        waits = tuple((s, v) for (s, v) in waits)
        if inc is not None:
            inc.count += inc_by

        def play(e, waits=waits, fn=fn, inc=inc, inc_by=inc_by):
            for s, v in waits:
                if v > 0:
                    e.wait_ge(s.h, v)
            r = fn(e)
            if inc is not None:
                r.then_inc(inc.h, inc_by)
        self.q[eng].append(play)

    def barrier(self, eng, waits):
        self.emit(eng, lambda e: None, waits=waits)

    def scope(self):
        return ExitStack()

    def sb_s(self, es, name, shape, dt=F32):
        return es.enter_context(self.nc.sbuf_tensor(name, shape, dt,
                                                    allow_name_mangling=True))

    def ps_s(self, es, name, shape, dt=F32):
        return es.enter_context(self.nc.psum_tensor(name, shape, dt,
                                                    allow_name_mangling=True))

    def fence(self, sems):
        for e in self.ENGINES:
            self.barrier(e, [(s, self.n(s)) for s in sems])

    def finalize(self):
        with self.nc.Block() as block:
            for e in self.ENGINES:
                if self.q[e]:
                    getattr(block, e)(lambda eng, e=e: [f(eng) for f in self.q[e]])
        self.es.close()


def build_nc():
    return bass.Bass("TRN2", target_bir_lowering=False)


def host_consts():
    return {
        "c_ident": np.eye(128, dtype=np.float32),
        "c_iota8": np.broadcast_to(np.arange(8, dtype=np.float32), (128, 8)).copy(),
        "c_ones1": np.ones((1, 128), dtype=np.float32),
        "c_onescol": np.ones((128, 1), dtype=np.float32),
        "c_iotacol": np.arange(128, dtype=np.float32).reshape(128, 1),
    }


class Consts:

    def __init__(self, prog, s_ld, s_v):
        nc = prog.nc
        self.identf = prog.sb("cs_identf", [128, 128], F32)
        self.ident = prog.sb("cs_ident", [128, 128], BF16)
        self.iota8 = prog.sb("cs_iota8", [128, 8], F32)
        self.ones1f = prog.sb("cs_ones1f", [1, 128], F32)
        self.ones1 = prog.sb("cs_ones1", [1, 128], BF16)
        self.onescolf = prog.sb("cs_onescolf", [128, 1], F32)
        self.onescol = prog.sb("cs_onescol", [128, 1], BF16)
        self.iotacol = prog.sb("cs_iotacol", [128, 1], F32)
        ident_in = nc.dram_tensor("c_ident", [128, 128], F32, kind="ExternalInput").ap()
        iota_in = nc.dram_tensor("c_iota8", [128, 8], F32, kind="ExternalInput").ap()
        ones_in = nc.dram_tensor("c_ones1", [1, 128], F32, kind="ExternalInput").ap()
        onesc_in = nc.dram_tensor("c_onescol", [128, 1], F32, kind="ExternalInput").ap()
        iotac_in = nc.dram_tensor("c_iotacol", [128, 1], F32, kind="ExternalInput").ap()
        prog.emit("sync", lambda e: e.dma_start(out=self.identf[:], in_=ident_in), inc=s_ld, inc_by=16)
        prog.emit("sync", lambda e: e.dma_start(out=self.iota8[:], in_=iota_in), inc=s_ld, inc_by=16)
        prog.emit("sync", lambda e: e.dma_start(out=self.ones1f[:], in_=ones_in), inc=s_ld, inc_by=16)
        prog.emit("sync", lambda e: e.dma_start(out=self.onescolf[:], in_=onesc_in), inc=s_ld, inc_by=16)
        prog.emit("sync", lambda e: e.dma_start(out=self.iotacol[:], in_=iotac_in), inc=s_ld, inc_by=16)
        w = prog.n(s_ld)
        prog.emit("vector", lambda e: e.tensor_copy(out=self.ident[:], in_=self.identf[:]),
                  waits=[(s_ld, w)], inc=s_v)
        prog.emit("vector", lambda e: e.tensor_copy(out=self.ones1[:], in_=self.ones1f[:]),
                  waits=[(s_ld, w)], inc=s_v)
        prog.emit("vector", lambda e: e.tensor_copy(out=self.onescol[:], in_=self.onescolf[:]),
                  waits=[(s_ld, w)], inc=s_v)
        self.ready_v = prog.n(s_v)
        self.sem_v = s_v



def declare_weights(nc, prefix, spec):
    return {k: nc.dram_tensor(f"{prefix}_{k}", list(s), F32, kind="ExternalInput").ap()
            for k, s in spec.items()}


def emit_weight_convert(prog, prefix, aps, names, s_cvt):
    out = {}
    for k in names:
        ap = aps[k]
        b = prog.nc.dram_tensor(f"{prefix}_{k}_b16", list(ap.shape), BF16, kind="Internal").ap()
        prog.emit("gpsimd", lambda e, b=b, ap=ap: e.dma_start(out=b, in_=ap),
                  inc=s_cvt, inc_by=16)
        out[k] = b
    return out, prog.n(s_cvt)


def emit_load_cast_bf16(prog, out_sb_ap, in_dram_ap, s_lw, waits=()):
    prog.emit("gpsimd", lambda e: e.dma_start(out=out_sb_ap, in_=in_dram_ap),
              waits=waits, inc=s_lw, inc_by=16)


def emit_x_ptile_prep(prog, x_dram, t0, x_tm, x_tmb, x_fm, fm_t_off, FMW,
                      ps_tp, s_ld, s_v, s_mm, dchunks=None):
    prog.emit("sync", lambda e: e.dma_start(out=x_tm[:, 0:D], in_=x_dram[t0:t0 + 128, :]),
              waits=[(s_v, prog.n(s_v))], inc=s_ld, inc_by=16)
    ld_done = prog.n(s_ld)
    prog.emit("vector", lambda e: e.tensor_copy(out=x_tmb[:], in_=x_tm[:, 0:D]),
              waits=[(s_ld, ld_done)], inc=s_v)
    for c in (dchunks if dchunks is not None else range(NC_D)):
        def tp(e, c=c):
            return e.transpose(ps_tp[:], x_tmb[:, c * 128: c * 128 + 128],
                               prog.consts.ident[:])
        prog.emit("tensor", tp,
                  waits=[(s_v, prog.n(s_v))],
                  inc=s_mm)
        mm_done = prog.n(s_mm)
        def ev(e, c=c, toff=fm_t_off):
            return e.tensor_copy(out=x_fm[:, c * FMW + toff: c * FMW + toff + 128],
                                 in_=ps_tp[:])
        prog.emit("vector", ev, waits=[(s_mm, mm_done)], inc=s_v)
    return {"tm_f32": ld_done, "fm_done_v": prog.n(s_v)}


def emit_ln_epilogue(prog, x_tm_ap, acc_ap, out_dram_slice, scratch, s_v, s_s, s_st,
                     waits_v=(), eps=1e-5, Dw=D):
    sq, stt = scratch["sq"], scratch["st"]
    prog.emit("vector", lambda e: e.tensor_tensor(out=sq[:, 0:Dw], in0=x_tm_ap, in1=acc_ap, op=ALU.add),
              waits=list(waits_v), inc=s_v)
    y_done = prog.n(s_v)
    prog.emit("scalar", lambda e: e.activation(out=sq[:, Dw:2 * Dw], in_=sq[:, 0:Dw], func=AF.Square,
                                               accum_out=stt[:, 1:2]),
              waits=[(s_v, y_done)], inc=s_s)
    prog.emit("vector", lambda e: e.reduce_sum(out=stt[:, 0:1], in_=sq[:, 0:Dw], axis=AX.X),
              waits=[(s_v, y_done)], inc=s_v)
    prog.emit("vector", lambda e: e.tensor_scalar(out=stt[:, 2:4], in0=stt[:, 0:2], scalar1=1.0 / Dw,
                                                  scalar2=None, op0=ALU.mult),
              waits=[(s_s, prog.n(s_s)), (s_v, prog.n(s_v))], inc=s_v)
    prog.emit("vector", lambda e: e.tensor_tensor(out=stt[:, 4:5], in0=stt[:, 2:3], in1=stt[:, 2:3], op=ALU.mult),
              waits=[(s_v, prog.n(s_v))], inc=s_v)
    prog.emit("vector", lambda e: e.tensor_tensor(out=stt[:, 5:6], in0=stt[:, 3:4], in1=stt[:, 4:5], op=ALU.subtract),
              waits=[(s_v, prog.n(s_v))], inc=s_v)
    prog.emit("vector", lambda e: e.tensor_scalar(out=stt[:, 5:6], in0=stt[:, 5:6], scalar1=eps,
                                                  scalar2=None, op0=ALU.add),
              waits=[(s_v, prog.n(s_v))], inc=s_v)
    prog.emit("vector", lambda e: e.reciprocal(out=stt[:, 6:7], in_=stt[:, 5:6]),
              waits=[(s_v, prog.n(s_v))], inc=s_v)
    rec_done = prog.n(s_v)
    prog.emit("scalar", lambda e: e.activation(out=stt[:, 7:8], in_=stt[:, 6:7], func=AF.Sqrt),
              waits=[(s_v, rec_done)], inc=s_s)
    prog.emit("vector", lambda e: e.tensor_scalar(out=stt[:, 4:5], in0=stt[:, 2:3], scalar1=-1.0,
                                                  scalar2=None, op0=ALU.mult),
              waits=[(s_v, prog.n(s_v)), (s_s, prog.n(s_s))], inc=s_v)
    prog.emit("vector", lambda e: e.tensor_scalar(out=sq[:, 0:Dw], in0=sq[:, 0:Dw], scalar1=stt[:, 4:5],
                                                  scalar2=stt[:, 7:8], op0=ALU.add, op1=ALU.mult),
              waits=[(s_v, prog.n(s_v))], inc=s_v)
    norm_done = prog.n(s_v)
    prog.emit("sync", lambda e: e.dma_start(out=out_dram_slice, in_=sq[:, 0:Dw]),
              waits=[(s_v, norm_done)], inc=s_st, inc_by=16)
    return prog.n(s_st)



def emit_softmax_top2(prog, ps_g, gsc, cw_ap, gmm, s_v, s_s, s_mm, consts, renorm=False):
    prog.emit("vector", lambda e: e.reduce_max(
        out=gsc[:, 32:33], in_=ps_g[:], axis=AX.X, negate=True),
        waits=[(s_mm, gmm), (s_v, prog.n(s_v))], inc=s_v)
    prog.emit("scalar", lambda e: e.activation(
        out=gsc[:, 8:16], in_=ps_g[:], func=AF.Exp, bias=gsc[:, 32:33],
        accum_out=gsc[:, 33:34]),
        waits=[(s_v, prog.n(s_v)), (s_s, prog.n(s_s))], inc=s_s)
    prog.emit("vector", lambda e: e.reciprocal(out=gsc[:, 34:35], in_=gsc[:, 33:34]),
              waits=[(s_s, prog.n(s_s)), (s_v, prog.n(s_v))], inc=s_v)
    prog.emit("vector", lambda e: e.tensor_scalar(
        out=gsc[:, 8:16], in0=gsc[:, 8:16], scalar1=gsc[:, 34:35], scalar2=None, op0=ALU.mult),
        waits=[(s_v, prog.n(s_v))], inc=s_v)
    prog.emit("vector", lambda e: e.reduce_max(out=gsc[:, 35:36], in_=gsc[:, 8:16], axis=AX.X),
              waits=[(s_v, prog.n(s_v))], inc=s_v)
    prog.emit("vector", lambda e: e.tensor_scalar(
        out=gsc[:, 0:8], in0=gsc[:, 8:16], scalar1=gsc[:, 35:36], scalar2=None, op0=ALU.is_equal),
        waits=[(s_v, prog.n(s_v))], inc=s_v)
    prog.emit("vector", lambda e: e.tensor_scalar(
        out=gsc[:, 24:32], in0=consts.iota8[:], scalar1=-IBIG, scalar2=None, op0=ALU.add),
        waits=[(s_v, prog.n(s_v))], inc=s_v)
    prog.emit("vector", lambda e: e.tensor_tensor(
        out=gsc[:, 0:8], in0=gsc[:, 0:8], in1=gsc[:, 24:32], op=ALU.mult),
        waits=[(s_v, prog.n(s_v))], inc=s_v)
    prog.emit("vector", lambda e: e.tensor_scalar(
        out=gsc[:, 0:8], in0=gsc[:, 0:8], scalar1=IBIG, scalar2=None, op0=ALU.add),
        waits=[(s_v, prog.n(s_v))], inc=s_v)
    prog.emit("vector", lambda e: e.tensor_reduce(
        out=gsc[:, 36:37], in_=gsc[:, 0:8], axis=AX.X, op=ALU.min),
        waits=[(s_v, prog.n(s_v))], inc=s_v)
    prog.emit("vector", lambda e: e.tensor_scalar(
        out=gsc[:, 16:24], in0=consts.iota8[:], scalar1=gsc[:, 36:37], scalar2=None, op0=ALU.is_equal),
        waits=[(s_v, prog.n(s_v))], inc=s_v)
    prog.emit("vector", lambda e: e.tensor_scalar(
        out=gsc[:, 0:8], in0=gsc[:, 16:24], scalar1=-BIG, scalar2=None, op0=ALU.mult),
        waits=[(s_v, prog.n(s_v))], inc=s_v)
    prog.emit("vector", lambda e: e.tensor_tensor(
        out=gsc[:, 0:8], in0=gsc[:, 0:8], in1=gsc[:, 8:16], op=ALU.add),
        waits=[(s_v, prog.n(s_v))], inc=s_v)
    prog.emit("vector", lambda e: e.reduce_max(out=gsc[:, 37:38], in_=gsc[:, 0:8], axis=AX.X),
              waits=[(s_v, prog.n(s_v))], inc=s_v)
    prog.emit("vector", lambda e: e.tensor_scalar(
        out=gsc[:, 0:8], in0=gsc[:, 0:8], scalar1=gsc[:, 37:38], scalar2=None, op0=ALU.is_equal),
        waits=[(s_v, prog.n(s_v))], inc=s_v)
    prog.emit("vector", lambda e: e.tensor_tensor(
        out=gsc[:, 0:8], in0=gsc[:, 0:8], in1=gsc[:, 24:32], op=ALU.mult),
        waits=[(s_v, prog.n(s_v))], inc=s_v)
    prog.emit("vector", lambda e: e.tensor_scalar(
        out=gsc[:, 0:8], in0=gsc[:, 0:8], scalar1=IBIG, scalar2=None, op0=ALU.add),
        waits=[(s_v, prog.n(s_v))], inc=s_v)
    prog.emit("vector", lambda e: e.tensor_reduce(
        out=gsc[:, 38:39], in_=gsc[:, 0:8], axis=AX.X, op=ALU.min),
        waits=[(s_v, prog.n(s_v))], inc=s_v)
    if renorm:
        prog.emit("vector", lambda e: e.tensor_tensor(
            out=gsc[:, 39:40], in0=gsc[:, 35:36], in1=gsc[:, 37:38], op=ALU.add),
            waits=[(s_v, prog.n(s_v))], inc=s_v)
        prog.emit("vector", lambda e: e.tensor_scalar(
            out=gsc[:, 39:40], in0=gsc[:, 39:40], scalar1=1e-12, scalar2=None, op0=ALU.max),
            waits=[(s_v, prog.n(s_v))], inc=s_v)
        prog.emit("vector", lambda e: e.reciprocal(out=gsc[:, 39:40], in_=gsc[:, 39:40]),
                  waits=[(s_v, prog.n(s_v))], inc=s_v)
        prog.emit("vector", lambda e: e.tensor_scalar(
            out=gsc[:, 35:36], in0=gsc[:, 35:36], scalar1=gsc[:, 39:40], scalar2=None, op0=ALU.mult),
            waits=[(s_v, prog.n(s_v))], inc=s_v)
        prog.emit("vector", lambda e: e.tensor_scalar(
            out=gsc[:, 37:38], in0=gsc[:, 37:38], scalar1=gsc[:, 39:40], scalar2=None, op0=ALU.mult),
            waits=[(s_v, prog.n(s_v))], inc=s_v)
    if cw_ap is not None:
        prog.emit("vector", lambda e: e.tensor_scalar(
            out=gsc[:, 24:32], in0=consts.iota8[:], scalar1=gsc[:, 38:39], scalar2=None, op0=ALU.is_equal),
            waits=[(s_v, prog.n(s_v))], inc=s_v)
        prog.emit("vector", lambda e: e.tensor_scalar(
            out=gsc[:, 16:24], in0=gsc[:, 16:24], scalar1=gsc[:, 35:36], scalar2=None, op0=ALU.mult),
            waits=[(s_v, prog.n(s_v))], inc=s_v)
        prog.emit("vector", lambda e: e.scalar_tensor_tensor(
            out=cw_ap, in0=gsc[:, 24:32], scalar=gsc[:, 37:38], in1=gsc[:, 16:24],
            op0=ALU.mult, op1=ALU.add),
            waits=[(s_v, prog.n(s_v))], inc=s_v)



def emit_l3ffn(prog, C, x_dram, y_dram, W):
    T, ST = C["T"], C["ST"]
    NS = T // ST
    ntt = ST // 512
    E, H = 8, 2048
    KC1, HC = NC_D, H // 128
    KC2 = H // 128

    s_ld = prog.sem("f3_ld"); s_lx = prog.sem("f3_lx"); s_lwi = prog.sem("f3_lwi")
    s_lwp = [prog.sem("f3_lw0"), prog.sem("f3_lw1")]
    s_v = prog.sem("f3_v")
    s_s = prog.sem("f3_s"); s_mm = prog.sem("f3_mm"); s_st = prog.sem("f3_st")
    prog.block_begin()

    consts = prog.consts

    x_tm = prog.sb("f3_x_tm", [128, D], F32)
    x_tmb = prog.sb("f3_x_tmb", [128, D], BF16)
    x_fm = prog.sb("f3_x_fm", [128, NC_D * ST], BF16)
    h_fm = prog.sb("f3_h_fm", [128, HC * 512], BF16)
    acc = prog.sb("f3_acc", [128, (ST // 128) * D], F32)
    cw = prog.sb("f3_cw", [128, (ST // 128) * 8], F32)
    gsc = prog.sb("f3_gsc", [128, 40], F32)
    lnsq = prog.sb("f3_lnsq", [128, 2 * D], F32)
    lnst = prog.sb("f3_lnst", [128, 8], F32)
    wrb = prog.sb("f3_wrb", [128, NC_D * 8], BF16)
    brb = prog.sb("f3_brb", [1, 8], BF16)
    b2b = prog.sb("f3_b2b", [1, E * D], BF16)
    b1c = prog.sb("f3_b1c", [128, E * HC], F32)
    w1b = [prog.sb(f"f3_w1b{i}", [128, KC1 * H], BF16) for i in range(2)]
    w2b = prog.sb("f3_w2b", [128, KC2 * D], BF16)

    ps_tp = prog.ps("f3_ps_tp", [128, 128], BF16)
    ps_g = prog.ps("f3_ps_g", [128, 8], F32)
    ps_h = [prog.ps(f"f3_ps_h{i}", [128, 512], F32) for i in range(2)]
    ps_y = [prog.ps(f"f3_ps_y{i}", [128, 512], F32) for i in range(2)]

    s_cvt = prog.sem("f3_cvt")
    WB, cvt_done = emit_weight_convert(prog, "l3f", W, ["Wr", "brrow", "b2row", "W1", "W2"], s_cvt)

    prog.emit("sync", lambda e: e.dma_start(
        out=wrb[:].rearrange("p (c n) -> p c n", c=NC_D),
        in_=WB["Wr"].rearrange("(c p) n -> p c n", p=128)),
        waits=[(s_cvt, cvt_done)], inc=s_lwi, inc_by=16)
    prog.emit("sync", lambda e: e.dma_start(out=brb[:], in_=WB["brrow"]), inc=s_lwi, inc_by=16)
    prog.emit("sync", lambda e: e.dma_start(out=b2b[:], in_=WB["b2row"]), inc=s_lwi, inc_by=16)
    prog.emit("sync", lambda e: e.dma_start(
        out=b1c[:].rearrange("p (e c) -> p e c", e=E),
        in_=W["b1col"].rearrange("e p c -> p e c")), inc=s_ld, inc_by=16)
    init_lw = prog.n(s_lwi)
    init_ld = prog.n(s_ld)

    s_lw2 = prog.sem("f3_lw2")

    def load_w1(e_idx, buf, war_mm=0):
        prog.emit("sync", lambda e, e_idx=e_idx, buf=buf: e.dma_start(
            out=w1b[buf][:].rearrange("p (c n) -> p c n", c=KC1),
            in_=WB["W1"][e_idx].rearrange("(c p) n -> p c n", p=128)),
            waits=[(s_mm, war_mm), (s_cvt, cvt_done)], inc=s_lwp[buf], inc_by=16)
        return (s_lwp[buf], prog.n(s_lwp[buf]))

    def load_w2(e_idx, war_mm=0):
        prog.emit("sync", lambda e, e_idx=e_idx: e.dma_start(
            out=w2b[:].rearrange("p (c n) -> p c n", c=KC2),
            in_=WB["W2"][e_idx].rearrange("(c p) n -> p c n", p=128)),
            waits=[(s_mm, war_mm), (s_cvt, cvt_done)], inc=s_lw2, inc_by=16)
        return (s_lw2, prog.n(s_lw2))

    for st_i in range(NS):
        stbase = st_i * ST
        war0 = prog.n(s_mm)
        w1ready = {0: load_w1(0, 0, war_mm=war0), 1: load_w1(1, 1, war_mm=war0)}
        w2ready = {0: load_w2(0, war_mm=war0)}
        for pt in range(ST // 128):
            emit_x_ptile_prep(prog, x_dram, stbase + pt * 128,
                              x_tm, x_tmb, x_fm, pt * 128, ST,
                              ps_tp, s_lx, s_v, s_mm)
        xfm_ready = prog.n(s_v)

        for pt in range(ST // 128):
            def g_mm(e, pt=pt):
                for c in range(NC_D):
                    e.matmul(ps_g[:], x_fm[:, c * ST + pt * 128: c * ST + pt * 128 + 128],
                             wrb[:, c * 8: c * 8 + 8], start=(c == 0), stop=False)
                return e.matmul(ps_g[:], consts.ones1[:], brb[:], start=False, stop=True)
            prog.emit("tensor", g_mm,
                      waits=[(s_v, xfm_ready), (s_lwi, init_lw), (s_v, prog.n(s_v)),
                             (s_s, prog.n(s_s)), (s_ld, init_ld)],
                      inc=s_mm)
            gmm = prog.n(s_mm)
            emit_softmax_top2(prog, ps_g, gsc, cw[:, pt * 8: pt * 8 + 8], gmm,
                              s_v, s_s, s_mm, consts, renorm=False)
        cw_ready = prog.n(s_v)

        ps_h_war = [0, 0]
        ps_y_war = [0, 0]
        for e_i in range(E):
            buf = e_i % 2
            for tt in range(ntt):
                for hc in range(HC):
                    pbuf = hc % 2
                    def w1mm(e, tt=tt, hc=hc, buf=buf, pbuf=pbuf):
                        r = None
                        for kc in range(KC1):
                            r = e.matmul(ps_h[pbuf][:],
                                         w1b[buf][:, kc * H + hc * 128: kc * H + hc * 128 + 128],
                                         x_fm[:, kc * ST + tt * 512: kc * ST + (tt) * 512 + 512],
                                         start=(kc == 0), stop=(kc == KC1 - 1))
                        return r
                    prog.emit("tensor", w1mm,
                              waits=[(s_v, xfm_ready), w1ready[e_i],
                                     (s_s, ps_h_war[pbuf])],
                              inc=s_mm)
                    w1_done = prog.n(s_mm)
                    def hact(e, hc=hc, e_i=e_i, pbuf=pbuf):
                        return e.activation(out=h_fm[:, hc * 512: (hc + 1) * 512],
                                            in_=ps_h[pbuf][:], func=AF.Relu,
                                            bias=b1c[:, e_i * HC + hc: e_i * HC + hc + 1])
                    prog.emit("scalar", hact, waits=[(s_mm, w1_done)], inc=s_s)
                    ps_h_war[pbuf] = prog.n(s_s)
                h_ready = prog.n(s_s)
                for pt in range(4):
                    for dn in range(2):
                        pbuf = (pt * 2 + dn) % 2
                        def w2mm(e, tt=tt, pt=pt, dn=dn, buf=buf, e_i=e_i, pbuf=pbuf):
                            for kc in range(KC2):
                                e.matmul(ps_y[pbuf][:],
                                         h_fm[:, kc * 512 + pt * 128: kc * 512 + pt * 128 + 128],
                                         w2b[:, kc * D + dn * 512: kc * D + dn * 512 + 512],
                                         start=(kc == 0), stop=False)
                            return e.matmul(ps_y[pbuf][:], consts.ones1[:],
                                            b2b[:, e_i * D + dn * 512: e_i * D + dn * 512 + 512],
                                            start=False, stop=True)
                        prog.emit("tensor", w2mm,
                                  waits=[(s_s, h_ready), (s_v, ps_y_war[pbuf]),
                                         w2ready[e_i]],
                                  inc=s_mm)
                        y_done = prog.n(s_mm)
                        aoff = (tt * 4 + pt) * D + dn * 512
                        ccol = (tt * 4 + pt) * 8 + e_i
                        if e_i == 0:
                            def comb(e, aoff=aoff, ccol=ccol, pbuf=pbuf):
                                return e.tensor_scalar(out=acc[:, aoff:aoff + 512], in0=ps_y[pbuf][:],
                                                       scalar1=cw[:, ccol:ccol + 1], scalar2=None,
                                                       op0=ALU.mult)
                        else:
                            def comb(e, aoff=aoff, ccol=ccol, pbuf=pbuf):
                                return e.scalar_tensor_tensor(out=acc[:, aoff:aoff + 512],
                                                              in0=ps_y[pbuf][:],
                                                              scalar=cw[:, ccol:ccol + 1],
                                                              in1=acc[:, aoff:aoff + 512],
                                                              op0=ALU.mult, op1=ALU.add)
                        prog.emit("vector", comb, waits=[(s_mm, y_done), (s_v, cw_ready)], inc=s_v)
                        ps_y_war[pbuf] = prog.n(s_v)
            if e_i + 2 < E:
                w1ready[e_i + 2] = load_w1(e_i + 2, buf, war_mm=prog.n(s_mm))
            if e_i + 1 < E:
                w2ready[e_i + 1] = load_w2(e_i + 1, war_mm=prog.n(s_mm))

        for pt in range(ST // 128):
            t0 = stbase + pt * 128
            prog.emit("sync", lambda e, t0=t0: e.dma_start(
                out=x_tm[:, 0:D], in_=x_dram[t0:t0 + 128, :]),
                waits=[(s_v, prog.n(s_v))], inc=s_lx, inc_by=16)
            ld = prog.n(s_lx)
            emit_ln_epilogue(prog, x_tm[:, 0:D], acc[:, pt * D:(pt + 1) * D],
                             y_dram[t0:t0 + 128, :], {"sq": lnsq, "st": lnst},
                             s_v, s_s, s_st, waits_v=[(s_lx, ld), (s_v, prog.n(s_v)),
                                                      (s_s, prog.n(s_s)),
                                                      (s_st, prog.n(s_st))])
    prog.barrier("sync", [(s_st, prog.n(s_st))])
    prog.block_end([s_ld, s_lx, s_lwi, s_lwp[0], s_lwp[1], s_lw2, s_cvt, s_v, s_s, s_mm, s_st])




def emit_l0ffn(prog, C, x_dram, y_dram, W):
    T, ST = C["T"], C["ST"]
    NS = T // ST
    ntt = ST // 512
    NE, H = 10, 2048
    KC1, HC = NC_D, H // 128
    KC3 = H // 128

    s_ld = prog.sem("f0_ld"); s_lx = prog.sem("f0_lx"); s_lwi = prog.sem("f0_lwi")
    s_lw12 = prog.sem("f0_lw12"); s_lw3 = prog.sem("f0_lw3")
    s_v = prog.sem("f0_v"); s_s = prog.sem("f0_s")
    s_mm = prog.sem("f0_mm"); s_st = prog.sem("f0_st")
    prog.block_begin()
    consts = prog.consts

    x_tm = prog.sb("f0_x_tm", [128, D], F32)
    x_tmb = prog.sb("f0_x_tmb", [128, D], BF16)
    x_fm = prog.sb("f0_x_fm", [128, NC_D * ST], BF16)
    s1 = prog.sb("f0_s1", [128, 512], BF16)
    glu = prog.sb("f0_glu", [128, HC * 512], BF16)
    acc = prog.sb("f0_acc", [128, (ST // 128) * D], F32)
    cw = prog.sb("f0_cw", [128, (ST // 128) * 16], F32)
    gsc = prog.sb("f0_gsc", [128, 40], F32)
    lnsq = prog.sb("f0_lnsq", [128, 2 * D], F32)
    lnst = prog.sb("f0_lnst", [128, 8], F32)
    gwb = prog.sb("f0_gwb", [128, NC_D * 8], BF16)
    gbb = prog.sb("f0_gbb", [1, 8], BF16)
    b3b = prog.sb("f0_b3b", [1, NE * D], BF16)
    b1c = prog.sb("f0_b1c", [128, NE * HC], F32)
    b2c = prog.sb("f0_b2c", [128, NE * HC], F32)
    w1b = prog.sb("f0_w1b", [128, KC1 * H], BF16)
    w2b = prog.sb("f0_w2b", [128, KC1 * H], BF16)
    w3b = prog.sb("f0_w3b", [128, KC3 * D], BF16)

    ps_tp = prog.ps("f0_ps_tp", [128, 128], BF16)
    ps_g = prog.ps("f0_ps_g", [128, 8], F32)
    ps_h = [prog.ps(f"f0_ps_h{i}", [128, 512], F32) for i in range(3)]
    ps_y = [prog.ps(f"f0_ps_y{i}", [128, 512], F32) for i in range(2)]

    s_cvt = prog.sem("f0_cvt")
    WB, cvt_done = emit_weight_convert(prog, "l0f", W,
                                       ["gate_w", "gate_b_row", "w1", "w2", "w3", "b3row"], s_cvt)

    prog.emit("sync", lambda e: e.dma_start(
        out=gwb[:].rearrange("p (c n) -> p c n", c=NC_D),
        in_=WB["gate_w"].rearrange("(c p) n -> p c n", p=128)),
        waits=[(s_cvt, cvt_done)], inc=s_lwi, inc_by=16)
    prog.emit("sync", lambda e: e.dma_start(out=gbb[:], in_=WB["gate_b_row"]), inc=s_lwi, inc_by=16)
    prog.emit("sync", lambda e: e.dma_start(out=b3b[:], in_=WB["b3row"]), inc=s_lwi, inc_by=16)
    prog.emit("sync", lambda e: e.dma_start(
        out=b1c[:].rearrange("p (e c) -> p e c", e=NE),
        in_=W["b1col"].rearrange("e p c -> p e c")), inc=s_ld, inc_by=16)
    prog.emit("sync", lambda e: e.dma_start(
        out=b2c[:].rearrange("p (e c) -> p e c", e=NE),
        in_=W["b2col"].rearrange("e p c -> p e c")), inc=s_ld, inc_by=16)
    init_lw = prog.n(s_lwi)
    init_ld = prog.n(s_ld)

    def load_w12(e_idx, war_mm=0):
        prog.emit("sync", lambda e, e_idx=e_idx: e.dma_start(
            out=w1b[:].rearrange("p (c n) -> p c n", c=KC1),
            in_=WB["w1"][e_idx].rearrange("(c p) n -> p c n", p=128)),
            waits=[(s_mm, war_mm), (s_cvt, cvt_done)], inc=s_lw12, inc_by=16)
        prog.emit("sync", lambda e, e_idx=e_idx: e.dma_start(
            out=w2b[:].rearrange("p (c n) -> p c n", c=KC1),
            in_=WB["w2"][e_idx].rearrange("(c p) n -> p c n", p=128)),
            inc=s_lw12, inc_by=16)
        return (s_lw12, prog.n(s_lw12))

    def load_w3(e_idx, war_mm=0):
        prog.emit("sync", lambda e, e_idx=e_idx: e.dma_start(
            out=w3b[:].rearrange("p (c n) -> p c n", c=KC3),
            in_=WB["w3"][e_idx].rearrange("(c p) n -> p c n", p=128)),
            waits=[(s_mm, war_mm), (s_cvt, cvt_done)], inc=s_lw3, inc_by=16)
        return (s_lw3, prog.n(s_lw3))

    for st_i in range(NS):
        stbase = st_i * ST
        war0 = prog.n(s_mm)
        w12ready = {0: load_w12(0, war_mm=war0)}
        w3ready = {0: load_w3(0, war_mm=war0)}

        for pt in range(ST // 128):
            emit_x_ptile_prep(prog, x_dram, stbase + pt * 128,
                              x_tm, x_tmb, x_fm, pt * 128, ST,
                              ps_tp, s_lx, s_v, s_mm)
        xfm_ready = prog.n(s_v)

        for pt in range(ST // 128):
            def g_mm(e, pt=pt):
                for c in range(NC_D):
                    e.matmul(ps_g[:], x_fm[:, c * ST + pt * 128: c * ST + pt * 128 + 128],
                             gwb[:, c * 8: c * 8 + 8], start=(c == 0), stop=False)
                return e.matmul(ps_g[:], consts.ones1[:], gbb[:], start=False, stop=True)
            prog.emit("tensor", g_mm,
                      waits=[(s_v, xfm_ready), (s_lwi, init_lw), (s_v, prog.n(s_v)),
                             (s_s, prog.n(s_s)), (s_ld, init_ld)],
                      inc=s_mm)
            gmm = prog.n(s_mm)
            pcol = pt * 16
            emit_softmax_top2(prog, ps_g, gsc, cw[:, pcol:pcol + 8], gmm,
                              s_v, s_s, s_mm, consts, renorm=False)
            prog.emit("vector", lambda e, pcol=pcol: e.memset(cw[:, pcol + 8: pcol + 10], 0.5),
                      waits=[(s_v, prog.n(s_v))], inc=s_v)
        cw_ready = prog.n(s_v)

        ps_h_war_s = [0, 0, 0]
        ps_h_war_v = [0, 0, 0]
        ps_y_war = [0, 0]
        for e_i in range(NE):
            for tt in range(ntt):
                for hc in range(HC):
                    pb1 = (2 * hc) % 3
                    pb2 = (2 * hc + 1) % 3
                    def p1mm(e, tt=tt, hc=hc, pb1=pb1):
                        r = None
                        for kc in range(KC1):
                            r = e.matmul(ps_h[pb1][:],
                                         w1b[:, kc * H + hc * 128: kc * H + hc * 128 + 128],
                                         x_fm[:, kc * ST + tt * 512: kc * ST + tt * 512 + 512],
                                         start=(kc == 0), stop=(kc == KC1 - 1))
                        return r
                    prog.emit("tensor", p1mm,
                              waits=[(s_v, xfm_ready), w12ready[e_i],
                                     (s_s, ps_h_war_s[pb1]), (s_v, ps_h_war_v[pb1])],
                              inc=s_mm)
                    p1_done = prog.n(s_mm)
                    def p2mm(e, tt=tt, hc=hc, pb2=pb2):
                        r = None
                        for kc in range(KC1):
                            r = e.matmul(ps_h[pb2][:],
                                         w2b[:, kc * H + hc * 128: kc * H + hc * 128 + 128],
                                         x_fm[:, kc * ST + tt * 512: kc * ST + tt * 512 + 512],
                                         start=(kc == 0), stop=(kc == KC1 - 1))
                        return r
                    prog.emit("tensor", p2mm,
                              waits=[(s_s, ps_h_war_s[pb2]), (s_v, ps_h_war_v[pb2])],
                              inc=s_mm)
                    p2_done = prog.n(s_mm)
                    def s1act(e, hc=hc, e_i=e_i, pb1=pb1):
                        return e.activation(out=s1[:], in_=ps_h[pb1][:], func=AF.Silu,
                                            bias=b1c[:, e_i * HC + hc: e_i * HC + hc + 1])
                    prog.emit("scalar", s1act, waits=[(s_mm, p1_done), (s_v, prog.n(s_v))], inc=s_s)
                    ps_h_war_s[pb1] = prog.n(s_s)
                    s1_done = prog.n(s_s)
                    def glumul(e, hc=hc, e_i=e_i, pb2=pb2):
                        return e.scalar_tensor_tensor(
                            out=glu[:, hc * 512:(hc + 1) * 512], in0=ps_h[pb2][:],
                            scalar=b2c[:, e_i * HC + hc: e_i * HC + hc + 1], in1=s1[:],
                            op0=ALU.add, op1=ALU.mult)
                    prog.emit("vector", glumul, waits=[(s_mm, p2_done), (s_s, s1_done)], inc=s_v)
                    ps_h_war_v[pb2] = prog.n(s_v)
                glu_ready_v = prog.n(s_v)
                if tt == ntt - 1 and e_i + 1 < NE:
                    w12ready[e_i + 1] = load_w12(e_i + 1, war_mm=prog.n(s_mm))
                for pt in range(4):
                    for dn in range(2):
                        pbuf = (pt * 2 + dn) % 2
                        def w3mm(e, tt=tt, pt=pt, dn=dn, e_i=e_i, pbuf=pbuf):
                            for kc in range(KC3):
                                e.matmul(ps_y[pbuf][:],
                                         glu[:, kc * 512 + pt * 128: kc * 512 + pt * 128 + 128],
                                         w3b[:, kc * D + dn * 512: kc * D + dn * 512 + 512],
                                         start=(kc == 0), stop=False)
                            return e.matmul(ps_y[pbuf][:], consts.ones1[:],
                                            b3b[:, e_i * D + dn * 512: e_i * D + dn * 512 + 512],
                                            start=False, stop=True)
                        prog.emit("tensor", w3mm,
                                  waits=[(s_v, glu_ready_v), (s_v, ps_y_war[pbuf]), w3ready[e_i]],
                                  inc=s_mm)
                        y_done = prog.n(s_mm)
                        aoff = (tt * 4 + pt) * D + dn * 512
                        ccol = (tt * 4 + pt) * 16 + e_i
                        if e_i == 0:
                            def comb(e, aoff=aoff, ccol=ccol, pbuf=pbuf):
                                return e.tensor_scalar(out=acc[:, aoff:aoff + 512], in0=ps_y[pbuf][:],
                                                       scalar1=cw[:, ccol:ccol + 1], scalar2=None,
                                                       op0=ALU.mult)
                        else:
                            def comb(e, aoff=aoff, ccol=ccol, pbuf=pbuf):
                                return e.scalar_tensor_tensor(out=acc[:, aoff:aoff + 512],
                                                              in0=ps_y[pbuf][:],
                                                              scalar=cw[:, ccol:ccol + 1],
                                                              in1=acc[:, aoff:aoff + 512],
                                                              op0=ALU.mult, op1=ALU.add)
                        prog.emit("vector", comb, waits=[(s_mm, y_done), (s_v, cw_ready)], inc=s_v)
                        ps_y_war[pbuf] = prog.n(s_v)
            if e_i + 1 < NE:
                w3ready[e_i + 1] = load_w3(e_i + 1, war_mm=prog.n(s_mm))

        for pt in range(ST // 128):
            t0 = stbase + pt * 128
            prog.emit("sync", lambda e, t0=t0: e.dma_start(
                out=x_tm[:, 0:D], in_=x_dram[t0:t0 + 128, :]),
                waits=[(s_v, prog.n(s_v))], inc=s_lx, inc_by=16)
            ld = prog.n(s_lx)
            emit_ln_epilogue(prog, x_tm[:, 0:D], acc[:, pt * D:(pt + 1) * D],
                             y_dram[t0:t0 + 128, :], {"sq": lnsq, "st": lnst},
                             s_v, s_s, s_st, waits_v=[(s_lx, ld), (s_v, prog.n(s_v)),
                                                      (s_s, prog.n(s_s)),
                                                      (s_st, prog.n(s_st))])
    prog.barrier("sync", [(s_st, prog.n(s_st))])
    prog.block_end([s_ld, s_lx, s_lwi, s_lw12, s_lw3, s_cvt, s_v, s_s, s_mm, s_st])


def prep_l0ffn_weights(inputs):
    w1 = np.concatenate([inputs["L0_ffn_rw1"], inputs["L0_ffn_sw1"]], 0)
    w2 = np.concatenate([inputs["L0_ffn_rw2"], inputs["L0_ffn_sw2"]], 0)
    w3 = np.concatenate([inputs["L0_ffn_rw3"], inputs["L0_ffn_sw3"]], 0)
    b1 = np.concatenate([inputs["L0_ffn_rb1"], inputs["L0_ffn_sb1"]], 0)
    b2 = np.concatenate([inputs["L0_ffn_rb2"], inputs["L0_ffn_sb2"]], 0)
    b3 = np.concatenate([inputs["L0_ffn_rb3"], inputs["L0_ffn_sb3"]], 0)
    return {
        "gate_w": np.ascontiguousarray(inputs["L0_ffn_gate_w"]),
        "gate_b_row": np.ascontiguousarray(inputs["L0_ffn_gate_b"]).reshape(1, 8),
        "w1": np.ascontiguousarray(w1), "w2": np.ascontiguousarray(w2),
        "w3": np.ascontiguousarray(w3),
        "b1col": np.ascontiguousarray(b1.reshape(10, 16, 128).transpose(0, 2, 1)),
        "b2col": np.ascontiguousarray(b2.reshape(10, 16, 128).transpose(0, 2, 1)),
        "b3row": np.ascontiguousarray(b3).reshape(1, 10 * 1024),
    }


L0FFN_SPEC = {"gate_w": (1024, 8), "gate_b_row": (1, 8),
              "w1": (10, 1024, 2048), "w2": (10, 1024, 2048), "w3": (10, 2048, 1024),
              "b1col": (10, 128, 16), "b2col": (10, 128, 16), "b3row": (1, 10 * 1024)}




L1FFN_SPEC = {"cn_w1": (1024, 1024), "cn_w2": (1024, 512), "cn_w3": (512, 256),
              "cn_w4": (256, 1), "gc_w1": (1024, 512), "gc_w2": (512, 256),
              "gc_w3": (256, 1),
              "cn_b1c": (128, 8), "cn_b2c": (128, 4), "cn_b3c": (128, 2),
              "cn_b4": (1, 1), "gc_b1c": (128, 4), "gc_b2c": (128, 2),
              "gc_b3": (1, 1), "ar": (1, 1)}


def prep_l1ffn_weights(inputs):
    cc = np.ascontiguousarray
    return {
        "cn_w1": cc(inputs["L1_ffn_cn_w1"]), "cn_w2": cc(inputs["L1_ffn_cn_w2"]),
        "cn_w3": cc(inputs["L1_ffn_cn_w3"]), "cn_w4": cc(inputs["L1_ffn_cn_w4"]),
        "gc_w1": cc(inputs["L1_ffn_gc_w1"]), "gc_w2": cc(inputs["L1_ffn_gc_w2"]),
        "gc_w3": cc(inputs["L1_ffn_gc_w3"]),
        "cn_b1c": cc(inputs["L1_ffn_cn_b1"].reshape(8, 128).T),
        "cn_b2c": cc(inputs["L1_ffn_cn_b2"].reshape(4, 128).T),
        "cn_b3c": cc(inputs["L1_ffn_cn_b3"].reshape(2, 128).T),
        "cn_b4": cc(inputs["L1_ffn_cn_b4"].reshape(1, 1)),
        "gc_b1c": cc(inputs["L1_ffn_gc_b1"].reshape(4, 128).T),
        "gc_b2c": cc(inputs["L1_ffn_gc_b2"].reshape(2, 128).T),
        "gc_b3": cc(inputs["L1_ffn_gc_b3"].reshape(1, 1)),
        "ar": cc(inputs["L1_ffn_adaptive_range"].reshape(1, 1)),
    }


def emit_l1ffn(prog, C, x_dram, y_dram, W):
    T, Lseq = C["T"], C["L"]
    NSEQ = T // Lseq
    ntile = T // 512
    consts = prog.consts

    s_ld = prog.sem("f1_ld"); s_lx = prog.sem("f1_lx"); s_lwi = prog.sem("f1_lwi")
    s_v = prog.sem("f1_v"); s_s = prog.sem("f1_s")
    s_mm = prog.sem("f1_mm"); s_st = prog.sem("f1_st")
    s_cvt = prog.sem("f1_cvt")
    prog.block_begin()

    WB, cvt_done = emit_weight_convert(prog, "l1f", W,
                                       ["cn_w1", "cn_w2", "cn_w3", "cn_w4",
                                        "gc_w1", "gc_w2", "gc_w3", "cn_b4", "gc_b3"], s_cvt)

    st0 = prog.sb("f1_st0", [128, 4 * D], F32)
    st1 = prog.sb("f1_st1", [128, 4 * D], F32)
    sfm = prog.sb("f1_sfm", [128, NC_D * 512], BF16)
    x_tmb = prog.sb("f1_x_tmb", [128, D], BF16)
    x_tm = prog.sb("f1_x_tm", [128, D], F32)
    h1 = prog.sb("f1_h1", [128, 8 * 512], BF16)
    h2 = prog.sb("f1_h2", [128, 4 * 512], BF16)
    h3 = prog.sb("f1_h3", [128, 2 * 512], BF16)
    cn1b = prog.sb("f1_cn1b", [128, 8 * 1024], BF16)
    cn2b = prog.sb("f1_cn2b", [128, 8 * 512], BF16)
    cn3b = prog.sb("f1_cn3b", [128, 4 * 256], BF16)
    cn4b = prog.sb("f1_cn4b", [128, 2 * 1], BF16)
    gc1b = prog.sb("f1_gc1b", [128, 8 * 512], BF16)
    gc2b = prog.sb("f1_gc2b", [128, 4 * 256], BF16)
    gc3b = prog.sb("f1_gc3b", [128, 2 * 1], BF16)
    cb4b = prog.sb("f1_cb4b", [1, 1], BF16)
    gb3b = prog.sb("f1_gb3b", [1, 1], BF16)
    bcols = prog.sb("f1_bcols", [128, 16], F32)
    gbcols = prog.sb("f1_gbcols", [128, 6], F32)
    arf = prog.sb("f1_arf", [1, 1], F32)
    meanf = prog.sb("f1_meanf", [128, NC_D], BF16)
    g1f = prog.sb("f1_g1f", [128, 4 * 1], BF16)
    g2f = prog.sb("f1_g2f", [128, 2 * 1], BF16)
    gfrow = prog.sb("f1_gfrow", [1, 8], F32)
    colsc = prog.sb("f1_colsc", [128, 16], F32)
    lnsq = prog.sb("f1_lnsq", [128, 2 * D], F32)
    lnst = prog.sb("f1_lnst", [128, 8], F32)

    ps_tp = prog.ps("f1_ps_tp", [128, 128], BF16)
    ps_a = [prog.ps(f"f1_ps_a{i}", [128, 512], F32) for i in range(2)]
    ps_m = prog.ps("f1_ps_m", [128, NC_D], F32)
    ps_c = prog.ps("f1_ps_c", [128, 4], F32)

    def ldw(dst, key, nchunks):
        prog.emit("sync", lambda e, dst=dst, key=key, nchunks=nchunks: e.dma_start(
            out=dst[:].rearrange("p (c n) -> p c n", c=nchunks),
            in_=WB[key].rearrange("(c p) n -> p c n", p=128),
            allow_slow_non_contiguous=True),
            waits=[(s_cvt, cvt_done)], inc=s_lwi, inc_by=16)
    ldw(cn1b, "cn_w1", 8)
    ldw(cn2b, "cn_w2", 8)
    ldw(cn3b, "cn_w3", 4)
    ldw(cn4b, "cn_w4", 2)
    ldw(gc1b, "gc_w1", 8)
    ldw(gc2b, "gc_w2", 4)
    ldw(gc3b, "gc_w3", 2)
    prog.emit("sync", lambda e: e.dma_start(out=cb4b[:], in_=WB["cn_b4"]),
              waits=[(s_cvt, cvt_done)], inc=s_lwi, inc_by=16)
    prog.emit("sync", lambda e: e.dma_start(out=gb3b[:], in_=WB["gc_b3"]),
              waits=[(s_cvt, cvt_done)], inc=s_lwi, inc_by=16)
    prog.emit("sync", lambda e: e.dma_start(out=bcols[:, 0:8], in_=W["cn_b1c"]), inc=s_lwi, inc_by=16)
    prog.emit("sync", lambda e: e.dma_start(out=bcols[:, 8:12], in_=W["cn_b2c"]), inc=s_lwi, inc_by=16)
    prog.emit("sync", lambda e: e.dma_start(out=bcols[:, 12:14], in_=W["cn_b3c"]), inc=s_lwi, inc_by=16)
    prog.emit("sync", lambda e: e.dma_start(out=gbcols[:, 0:4], in_=W["gc_b1c"]), inc=s_lwi, inc_by=16)
    prog.emit("sync", lambda e: e.dma_start(out=gbcols[:, 4:6], in_=W["gc_b2c"]), inc=s_lwi, inc_by=16)
    prog.emit("sync", lambda e: e.dma_start(out=arf[:], in_=W["ar"]), inc=s_lwi, inc_by=16)
    init_lw = prog.n(s_lwi)
    prog.barrier("tensor", [(s_lwi, init_lw)])
    prog.barrier("vector", [(s_lwi, init_lw)])
    prog.barrier("scalar", [(s_lwi, init_lw)])

    prog.emit("tensor", lambda e: e.matmul(ps_c[:, 0:1], consts.ones1f[:], arf[:],
                                           start=True, stop=True), inc=s_mm)
    prog.emit("vector", lambda e: e.tensor_copy(out=colsc[:, 0:1], in_=ps_c[:, 0:1]),
              waits=[(s_mm, prog.n(s_mm))], inc=s_v)
    arb_done = prog.n(s_v)

    macc = prog.sb("f1_macc", [128, NC_D], F32)
    for sq in range(NSEQ):
        base = sq * Lseq
        prog.emit("vector", lambda e: e.memset(macc[:], 0.0),
                  waits=[(s_v, prog.n(s_v))], inc=s_v)
        for pt in range(Lseq // 128):
            t0 = base + pt * 128
            prog.emit("sync", lambda e, t0=t0: e.dma_start(out=x_tm[:], in_=x_dram[t0:t0 + 128, :]),
                      waits=[(s_v, prog.n(s_v))], inc=s_lx, inc_by=16)
            prog.emit("vector", lambda e: e.tensor_copy(out=x_tmb[:], in_=x_tm[:]),
                      waits=[(s_lx, prog.n(s_lx))], inc=s_v)
            cast_done = prog.n(s_v)
            for c in range(NC_D):
                def msum(e, c=c):
                    return e.matmul(ps_m[:, 0:1], x_tmb[:, c * 128:(c + 1) * 128],
                                    consts.onescol[:], start=True, stop=True)
                prog.emit("tensor", msum, waits=[(s_v, cast_done), (s_v, prog.n(s_v))], inc=s_mm)
                prog.emit("vector", lambda e, c=c: e.tensor_tensor(
                    out=macc[:, c:c + 1], in0=macc[:, c:c + 1], in1=ps_m[:, 0:1], op=ALU.add),
                    waits=[(s_mm, prog.n(s_mm)), (s_v, prog.n(s_v))], inc=s_v)
        prog.emit("scalar", lambda e: e.activation(out=meanf[:], in_=macc[:], func=AF.Copy,
                                                   scale=1.0 / Lseq),
                  waits=[(s_v, prog.n(s_v)), (s_s, prog.n(s_s))], inc=s_s)
        mean_done = prog.n(s_s)
        for oc in range(4):
            def g1(e, oc=oc):
                r = None
                for kc in range(NC_D):
                    r = e.matmul(ps_a[0][:, 0:1],
                                 gc1b[:, kc * 512 + oc * 128: kc * 512 + oc * 128 + 128],
                                 meanf[:, kc:kc + 1], start=(kc == 0), stop=(kc == NC_D - 1))
                return r
            prog.emit("tensor", g1, waits=[(s_s, mean_done), (s_v, prog.n(s_v)), (s_s, prog.n(s_s))], inc=s_mm)
            prog.emit("scalar", lambda e, oc=oc: e.activation(
                out=g1f[:, oc:oc + 1], in_=ps_a[0][:, 0:1], func=AF.Gelu,
                bias=gbcols[:, oc:oc + 1]),
                waits=[(s_mm, prog.n(s_mm)), (s_s, prog.n(s_s))], inc=s_s)
        g1_done = prog.n(s_s)
        for oc in range(2):
            def g2(e, oc=oc):
                r = None
                for kc in range(4):
                    r = e.matmul(ps_a[1][:, 0:1],
                                 gc2b[:, kc * 256 + oc * 128: kc * 256 + oc * 128 + 128],
                                 g1f[:, kc:kc + 1], start=(kc == 0), stop=(kc == 3))
                return r
            prog.emit("tensor", g2, waits=[(s_s, g1_done), (s_v, prog.n(s_v)), (s_s, prog.n(s_s))], inc=s_mm)
            prog.emit("scalar", lambda e, oc=oc: e.activation(
                out=g2f[:, oc:oc + 1], in_=ps_a[1][:, 0:1], func=AF.Gelu,
                bias=gbcols[:, 4 + oc:5 + oc]),
                waits=[(s_mm, prog.n(s_mm)), (s_s, prog.n(s_s))], inc=s_s)
        g2_done = prog.n(s_s)
        def g3(e, sq=sq):
            for kc in range(2):
                e.matmul(ps_c[0:1, 0:1], gc3b[:, kc:kc + 1], g2f[:, kc:kc + 1],
                         start=(kc == 0), stop=False)
            return e.matmul(ps_c[0:1, 0:1], consts.ones1[:, 0:1], gb3b[:], start=False, stop=True)
        prog.emit("tensor", g3, waits=[(s_s, g2_done), (s_v, prog.n(s_v))], inc=s_mm)
        prog.emit("scalar", lambda e, sq=sq: e.activation(
            out=gfrow[:, sq:sq + 1], in_=ps_c[0:1, 0:1], func=AF.Sigmoid),
            waits=[(s_mm, prog.n(s_mm)), (s_s, prog.n(s_s))], inc=s_s)
    gf_done_s = prog.n(s_s)
    for sq in range(NSEQ):
        prog.emit("tensor", lambda e, sq=sq: e.matmul(ps_c[:, 0:1], consts.ones1f[:],
                                                      gfrow[:, sq:sq + 1], start=True, stop=True),
                  waits=[(s_s, gf_done_s), (s_v, prog.n(s_v)), (s_s, prog.n(s_s))], inc=s_mm)
        prog.emit("vector", lambda e, sq=sq: e.tensor_copy(out=colsc[:, 1 + sq:2 + sq], in_=ps_c[:, 0:1]),
                  waits=[(s_mm, prog.n(s_mm))], inc=s_v)
    gfb_done = prog.n(s_v)

    for ti in range(ntile):
        tb = ti * 512
        sq = tb // Lseq
        prog.emit("sync", lambda e, tb=tb: e.dma_start(
            out=st0[:].rearrange("p (pt d) -> p pt d", pt=4),
            in_=x_dram[tb:tb + 512, :].rearrange("(pt p) d -> p pt d", p=128)),
            waits=[(s_v, prog.n(s_v)), (s_s, prog.n(s_s)), (s_mm, prog.n(s_mm))],
            inc=s_lx, inc_by=16)
        x_ld = prog.n(s_lx)
        cur, nxt = st0, st1
        for cyc in range(3):
            for pt in range(4):
                prog.emit("vector", lambda e, pt=pt, cur=cur: e.tensor_copy(
                    out=x_tmb[:], in_=cur[:, pt * D:(pt + 1) * D]),
                    waits=[(s_lx, x_ld), (s_v, prog.n(s_v)), (s_mm, prog.n(s_mm))], inc=s_v)
                for c in range(NC_D):
                    prog.emit("tensor", lambda e, c=c: e.transpose(
                        ps_tp[:], x_tmb[:, c * 128:(c + 1) * 128], consts.ident[:]),
                        waits=[(s_v, prog.n(s_v))], inc=s_mm)
                    mm_done = prog.n(s_mm)
                    prog.emit("vector", lambda e, c=c, pt=pt: e.tensor_copy(
                        out=sfm[:, c * 512 + pt * 128: c * 512 + pt * 128 + 128], in_=ps_tp[:]),
                        waits=[(s_mm, mm_done)], inc=s_v)
            sfm_done = prog.n(s_v)
            act_war = [prog.n(s_s)] * 2
            for oc in range(8):
                pb = oc % 2
                def h1mm(e, oc=oc, pb=pb):
                    r = None
                    for kc in range(NC_D):
                        r = e.matmul(ps_a[pb][:],
                                     cn1b[:, kc * 1024 + oc * 128: kc * 1024 + oc * 128 + 128],
                                     sfm[:, kc * 512:(kc + 1) * 512],
                                     start=(kc == 0), stop=(kc == NC_D - 1))
                    return r
                prog.emit("tensor", h1mm, waits=[(s_v, sfm_done), (s_s, act_war[pb])], inc=s_mm)
                prog.emit("scalar", lambda e, oc=oc, pb=pb: e.activation(
                    out=h1[:, oc * 512:(oc + 1) * 512], in_=ps_a[pb][:], func=AF.Gelu,
                    bias=bcols[:, oc:oc + 1]),
                    waits=[(s_mm, prog.n(s_mm))], inc=s_s)
                act_war[pb] = prog.n(s_s)
            h1_done = prog.n(s_s)
            for oc in range(4):
                pb = oc % 2
                def h2mm(e, oc=oc, pb=pb):
                    r = None
                    for kc in range(8):
                        r = e.matmul(ps_a[pb][:],
                                     cn2b[:, kc * 512 + oc * 128: kc * 512 + oc * 128 + 128],
                                     h1[:, kc * 512:(kc + 1) * 512],
                                     start=(kc == 0), stop=(kc == 7))
                    return r
                prog.emit("tensor", h2mm, waits=[(s_s, h1_done), (s_s, act_war[pb])], inc=s_mm)
                prog.emit("scalar", lambda e, oc=oc, pb=pb: e.activation(
                    out=h2[:, oc * 512:(oc + 1) * 512], in_=ps_a[pb][:], func=AF.Gelu,
                    bias=bcols[:, 8 + oc:9 + oc]),
                    waits=[(s_mm, prog.n(s_mm))], inc=s_s)
                act_war[pb] = prog.n(s_s)
            h2_done = prog.n(s_s)
            for oc in range(2):
                pb = oc % 2
                def h3mm(e, oc=oc, pb=pb):
                    r = None
                    for kc in range(4):
                        r = e.matmul(ps_a[pb][:],
                                     cn3b[:, kc * 256 + oc * 128: kc * 256 + oc * 128 + 128],
                                     h2[:, kc * 512:(kc + 1) * 512],
                                     start=(kc == 0), stop=(kc == 3))
                    return r
                prog.emit("tensor", h3mm, waits=[(s_s, h2_done), (s_s, act_war[pb])], inc=s_mm)
                prog.emit("scalar", lambda e, oc=oc, pb=pb: e.activation(
                    out=h3[:, oc * 512:(oc + 1) * 512], in_=ps_a[pb][:], func=AF.Gelu,
                    bias=bcols[:, 12 + oc:13 + oc]),
                    waits=[(s_mm, prog.n(s_mm))], inc=s_s)
                act_war[pb] = prog.n(s_s)
            h3_done = prog.n(s_s)
            for pt in range(4):
                def tfmm(e, pt=pt):
                    for kc in range(2):
                        e.matmul(ps_c[:, 0:1], h3[:, kc * 512 + pt * 128: kc * 512 + pt * 128 + 128],
                                 cn4b[:, kc:kc + 1], start=(kc == 0), stop=False)
                    return e.matmul(ps_c[:, 0:1], consts.ones1[:], cb4b[:], start=False, stop=True)
                prog.emit("tensor", tfmm, waits=[(s_s, h3_done), (s_v, prog.n(s_v)),
                                                 (s_s, prog.n(s_s))], inc=s_mm)
                prog.emit("scalar", lambda e: e.activation(out=colsc[:, 8:9], in_=ps_c[:, 0:1],
                                                           func=AF.Sigmoid),
                          waits=[(s_mm, prog.n(s_mm)), (s_v, prog.n(s_v)), (s_s, prog.n(s_s))], inc=s_s)
                tf_done = prog.n(s_s)
                prog.emit("vector", lambda e: e.tensor_scalar(
                    out=colsc[:, 9:10], in0=colsc[:, 8:9], scalar1=0.3, scalar2=None, op0=ALU.mult),
                    waits=[(s_s, tf_done), (s_v, gfb_done), (s_v, prog.n(s_v)), (s_v, arb_done)], inc=s_v)
                prog.emit("vector", lambda e, sq=sq: e.scalar_tensor_tensor(
                    out=colsc[:, 9:10], in0=colsc[:, 1 + sq:2 + sq], scalar=0.7, in1=colsc[:, 9:10],
                    op0=ALU.mult, op1=ALU.add),
                    waits=[(s_v, prog.n(s_v))], inc=s_v)
                prog.emit("vector", lambda e: e.tensor_scalar(
                    out=colsc[:, 9:10], in0=colsc[:, 9:10], scalar1=-0.5, scalar2=None, op0=ALU.add),
                    waits=[(s_v, prog.n(s_v))], inc=s_v)
                prog.emit("vector", lambda e: e.tensor_tensor(
                    out=colsc[:, 9:10], in0=colsc[:, 9:10], in1=colsc[:, 0:1], op=ALU.mult),
                    waits=[(s_v, prog.n(s_v))], inc=s_v)
                prog.emit("vector", lambda e: e.tensor_scalar(
                    out=colsc[:, 10:11], in0=colsc[:, 9:10], scalar1=2.0, scalar2=0.1,
                    op0=ALU.mult, op1=ALU.add),
                    waits=[(s_v, prog.n(s_v))], inc=s_v)
                prog.emit("vector", lambda e: e.tensor_scalar(
                    out=colsc[:, 11:12], in0=colsc[:, 10:11], scalar1=-1.0, scalar2=None, op0=ALU.mult),
                    waits=[(s_v, prog.n(s_v))], inc=s_v)
                c_done = prog.n(s_v)
                base_o = pt * D
                quarters = [(0, 512, 10), (256, 768, 11), (512, 0, 11), (768, 256, 10)]
                for (dsto, srco, ccol) in quarters:
                    def upd(e, dsto=dsto, srco=srco, ccol=ccol, base_o=base_o, cur=cur, nxt=nxt):
                        return e.scalar_tensor_tensor(
                            out=nxt[:, base_o + dsto: base_o + dsto + 256],
                            in0=cur[:, base_o + srco: base_o + srco + 256],
                            scalar=colsc[:, ccol:ccol + 1],
                            in1=cur[:, base_o + dsto: base_o + dsto + 256],
                            op0=ALU.mult, op1=ALU.add)
                    prog.emit("vector", upd, waits=[(s_v, c_done)], inc=s_v)
            cur, nxt = nxt, cur
        for pt in range(4):
            t0 = tb + pt * 128
            prog.emit("sync", lambda e, t0=t0: e.dma_start(
                out=x_tm[:, 0:D], in_=x_dram[t0:t0 + 128, :]),
                waits=[(s_v, prog.n(s_v))], inc=s_lx, inc_by=16)
            ld = prog.n(s_lx)
            emit_ln_epilogue(prog, x_tm[:, 0:D], cur[:, pt * D:(pt + 1) * D],
                             y_dram[t0:t0 + 128, :], {"sq": lnsq, "st": lnst},
                             s_v, s_s, s_st, waits_v=[(s_lx, ld), (s_v, prog.n(s_v)),
                                                      (s_s, prog.n(s_s)),
                                                      (s_st, prog.n(s_st))])
    prog.barrier("sync", [(s_st, prog.n(s_st))])
    prog.block_end([s_ld, s_lx, s_lwi, s_cvt, s_v, s_s, s_mm, s_st])




L2FFN_SPEC = {"gate_w": (1024, 8), "sim": (8, 8),
              "wg": (8, 1024, 1024), "wu": (8, 1024, 1024), "wd": (8, 1024, 1024)}


def prep_l2ffn_weights(inputs):
    gu = inputs["L2_ffn_gate_up_proj"]
    dn = inputs["L2_ffn_down_proj"]
    wg = np.ascontiguousarray(gu[:, :1024, :].transpose(0, 2, 1))
    wu = np.ascontiguousarray(gu[:, 1024:, :].transpose(0, 2, 1))
    wd = np.ascontiguousarray(dn.transpose(0, 2, 1))
    return {"gate_w": np.ascontiguousarray(inputs["L2_ffn_gate_weight"].T),
            "sim": np.ascontiguousarray(inputs["L2_ffn_similarity_matrix"]),
            "wg": wg, "wu": wu, "wd": wd}


def emit_l2ffn(prog, C, x_dram, y_dram, W, n_cores=8, mode="all"):
    T, ST = C["T"], C["ST"]
    NS = T // ST
    ntt = ST // 512
    E = 8
    NPT = T // 128
    consts = prog.consts
    nc = prog.nc

    s_ld = prog.sem("f2_ld"); s_lx = prog.sem("f2_lx"); s_lwi = prog.sem("f2_lwi")
    s_lgu = prog.sem("f2_lgu"); s_lwd = prog.sem("f2_lwd")
    s_v = prog.sem("f2_v"); s_s = prog.sem("f2_s")
    s_mm = prog.sem("f2_mm"); s_st = prog.sem("f2_st")
    s_cvt = prog.sem("f2_cvt"); s_xf = prog.sem("f2_xf")
    s_cc = prog.sem("f2_cc"); s_ccd = prog.sem("f2_ccd")
    prog.block_begin()

    WB, cvt_done = emit_weight_convert(prog, "l2f", W,
                                       (["gate_w"] if mode == "pre" else
                                        ["wg", "wu", "wd"] if mode == "post" else
                                        ["gate_w", "wg", "wu", "wd"]), s_cvt)
    if mode == "all":
        xfm_dram = nc.dram_tensor("l2f_xfm_scr", [T // 512, 128, NC_D * 512], BF16, kind="Internal").ap()
    elif mode == "pre":
        xfm_dram = nc.dram_tensor("l2f_xfm_h", [T // 512, 128, NC_D * 512], BF16, kind="ExternalOutput").ap()
        tstat_out = nc.dram_tensor("l2f_tstat_h", [128, (T // 128) * 4], F32, kind="ExternalOutput").ap()
        cnt_out = nc.dram_tensor("l2f_cnt_h", [1, 8], F32, kind="ExternalOutput").ap()
    else:
        xfm_dram = nc.dram_tensor("l2f_xfm_h", [T // 512, 128, NC_D * 512], BF16, kind="ExternalInput").ap()
        tstat_in = nc.dram_tensor("l2f_tstat_h", [128, (T // 128) * 4], F32, kind="ExternalInput").ap()
        cnt_all = nc.dram_tensor("l2f_cnt_all", [8, 8], F32, kind="ExternalInput").ap()
    cc_in = nc.dram_tensor("l2f_cc_in", [1, 8], F32, kind="Internal").ap()
    cc_out = nc.dram_tensor("l2f_cc_out", [1, 8], F32, kind="Internal").ap()

    x_tm = prog.sb("f2_x_tm", [128, D], F32)
    x_tmb = prog.sb("f2_x_tmb", [128, D], BF16)
    x_fm = prog.sb("f2_x_fm", [128, NC_D * ST], BF16)
    hmid = prog.sb("f2_hmid", [128, 8 * 512], BF16)
    sg1 = prog.sb("f2_sg1", [128, 512], BF16)
    acc = prog.sb("f2_acc", [128, (ST // 128) * D], F32)
    gsc = prog.sb("f2_gsc", [128, 40], F32)
    tstat = prog.sb("f2_tstat", [128, NPT * 4], F32)
    cwall = prog.sb("f2_cw", [128, NPT * 8], F32)
    cntacc = prog.sb("f2_cnt", [128, 8], F32)
    lnsq = prog.sb("f2_lnsq", [128, 2 * D], F32)
    lnst = prog.sb("f2_lnst", [128, 8], F32)
    gwb = prog.sb("f2_gwb", [128, NC_D * 8], BF16)
    simt = prog.sb("f2_simt", [8, 8], F32)
    small = prog.sb("f2_small", [128, 24], F32)
    emapb = prog.sb("f2_emapb", [128, 8], F32)
    wgb = prog.sb("f2_wgb", [128, NC_D * 1024], BF16)
    wub = prog.sb("f2_wub", [128, NC_D * 1024], BF16)
    wdb = prog.sb("f2_wdb", [128, NC_D * 1024], BF16)

    ps_tp = prog.ps("f2_ps_tp", [128, 128], BF16)
    ps_g = prog.ps("f2_ps_g", [128, 8], F32)
    ps_h = [prog.ps(f"f2_ps_h{i}", [128, 512], F32) for i in range(3)]
    ps_y = [prog.ps(f"f2_ps_y{i}", [128, 512], F32) for i in range(2)]

    if mode in ("all", "pre"):
        prog.emit("sync", lambda e: e.dma_start(
            out=gwb[:].rearrange("p (c n) -> p c n", c=NC_D),
            in_=WB["gate_w"].rearrange("(c p) n -> p c n", p=128)),
            waits=[(s_cvt, cvt_done)], inc=s_lwi, inc_by=16)
    if mode in ("all", "post"):
        prog.emit("sync", lambda e: e.dma_start(out=simt[:], in_=W["sim"]), inc=s_lwi, inc_by=16)
    init_lw = prog.n(s_lwi)
    prog.barrier("tensor", [(s_lwi, init_lw)])
    prog.barrier("vector", [(s_lwi, init_lw)])

    if mode in ("all", "pre"):
        prog.emit("vector", lambda e: e.memset(cntacc[:], 0.0), inc=s_v)
        for ti in range(T // 512):
            prog.barrier("vector", [(s_xf, prog.n(s_xf))])
            for sp in range(4):
                pt = ti * 4 + sp
                emit_x_ptile_prep(prog, x_dram, pt * 128, x_tm, x_tmb, x_fm, sp * 128, 512,
                                  ps_tp, s_lx, s_v, s_mm)
            xfm_done = prog.n(s_v)
            prog.emit("sync", lambda e, ti=ti: e.dma_start(out=xfm_dram[ti], in_=x_fm[:, 0:NC_D * 512]),
                      waits=[(s_v, xfm_done), (s_xf, prog.n(s_xf))], inc=s_xf, inc_by=16)
            for sp in range(4):
                pt = ti * 4 + sp
                def g_mm(e, sp=sp):
                    r = None
                    for c in range(NC_D):
                        r = e.matmul(ps_g[:], x_fm[:, c * 512 + sp * 128: c * 512 + sp * 128 + 128],
                                     gwb[:, c * 8: c * 8 + 8], start=(c == 0), stop=(c == NC_D - 1))
                    return r
                prog.emit("tensor", g_mm,
                          waits=[(s_v, xfm_done), (s_v, prog.n(s_v)), (s_s, prog.n(s_s))],
                          inc=s_mm)
                gmm = prog.n(s_mm)
                emit_softmax_top2(prog, ps_g, gsc, None, gmm, s_v, s_s, s_mm, consts, renorm=True)
                prog.emit("vector", lambda e, pt=pt: e.tensor_copy(
                    out=tstat[:, pt * 4: pt * 4 + 4], in_=gsc[:, 35:39]),
                    waits=[(s_v, prog.n(s_v))], inc=s_v)
                prog.emit("vector", lambda e: e.tensor_scalar(
                    out=gsc[:, 24:32], in0=consts.iota8[:], scalar1=gsc[:, 36:37], scalar2=None,
                    op0=ALU.is_equal),
                    waits=[(s_v, prog.n(s_v))], inc=s_v)
                prog.emit("vector", lambda e: e.tensor_tensor(
                    out=cntacc[:], in0=cntacc[:], in1=gsc[:, 24:32], op=ALU.add),
                    waits=[(s_v, prog.n(s_v))], inc=s_v)

    cnt_done = prog.n(s_v)
    if mode in ("all", "pre"):
        prog.emit("tensor", lambda e: e.matmul(ps_g[0:1, :], consts.onescolf[:],
                                               cntacc[:], start=True, stop=True),
                  waits=[(s_v, cnt_done), (s_s, prog.n(s_s))], inc=s_mm)
        prog.emit("vector", lambda e: e.tensor_copy(out=small[0:1, 0:8], in_=ps_g[0:1, :]),
                  waits=[(s_mm, prog.n(s_mm))], inc=s_v)
    if mode == "pre":
        prog.emit("sync", lambda e: e.dma_start(out=cnt_out, in_=small[0:1, 0:8]),
                  waits=[(s_v, prog.n(s_v))], inc=s_st, inc_by=16)
        prog.emit("sync", lambda e: e.dma_start(out=tstat_out, in_=tstat[:]),
                  waits=[(s_v, prog.n(s_v))], inc=s_st, inc_by=16)
        prog.barrier("sync", [(s_st, prog.n(s_st))])
        prog.block_end([s_ld, s_lx, s_lwi, s_lgu, s_lwd, s_cvt, s_xf, s_cc, s_ccd,
                        s_v, s_s, s_mm, s_st])
        return
    if mode == "post":
        prog.emit("sync", lambda e: e.dma_start(out=tstat[:], in_=tstat_in), inc=s_ld, inc_by=16)
        prog.emit("sync", lambda e: e.dma_start(out=small[0:8, 0:8], in_=cnt_all), inc=s_ld, inc_by=16)
        cl = prog.n(s_ld)
        prog.barrier("tensor", [(s_ld, cl)])
        prog.barrier("vector", [(s_ld, cl)])
        prog.emit("tensor", lambda e: e.matmul(ps_g[0:1, :], consts.onescolf[0:8, :],
                                               small[0:8, 0:8], start=True, stop=True),
                  waits=[(s_ld, cl), (s_v, prog.n(s_v)), (s_s, prog.n(s_s))], inc=s_mm)
        prog.emit("vector", lambda e: e.tensor_copy(out=small[0:1, 8:16], in_=ps_g[0:1, :]),
                  waits=[(s_mm, prog.n(s_mm))], inc=s_v)
    elif mode == "all":
        prog.emit("sync", lambda e: e.dma_start(out=cc_in, in_=small[0:1, 0:8]),
                  waits=[(s_v, prog.n(s_v))], inc=s_cc, inc_by=16)
        if n_cores > 1:
            prog.emit("gpsimd", lambda e: e.collective_compute(
                "AllReduce", ALU.add, [list(range(n_cores))], [cc_in], [cc_out]),
                waits=[(s_cc, prog.n(s_cc))], inc=s_ccd, inc_by=16)
        else:
            prog.emit("gpsimd", lambda e: e.dma_start(out=cc_out, in_=cc_in),
                      waits=[(s_cc, prog.n(s_cc))], inc=s_ccd, inc_by=16)
        prog.emit("sync", lambda e: e.dma_start(out=small[0:1, 8:16], in_=cc_out),
                  waits=[(s_ccd, prog.n(s_ccd))], inc=s_ld, inc_by=16)
        cnt_ld = prog.n(s_ld)
    prog.emit("vector", lambda e: e.tensor_scalar(
        out=small[0:1, 8:16], in0=small[0:1, 8:16], scalar1=0.0, scalar2=None, op0=ALU.is_gt),
        waits=[(s_ld, prog.n(s_ld)), (s_v, prog.n(s_v))], inc=s_v)
    prog.emit("tensor", lambda e: e.matmul(ps_g[0:8, :], consts.ones1f[:, 0:8],
                                           small[0:1, 8:16], start=True, stop=True),
              waits=[(s_v, prog.n(s_v)), (s_s, prog.n(s_s))], inc=s_mm)
    prog.emit("vector", lambda e: e.tensor_copy(out=small[0:8, 16:24], in_=ps_g[0:8, :]),
              waits=[(s_mm, prog.n(s_mm))], inc=s_v)
    prog.emit("vector", lambda e: e.tensor_scalar(
        out=small[0:8, 0:8], in0=simt[:], scalar1=BIG, scalar2=None, op0=ALU.add),
        waits=[(s_v, prog.n(s_v))], inc=s_v)
    prog.emit("vector", lambda e: e.tensor_tensor(
        out=small[0:8, 0:8], in0=small[0:8, 0:8], in1=small[0:8, 16:24], op=ALU.mult),
        waits=[(s_v, prog.n(s_v))], inc=s_v)
    prog.emit("vector", lambda e: e.tensor_scalar(
        out=small[0:8, 0:8], in0=small[0:8, 0:8], scalar1=-BIG, scalar2=None, op0=ALU.add),
        waits=[(s_v, prog.n(s_v))], inc=s_v)
    prog.emit("vector", lambda e: e.reduce_max(out=small[0:8, 16:17], in_=small[0:8, 0:8], axis=AX.X),
              waits=[(s_v, prog.n(s_v))], inc=s_v)
    prog.emit("vector", lambda e: e.tensor_scalar(
        out=small[0:8, 0:8], in0=small[0:8, 0:8], scalar1=small[0:8, 16:17], scalar2=None,
        op0=ALU.is_equal),
        waits=[(s_v, prog.n(s_v))], inc=s_v)
    prog.emit("vector", lambda e: e.tensor_scalar(
        out=gsc[0:8, 0:8], in0=consts.iota8[0:8, :], scalar1=-IBIG, scalar2=None, op0=ALU.add),
        waits=[(s_v, prog.n(s_v))], inc=s_v)
    prog.emit("vector", lambda e: e.tensor_tensor(
        out=small[0:8, 0:8], in0=small[0:8, 0:8], in1=gsc[0:8, 0:8], op=ALU.mult),
        waits=[(s_v, prog.n(s_v))], inc=s_v)
    prog.emit("vector", lambda e: e.tensor_scalar(
        out=small[0:8, 0:8], in0=small[0:8, 0:8], scalar1=IBIG, scalar2=None, op0=ALU.add),
        waits=[(s_v, prog.n(s_v))], inc=s_v)
    prog.emit("vector", lambda e: e.tensor_reduce(
        out=small[0:8, 17:18], in_=small[0:8, 0:8], axis=AX.X, op=ALU.min),
        waits=[(s_v, prog.n(s_v))], inc=s_v)
    prog.emit("tensor", lambda e: e.matmul(ps_g[0:8, 0:1], small[0:1, 8:16],
                                           consts.ones1f[:, 0:1], start=True, stop=True),
              waits=[(s_v, prog.n(s_v))], inc=s_mm)
    prog.emit("vector", lambda e: e.tensor_copy(out=small[0:8, 18:19], in_=ps_g[0:8, 0:1]),
              waits=[(s_mm, prog.n(s_mm))], inc=s_v)
    prog.emit("vector", lambda e: e.tensor_scalar(
        out=small[0:8, 19:20], in0=small[0:8, 16:17], scalar1=0.7, scalar2=None, op0=ALU.is_ge),
        waits=[(s_v, prog.n(s_v))], inc=s_v)
    prog.emit("vector", lambda e: e.tensor_scalar(
        out=small[0:8, 20:21], in0=small[0:8, 18:19], scalar1=-1.0, scalar2=1.0,
        op0=ALU.mult, op1=ALU.add),
        waits=[(s_v, prog.n(s_v))], inc=s_v)
    prog.emit("vector", lambda e: e.tensor_tensor(
        out=small[0:8, 19:20], in0=small[0:8, 19:20], in1=small[0:8, 20:21], op=ALU.mult),
        waits=[(s_v, prog.n(s_v))], inc=s_v)
    prog.emit("vector", lambda e: e.tensor_tensor(
        out=small[0:8, 21:22], in0=small[0:8, 19:20], in1=small[0:8, 17:18], op=ALU.mult),
        waits=[(s_v, prog.n(s_v))], inc=s_v)
    prog.emit("vector", lambda e: e.tensor_scalar(
        out=small[0:8, 22:23], in0=small[0:8, 19:20], scalar1=-1.0, scalar2=1.0,
        op0=ALU.mult, op1=ALU.add),
        waits=[(s_v, prog.n(s_v))], inc=s_v)
    prog.emit("vector", lambda e: e.tensor_tensor(
        out=small[0:8, 22:23], in0=small[0:8, 22:23], in1=consts.iotacol[0:8, :], op=ALU.mult),
        waits=[(s_v, prog.n(s_v))], inc=s_v)
    prog.emit("vector", lambda e: e.tensor_tensor(
        out=small[0:8, 21:22], in0=small[0:8, 21:22], in1=small[0:8, 22:23], op=ALU.add),
        waits=[(s_v, prog.n(s_v))], inc=s_v)
    prog.emit("vector", lambda e: e.tensor_copy(out=gsc[0:8, 8:16], in_=consts.identf[0:8, 0:8]),
              waits=[(s_v, prog.n(s_v))], inc=s_v)
    prog.emit("tensor", lambda e: e.matmul(ps_g[0:1, :], small[0:8, 21:22],
                                           gsc[0:8, 8:16], start=True, stop=True),
              waits=[(s_v, prog.n(s_v))], inc=s_mm)
    prog.emit("vector", lambda e: e.tensor_copy(out=small[0:1, 8:16], in_=ps_g[0:1, :]),
              waits=[(s_mm, prog.n(s_mm))], inc=s_v)
    prog.emit("tensor", lambda e: e.matmul(ps_g[:, 0:8], consts.ones1f[:],
                                           small[0:1, 8:16], start=True, stop=True),
              waits=[(s_v, prog.n(s_v))], inc=s_mm)
    prog.emit("vector", lambda e: e.tensor_copy(out=emapb[:], in_=ps_g[:, 0:8]),
              waits=[(s_mm, prog.n(s_mm))], inc=s_v)
    emapb_done = prog.n(s_v)

    for pt in range(NPT):
        prog.emit("vector", lambda e, pt=pt: e.tensor_scalar(
            out=gsc[:, 0:8], in0=consts.iota8[:], scalar1=tstat[:, pt * 4 + 1: pt * 4 + 2],
            scalar2=None, op0=ALU.is_equal),
            waits=[(s_v, emapb_done), (s_v, prog.n(s_v))], inc=s_v)
        prog.emit("vector", lambda e: e.tensor_tensor(
            out=gsc[:, 0:8], in0=gsc[:, 0:8], in1=emapb[:], op=ALU.mult),
            waits=[(s_v, prog.n(s_v))], inc=s_v)
        prog.emit("vector", lambda e: e.reduce_sum(out=gsc[:, 32:33], in_=gsc[:, 0:8], axis=AX.X),
                  waits=[(s_v, prog.n(s_v))], inc=s_v)
        prog.emit("vector", lambda e, pt=pt: e.tensor_scalar(
            out=gsc[:, 0:8], in0=consts.iota8[:], scalar1=tstat[:, pt * 4 + 3: pt * 4 + 4],
            scalar2=None, op0=ALU.is_equal),
            waits=[(s_v, prog.n(s_v))], inc=s_v)
        prog.emit("vector", lambda e: e.tensor_tensor(
            out=gsc[:, 0:8], in0=gsc[:, 0:8], in1=emapb[:], op=ALU.mult),
            waits=[(s_v, prog.n(s_v))], inc=s_v)
        prog.emit("vector", lambda e: e.reduce_sum(out=gsc[:, 33:34], in_=gsc[:, 0:8], axis=AX.X),
                  waits=[(s_v, prog.n(s_v))], inc=s_v)
        prog.emit("vector", lambda e: e.tensor_scalar(
            out=gsc[:, 0:8], in0=consts.iota8[:], scalar1=gsc[:, 32:33], scalar2=None,
            op0=ALU.is_equal),
            waits=[(s_v, prog.n(s_v))], inc=s_v)
        prog.emit("vector", lambda e, pt=pt: e.tensor_scalar(
            out=gsc[:, 0:8], in0=gsc[:, 0:8], scalar1=tstat[:, pt * 4: pt * 4 + 1], scalar2=None,
            op0=ALU.mult),
            waits=[(s_v, prog.n(s_v))], inc=s_v)
        prog.emit("vector", lambda e: e.tensor_scalar(
            out=gsc[:, 8:16], in0=consts.iota8[:], scalar1=gsc[:, 33:34], scalar2=None,
            op0=ALU.is_equal),
            waits=[(s_v, prog.n(s_v))], inc=s_v)
        prog.emit("vector", lambda e, pt=pt: e.scalar_tensor_tensor(
            out=cwall[:, pt * 8:(pt + 1) * 8], in0=gsc[:, 8:16],
            scalar=tstat[:, pt * 4 + 2: pt * 4 + 3], in1=gsc[:, 0:8],
            op0=ALU.mult, op1=ALU.add),
            waits=[(s_v, prog.n(s_v))], inc=s_v)
    cw_done = prog.n(s_v)

    def load_gu(e_idx, war_mm=0):
        prog.emit("sync", lambda e, e_idx=e_idx: e.dma_start(
            out=wgb[:].rearrange("p (c n) -> p c n", c=NC_D),
            in_=WB["wg"][e_idx].rearrange("(c p) n -> p c n", p=128)),
            waits=[(s_mm, war_mm), (s_cvt, cvt_done)], inc=s_lgu, inc_by=16)
        prog.emit("sync", lambda e, e_idx=e_idx: e.dma_start(
            out=wub[:].rearrange("p (c n) -> p c n", c=NC_D),
            in_=WB["wu"][e_idx].rearrange("(c p) n -> p c n", p=128)),
            inc=s_lgu, inc_by=16)
        return (s_lgu, prog.n(s_lgu))

    def load_wd(e_idx, war_mm=0):
        prog.emit("sync", lambda e, e_idx=e_idx: e.dma_start(
            out=wdb[:].rearrange("p (c n) -> p c n", c=NC_D),
            in_=WB["wd"][e_idx].rearrange("(c p) n -> p c n", p=128)),
            waits=[(s_mm, war_mm), (s_cvt, cvt_done)], inc=s_lwd, inc_by=16)
        return (s_lwd, prog.n(s_lwd))

    for st_i in range(NS):
        stbase = st_i * ST
        war0 = prog.n(s_mm)
        guready = {0: load_gu(0, war_mm=war0)}
        wdready = {0: load_wd(0, war_mm=war0)}
        for tt in range(ntt):
            prog.emit("sync", lambda e, st_i=st_i, tt=tt: e.dma_start(
                out=x_fm[:].rearrange("p (c w) -> p c w", c=NC_D)[:, :, tt * 512:(tt + 1) * 512],
                in_=xfm_dram[st_i * ntt + tt].rearrange("p (c n) -> p c n", c=NC_D)),
                waits=[(s_mm, war0), (s_xf, prog.n(s_xf)), (s_v, prog.n(s_v))],
                inc=s_lx, inc_by=16)
        xfm_ready_ld = prog.n(s_lx)
        prog.barrier("tensor", [(s_lx, xfm_ready_ld)])
        ps_h_war_s = [0, 0, 0]
        ps_h_war_v = [0, 0, 0]
        ps_y_war = [0, 0]
        for e_i in range(E):
            for tt in range(ntt):
                for hc in range(NC_D):
                    pb1 = (2 * hc) % 3
                    pb2 = (2 * hc + 1) % 3
                    def p1mm(e, tt=tt, hc=hc, pb1=pb1):
                        r = None
                        for kc in range(NC_D):
                            r = e.matmul(ps_h[pb1][:],
                                         wgb[:, kc * 1024 + hc * 128: kc * 1024 + hc * 128 + 128],
                                         x_fm[:, kc * ST + tt * 512: kc * ST + tt * 512 + 512],
                                         start=(kc == 0), stop=(kc == NC_D - 1))
                        return r
                    prog.emit("tensor", p1mm,
                              waits=[guready[e_i], (s_s, ps_h_war_s[pb1]), (s_v, ps_h_war_v[pb1])],
                              inc=s_mm)
                    p1_done = prog.n(s_mm)
                    def p2mm(e, tt=tt, hc=hc, pb2=pb2):
                        r = None
                        for kc in range(NC_D):
                            r = e.matmul(ps_h[pb2][:],
                                         wub[:, kc * 1024 + hc * 128: kc * 1024 + hc * 128 + 128],
                                         x_fm[:, kc * ST + tt * 512: kc * ST + tt * 512 + 512],
                                         start=(kc == 0), stop=(kc == NC_D - 1))
                        return r
                    prog.emit("tensor", p2mm,
                              waits=[(s_s, ps_h_war_s[pb2]), (s_v, ps_h_war_v[pb2])],
                              inc=s_mm)
                    p2_done = prog.n(s_mm)
                    def s1act(e, pb1=pb1):
                        return e.activation(out=sg1[:], in_=ps_h[pb1][:], func=AF.Silu)
                    prog.emit("scalar", s1act, waits=[(s_mm, p1_done), (s_v, prog.n(s_v))], inc=s_s)
                    ps_h_war_s[pb1] = prog.n(s_s)
                    s1_done = prog.n(s_s)
                    def glumul(e, hc=hc, pb2=pb2):
                        return e.tensor_tensor(
                            out=hmid[:, hc * 512:(hc + 1) * 512], in0=ps_h[pb2][:], in1=sg1[:],
                            op=ALU.mult)
                    prog.emit("vector", glumul, waits=[(s_mm, p2_done), (s_s, s1_done)], inc=s_v)
                    ps_h_war_v[pb2] = prog.n(s_v)
                glu_ready_v = prog.n(s_v)
                if tt == ntt - 1 and e_i + 1 < E:
                    guready[e_i + 1] = load_gu(e_i + 1, war_mm=prog.n(s_mm))
                for pt in range(4):
                    for dn in range(2):
                        pbuf = (pt * 2 + dn) % 2
                        def wdmm(e, tt=tt, pt=pt, dn=dn, pbuf=pbuf):
                            r = None
                            for kc in range(NC_D):
                                r = e.matmul(ps_y[pbuf][:],
                                             hmid[:, kc * 512 + pt * 128: kc * 512 + pt * 128 + 128],
                                             wdb[:, kc * D + dn * 512: kc * D + dn * 512 + 512],
                                             start=(kc == 0), stop=(kc == NC_D - 1))
                            return r
                        prog.emit("tensor", wdmm,
                                  waits=[(s_v, glu_ready_v), (s_v, ps_y_war[pbuf]), wdready[e_i]],
                                  inc=s_mm)
                        y_done = prog.n(s_mm)
                        aoff = (tt * 4 + pt) * D + dn * 512
                        gpt = (stbase // 128) + tt * 4 + pt
                        ccol = gpt * 8 + e_i
                        if e_i == 0:
                            def comb(e, aoff=aoff, ccol=ccol, pbuf=pbuf):
                                return e.tensor_scalar(out=acc[:, aoff:aoff + 512], in0=ps_y[pbuf][:],
                                                       scalar1=cwall[:, ccol:ccol + 1], scalar2=None,
                                                       op0=ALU.mult)
                        else:
                            def comb(e, aoff=aoff, ccol=ccol, pbuf=pbuf):
                                return e.scalar_tensor_tensor(out=acc[:, aoff:aoff + 512],
                                                              in0=ps_y[pbuf][:],
                                                              scalar=cwall[:, ccol:ccol + 1],
                                                              in1=acc[:, aoff:aoff + 512],
                                                              op0=ALU.mult, op1=ALU.add)
                        prog.emit("vector", comb, waits=[(s_mm, y_done), (s_v, cw_done)], inc=s_v)
                        ps_y_war[pbuf] = prog.n(s_v)
            if e_i + 1 < E:
                wdready[e_i + 1] = load_wd(e_i + 1, war_mm=prog.n(s_mm))
        for pt in range(ST // 128):
            t0 = stbase + pt * 128
            prog.emit("sync", lambda e, t0=t0: e.dma_start(
                out=x_tm[:, 0:D], in_=x_dram[t0:t0 + 128, :]),
                waits=[(s_v, prog.n(s_v))], inc=s_lx, inc_by=16)
            ld = prog.n(s_lx)
            emit_ln_epilogue(prog, x_tm[:, 0:D], acc[:, pt * D:(pt + 1) * D],
                             y_dram[t0:t0 + 128, :], {"sq": lnsq, "st": lnst},
                             s_v, s_s, s_st, waits_v=[(s_lx, ld), (s_v, prog.n(s_v)),
                                                      (s_s, prog.n(s_s)),
                                                      (s_st, prog.n(s_st))])
    prog.barrier("sync", [(s_st, prog.n(s_st))])
    prog.block_end([s_ld, s_lx, s_lwi, s_lgu, s_lwd, s_cvt, s_xf, s_cc, s_ccd, s_v, s_s, s_mm, s_st])




L0ATT_SPEC = {"Wq": (1024, 1024), "Wk": (1024, 1024), "Wv": (1024, 1024),
              "Wb": (1024, 4), "convq": (128, 32), "convk": (128, 32),
              "convv": (128, 32), "firs": (128, 24), "firl": (128, 248),
              "Wf1": (4096, 2048), "bf1c": (128, 16), "Wf2": (2048, 16),
              "bf2row": (1, 16), "ltau": (1, 4), "normw": (1, 256), "Wo": (1024, 1024)}


def prep_l0att_weights(inputs):
    cc = np.ascontiguousarray
    def fmcols(w, k):
        return cc(w.reshape(8, 128, k).transpose(1, 0, 2).reshape(128, 8 * k))
    return {
        "Wq": cc(inputs["L0_attn_W_q"]), "Wk": cc(inputs["L0_attn_W_k"]),
        "Wv": cc(inputs["L0_attn_W_v"]), "Wb": cc(inputs["L0_attn_W_b"]),
        "convq": fmcols(inputs["L0_attn_conv_w_q"], 4),
        "convk": fmcols(inputs["L0_attn_conv_w_k"], 4),
        "convv": fmcols(inputs["L0_attn_conv_w_v"], 4),
        "firs": fmcols(inputs["L0_attn_fir_short_w"].reshape(1024, 3), 3),
        "firl": fmcols(inputs["L0_attn_fir_long_w"].reshape(1024, 31), 31),
        "Wf1": cc(inputs["L0_attn_W_fuse1"]),
        "bf1c": cc(inputs["L0_attn_b_fuse1"].reshape(16, 128).T),
        "Wf2": cc(inputs["L0_attn_W_fuse2"]),
        "bf2row": cc(inputs["L0_attn_b_fuse2"].reshape(1, 16)),
        "ltau": cc(inputs["L0_attn_others_log_tau"].reshape(1, 4)),
        "normw": cc(inputs["L0_attn_norm_w"].reshape(1, 256)),
        "Wo": cc(inputs["L0_attn_W_o"]),
    }


def host_consts_l0():
    sl = np.tril(np.ones((128, 128), np.float32), -1)
    su = np.triu(np.ones((128, 128), np.float32), 1)
    ui = np.triu(np.ones((128, 128), np.float32), 0)
    return {"c_msl_neg": -sl, "c_msu_neg": -su, "c_mui": ui}


def emit_l0att(prog, C, x_dram, y_dram, W):
    T, Lseq = C["T"], C["L"]
    NSEQ = T // Lseq
    NT = Lseq // 128
    NTT = Lseq // 512
    NT4 = Lseq // 256
    H = 4
    consts = prog.consts
    nc = prog.nc

    s_lx = prog.sem("a0_lx"); s_lw = prog.sem("a0_lw"); s_lw1 = prog.sem("a0_lw1")
    s_lsc = prog.sem("a0_lsc")
    s_v = prog.sem("a0_v"); s_s = prog.sem("a0_s"); s_g = prog.sem("a0_g")
    s_mm = prog.sem("a0_mm"); s_st = prog.sem("a0_st"); s_scr = prog.sem("a0_scr")
    s_cvt = prog.sem("a0_cvt")
    ALLSEMS = [s_lx, s_lw, s_lw1, s_lsc, s_v, s_s, s_g, s_mm, s_st, s_scr, s_cvt]
    prog.block_begin()

    WB, cvt_done = emit_weight_convert(prog, "l0a", W,
                                       ["Wq", "Wk", "Wv", "Wb", "Wf1", "Wf2",
                                        "bf2row", "Wo", "normw"], s_cvt)

    def scr(name, width):
        return nc.dram_tensor(f"l0a_{name}_scr", [NSEQ, 128, width], BF16, kind="Internal").ap()
    praw_scr = [scr(f"praw{i}", NC_D * Lseq) for i in range(3)]
    qs_scr = scr("qs", NC_D * Lseq)
    ks_scr = scr("ks", NC_D * Lseq)
    vs_scr = scr("vs", NC_D * Lseq)
    firs_scr = scr("fs", NC_D * Lseq)
    firl_scr = scr("fl", NC_D * Lseq)
    dfm_scr = scr("dfm", NC_D * Lseq)
    dtm_scr = scr("dtm", NT * 1024)
    onorm_scr = scr("onorm", NT * 1024)

    mslf = prog.sb("a0_mslf", [128, 128], F32)
    msuf = prog.sb("a0_msuf", [128, 128], F32)
    muif = prog.sb("a0_muif", [128, 128], F32)
    for nm, t in [("c_msl_neg", mslf), ("c_msu_neg", msuf), ("c_mui", muif)]:
        ap = nc.dram_tensor(nm, [128, 128], F32, kind="ExternalInput").ap()
        prog.emit("sync", lambda e, t=t, ap=ap: e.dma_start(out=t[:], in_=ap), inc=s_lw1, inc_by=16)
    x_tm = prog.sb("a0_x_tm", [128, D], F32)
    x_tmb = prog.sb("a0_x_tmb", [128, D], BF16)
    bcol = prog.sb("a0_bcol", [128, NT * 4], F32)
    logit_sb = prog.sb("a0_logit", [128, NT * 16], F32)
    rtau = prog.sb("a0_rtau", [128, 16], F32)
    taurow = prog.sb("a0_taurow", [1, 16], F32)
    ltrow = prog.sb("a0_ltrow", [1, 4], F32)
    nwb = prog.sb("a0_nwb", [128, 256], F32)
    nwrow = prog.sb("a0_nwrow", [1, 256], BF16)
    gcol = prog.sb("a0_gcol", [128, 16], F32)
    l0sc = prog.sb("a0_l0sc", [128, 40], F32)
    prog.l0_sc = l0sc
    convc = prog.sb("a0_convc", [128, NC_D * 4 * 3], F32)
    fircs = prog.sb("a0_fircs", [128, NC_D * 3], F32)
    fircl = prog.sb("a0_fircl", [128, NC_D * 31], F32)
    ltau_in = nc.dram_tensor("l0a_ltau", [1, 4], F32, kind="ExternalInput").ap()
    prog.emit("sync", lambda e: e.dma_start(out=ltrow[:], in_=ltau_in), inc=s_lw1, inc_by=16)
    for i, ckey in enumerate(["convq", "convk", "convv"]):
        prog.emit("sync", lambda e, i=i, ckey=ckey: e.dma_start(
            out=convc[:, i * 32:(i + 1) * 32], in_=W[ckey]), inc=s_lw1, inc_by=16)
    prog.emit("sync", lambda e: e.dma_start(out=fircs[:], in_=W["firs"]), inc=s_lw1, inc_by=16)
    prog.emit("sync", lambda e: e.dma_start(out=fircl[:], in_=W["firl"]), inc=s_lw1, inc_by=16)
    prog.emit("sync", lambda e: e.dma_start(out=nwrow[:], in_=WB["normw"]),
              waits=[(s_cvt, cvt_done)], inc=s_lw1, inc_by=16)
    misc_w = prog.n(s_lw1)
    prog.emit("scalar", lambda e: e.activation(out=ltrow[:], in_=ltrow[:], func=AF.Exp),
              waits=[(s_lw1, misc_w), (s_s, prog.n(s_s))], inc=s_s)
    prog.emit("vector", lambda e: e.reciprocal(out=ltrow[:], in_=ltrow[:]),
              waits=[(s_s, prog.n(s_s)), (s_v, prog.n(s_v))], inc=s_v)
    prog.emit("vector", lambda e: e.tensor_copy(
        out=taurow[:].rearrange("a (h j) -> a h j", h=4)[:, :, 0:3],
        in_=ltrow[:].rearrange("a (h j) -> a h j", j=1).broadcast_to((1, 4, 3))),
        waits=[(s_v, prog.n(s_v))], inc=s_v)
    prog.emit("vector", lambda e: e.memset(
        taurow[:].rearrange("a (h j) -> a h j", h=4)[:, :, 3:4], 1.0),
        waits=[(s_v, prog.n(s_v))], inc=s_v)
    with prog.scope() as es0:
        ps_b0 = prog.ps_s(es0, "a0_ps_b0", [128, 256], F32)
        prog.emit("tensor", lambda e: e.matmul(ps_b0[:, 0:16], consts.ones1f[:], taurow[:],
                                               start=True, stop=True),
                  waits=[(s_v, prog.n(s_v)), (s_s, prog.n(s_s))], inc=s_mm)
        prog.emit("vector", lambda e: e.tensor_copy(out=rtau[:], in_=ps_b0[:, 0:16]),
                  waits=[(s_mm, prog.n(s_mm))], inc=s_v)
        prog.emit("tensor", lambda e: e.matmul(ps_b0[:], consts.ones1[:], nwrow[:],
                                               start=True, stop=True),
                  waits=[(s_lw1, misc_w), (s_v, prog.n(s_v))], inc=s_mm)
        prog.emit("vector", lambda e: e.tensor_copy(out=nwb[:], in_=ps_b0[:]),
                  waits=[(s_mm, prog.n(s_mm))], inc=s_v)
        prog.fence(ALLSEMS)

    def _pass1(sq):
        sb0 = sq * Lseq
        with prog.scope() as es:
            x_fm = prog.sb_s(es, "a0_x_fm", [128, NC_D * 512], BF16)
            w3b = [prog.sb_s(es, f"a0_w3b{i}", [128, NC_D * 1024], BF16) for i in range(3)]
            wbb = prog.sb_s(es, "a0_wbb", [128, NC_D * 4], BF16)
            pout = prog.sb_s(es, "a0_pout", [128, NC_D * 512], BF16)
            ps_tp = prog.ps_s(es, "a0_ps_tp", [128, 128], BF16)
            ps_p = [prog.ps_s(es, f"a0_ps_p{i}", [128, 512], F32) for i in range(2)]
            ps_b = prog.ps_s(es, "a0_ps_b", [128, 16], F32)
            prog.emit("sync", lambda e: e.dma_start(
                out=wbb[:].rearrange("p (c n) -> p c n", c=NC_D),
                in_=WB["Wb"].rearrange("(c p) n -> p c n", p=128)),
                waits=[(s_cvt, cvt_done)], inc=s_lw, inc_by=16)
            for i, wkey in enumerate(["Wq", "Wk", "Wv"]):
                prog.emit("sync", lambda e, i=i, wkey=wkey: e.dma_start(
                    out=w3b[i][:].rearrange("p (c n) -> p c n", c=NC_D),
                    in_=WB[wkey].rearrange("(c p) n -> p c n", p=128)),
                    inc=s_lw, inc_by=16)
            w_ready = prog.n(s_lw)
            prog.barrier("tensor", [(s_lw, w_ready)])
            for tt in range(NTT):
                for sp in range(4):
                    emit_x_ptile_prep(prog, x_dram, sb0 + tt * 512 + sp * 128,
                                      x_tm, x_tmb, x_fm, sp * 128, 512,
                                      ps_tp, s_lx, s_v, s_mm)
                xfm_ready = prog.n(s_v)
                for sp in range(4):
                    def bmm(e, sp=sp):
                        r = None
                        for c in range(NC_D):
                            r = e.matmul(ps_b[:, 0:4],
                                         x_fm[:, c * 512 + sp * 128: c * 512 + sp * 128 + 128],
                                         wbb[:, c * 4: c * 4 + 4],
                                         start=(c == 0), stop=(c == NC_D - 1))
                        return r
                    prog.emit("tensor", bmm,
                              waits=[(s_v, xfm_ready), (s_v, prog.n(s_v)), (s_s, prog.n(s_s))],
                              inc=s_mm)
                    prog.emit("scalar", lambda e, tt=tt, sp=sp: e.activation(
                        out=bcol[:, (tt * 4 + sp) * 4:(tt * 4 + sp) * 4 + 4],
                        in_=ps_b[:, 0:4], func=AF.Sigmoid),
                        waits=[(s_mm, prog.n(s_mm)), (s_s, prog.n(s_s))], inc=s_s)
                for pi in range(3):
                    for oc in range(NC_D):
                        pb = oc % 2
                        def pmm(e, pi=pi, oc=oc, pb=pb):
                            r = None
                            for kc in range(NC_D):
                                r = e.matmul(ps_p[pb][:],
                                             w3b[pi][:, kc * 1024 + oc * 128: kc * 1024 + oc * 128 + 128],
                                             x_fm[:, kc * 512:(kc + 1) * 512],
                                             start=(kc == 0), stop=(kc == NC_D - 1))
                            return r
                        prog.emit("tensor", pmm,
                                  waits=[(s_v, xfm_ready), (s_s, prog.n(s_s)), (s_v, prog.n(s_v))],
                                  inc=s_mm)
                        prog.emit("vector", lambda e, oc=oc, pb=pb: e.tensor_copy(
                            out=pout[:, oc * 512:(oc + 1) * 512], in_=ps_p[pb][:]),
                            waits=[(s_mm, prog.n(s_mm)), (s_scr, prog.n(s_scr))], inc=s_v)
                    pr_done = prog.n(s_v)
                    prog.emit("sync", lambda e, pi=pi, sq=sq, tt=tt: e.dma_start(
                        out=praw_scr[pi][sq][:, :].rearrange("p (c w) -> p c w", c=NC_D)[:, :, tt * 512:(tt + 1) * 512],
                        in_=pout[:].rearrange("p (c n) -> p c n", c=NC_D)),
                        waits=[(s_v, pr_done)], inc=s_scr, inc_by=16)
            prog.fence(ALLSEMS)

    
    def _pass2(sq):
        sb0 = sq * Lseq
        with prog.scope() as es:
            chalo = prog.sb_s(es, "a0_chalo", [128, 32 + Lseq], BF16)
            cacc = prog.sb_s(es, "a0_cacc", [128, Lseq], F32)
            cacc2 = prog.sb_s(es, "a0_cacc2", [128, Lseq], F32)
            cout = prog.sb_s(es, "a0_cout", [128, Lseq], BF16)
            cout2 = prog.sb_s(es, "a0_cout2", [128, Lseq], BF16)
            for pi, outsc in [(0, qs_scr), (1, ks_scr), (2, vs_scr)]:
                for c in range(NC_D):
                    def ldh(e, pi=pi, sq=sq, c=c):
                        return e.dma_start(out=chalo[:, 32:32 + Lseq],
                                           in_=praw_scr[pi][sq][:, c * Lseq:(c + 1) * Lseq])
                    prog.emit("sync", ldh, waits=[(s_v, prog.n(s_v)), (s_s, prog.n(s_s))],
                              inc=s_lsc, inc_by=16)
                    hl = prog.n(s_lsc)
                    prog.emit("vector", lambda e: e.memset(chalo[:, 29:32], 0.0),
                              waits=[(s_lsc, hl)], inc=s_v)
                    for j in range(4):
                        scol = 32 * pi + c * 4 + j
                        if j == 0:
                            def tap(e, j=j, scol=scol):
                                return e.tensor_scalar(out=cacc[:], in0=chalo[:, 29 + j:29 + j + Lseq],
                                                       scalar1=convc[:, scol:scol + 1], scalar2=None,
                                                       op0=ALU.mult)
                        else:
                            def tap(e, j=j, scol=scol):
                                return e.scalar_tensor_tensor(out=cacc[:], in0=chalo[:, 29 + j:29 + j + Lseq],
                                                              scalar=convc[:, scol:scol + 1], in1=cacc[:],
                                                              op0=ALU.mult, op1=ALU.add)
                        prog.emit("vector", tap, waits=[(s_lsc, hl), (s_v, prog.n(s_v)),
                                                        (s_s, prog.n(s_s))], inc=s_v)
                    prog.emit("scalar", lambda e: e.activation(out=cout[:], in_=cacc[:], func=AF.Silu),
                              waits=[(s_v, prog.n(s_v)), (s_s, prog.n(s_s)), (s_scr, prog.n(s_scr))], inc=s_s)
                    prog.emit("sync", lambda e, sq=sq, c=c, outsc=outsc: e.dma_start(
                        out=outsc[sq][:, c * Lseq:(c + 1) * Lseq], in_=cout[:]),
                        waits=[(s_s, prog.n(s_s))], inc=s_scr, inc_by=16)
            for c in range(NC_D):
                def ldh2(e, sq=sq, c=c):
                    return e.dma_start(out=chalo[:, 32:32 + Lseq],
                                       in_=vs_scr[sq][:, c * Lseq:(c + 1) * Lseq])
                prog.emit("sync", ldh2, waits=[(s_v, prog.n(s_v)), (s_g, prog.n(s_g)),
                                               (s_scr, prog.n(s_scr))],
                          inc=s_lsc, inc_by=16)
                hl = prog.n(s_lsc)
                prog.emit("vector", lambda e: e.memset(chalo[:, 0:32], 0.0),
                          waits=[(s_lsc, hl)], inc=s_v)
                hl_v = prog.n(s_v)
                for j in range(3):
                    scol = c * 3 + j
                    if j == 0:
                        def tap(e, j=j, scol=scol):
                            return e.tensor_scalar(out=cacc[:], in0=chalo[:, 30 + j:30 + j + Lseq],
                                                   scalar1=fircs[:, scol:scol + 1], scalar2=None,
                                                   op0=ALU.mult)
                    else:
                        def tap(e, j=j, scol=scol):
                            return e.scalar_tensor_tensor(out=cacc[:], in0=chalo[:, 30 + j:30 + j + Lseq],
                                                          scalar=fircs[:, scol:scol + 1], in1=cacc[:],
                                                          op0=ALU.mult, op1=ALU.add)
                    prog.emit("vector", tap, waits=[(s_lsc, hl), (s_v, hl_v), (s_v, prog.n(s_v))], inc=s_v)
                prog.emit("vector", lambda e: e.tensor_copy(out=cout[:], in_=cacc[:]),
                          waits=[(s_v, prog.n(s_v)), (s_scr, prog.n(s_scr))], inc=s_v)
                prog.emit("sync", lambda e, sq=sq, c=c: e.dma_start(
                    out=firs_scr[sq][:, c * Lseq:(c + 1) * Lseq], in_=cout[:]),
                    waits=[(s_v, prog.n(s_v))], inc=s_scr, inc_by=16)
                for j in range(31):
                    scol = c * 31 + j
                    if j == 0:
                        def tapl(e, j=j, scol=scol):
                            return e.tensor_scalar(out=cacc2[:], in0=chalo[:, 2 + j:2 + j + Lseq],
                                                   scalar1=fircl[:, scol:scol + 1], scalar2=None,
                                                   op0=ALU.mult)
                    else:
                        def tapl(e, j=j, scol=scol):
                            return e.scalar_tensor_tensor(out=cacc2[:], in0=chalo[:, 2 + j:2 + j + Lseq],
                                                          scalar=fircl[:, scol:scol + 1], in1=cacc2[:],
                                                          op0=ALU.mult, op1=ALU.add)
                    prog.emit("vector", tapl, waits=[(s_lsc, hl), (s_v, hl_v), (s_v, prog.n(s_v))], inc=s_v)
                prog.emit("vector", lambda e: e.tensor_copy(out=cout2[:], in_=cacc2[:]),
                          waits=[(s_v, prog.n(s_v)), (s_scr, prog.n(s_scr))], inc=s_v)
                prog.emit("sync", lambda e, sq=sq, c=c: e.dma_start(
                    out=firl_scr[sq][:, c * Lseq:(c + 1) * Lseq], in_=cout2[:]),
                    waits=[(s_v, prog.n(s_v))], inc=s_scr, inc_by=16)
            prog.fence(ALLSEMS)

    def _pass3(sq):
        sb0 = sq * Lseq
        with prog.scope() as es:
            q_tm = prog.sb_s(es, "a0_q_tm", [128, NT * 256], BF16)
            k_tm = prog.sb_s(es, "a0_k_tm", [128, NT * 256], BF16)
            zsrc = prog.sb_s(es, "a0_zsrc", [128, NT * 512], BF16)
            q_fm = prog.sb_s(es, "a0_q_fm", [128, 2 * Lseq], BF16)
            k_fm = prog.sb_s(es, "a0_k_fm", [128, 2 * Lseq], BF16)
            kb_fm = prog.sb_s(es, "a0_kb_fm", [128, 2 * Lseq], BF16)
            nsc = prog.sb_s(es, "a0_nsc", [128, NT * 8], F32)
            sqsc = prog.sb_s(es, "a0_sqsc", [128, 256], F32)
            pw = prog.sb_s(es, "a0_pw", [128, 7 * 128], BF16)
            pwT = prog.sb_s(es, "a0_pwT", [128, 7 * 128], BF16)
            zbuf = [prog.sb_s(es, f"a0_z{i}", [128, 512], BF16) for i in range(2)]
            uadj = prog.sb_s(es, "a0_uadj", [128, 256], BF16)
            lTs = prog.sb_s(es, "a0_lTs", [128, 128], BF16)
            lT_all = prog.sb_s(es, "a0_lT_all", [128, NT * 128], BF16)
            S_sb = prog.sb_s(es, "a0_S", [128, 2 * 256], BF16)
            S_f32 = prog.sb_s(es, "a0_Sf", [128, 2 * 256], F32)
            dh_tm = prog.sb_s(es, "a0_dh_tm", [128, NT * 256], BF16)
            dh_fm = prog.sb_s(es, "a0_dh_fm", [128, 2 * Lseq], BF16)
            ps_tp = prog.ps_s(es, "a0_ps_tp3", [128, 128], BF16)
            ps_A = prog.ps_s(es, "a0_ps_A", [128, 128], F32)
            ps_sq = prog.ps_s(es, "a0_ps_sq", [128, 128], F32)
            ps_z = prog.ps_s(es, "a0_ps_z", [128, 512], F32)
            ps_u = prog.ps_s(es, "a0_ps_u", [128, 256], F32)
            ps_o = prog.ps_s(es, "a0_ps_o", [128, 256], F32)
            ps_S0 = prog.ps_s(es, "a0_ps_S0", [128, 256], F32)
            ps_S1 = prog.ps_s(es, "a0_ps_S1", [128, 256], F32)

            for h in range(H):
                ch0 = 2 * h
                prog.emit("sync", lambda e, sq=sq, ch0=ch0: e.dma_start(
                    out=q_fm[:].rearrange("p (c w) -> p c w", c=2),
                    in_=qs_scr[sq][:, ch0 * Lseq:(ch0 + 2) * Lseq].rearrange("p (c w) -> p c w", c=2)),
                    waits=[(s_v, prog.n(s_v)), (s_mm, prog.n(s_mm))], inc=s_lsc, inc_by=16)
                prog.emit("sync", lambda e, sq=sq, ch0=ch0: e.dma_start(
                    out=k_fm[:].rearrange("p (c w) -> p c w", c=2),
                    in_=ks_scr[sq][:, ch0 * Lseq:(ch0 + 2) * Lseq].rearrange("p (c w) -> p c w", c=2)),
                    inc=s_lsc, inc_by=16)
                prog.emit("sync", lambda e, sq=sq, ch0=ch0: e.dma_start(
                    out=kb_fm[:].rearrange("p (c w) -> p c w", c=2),
                    in_=vs_scr[sq][:, ch0 * Lseq:(ch0 + 2) * Lseq].rearrange("p (c w) -> p c w", c=2)),
                    inc=s_lsc, inc_by=16)
                qkv_ld = prog.n(s_lsc)
                prog.barrier("tensor", [(s_lsc, qkv_ld)])
                for ci in range(NT):
                    for half in range(2):
                        for (srcb, dstb, do, w_) in [(q_fm, q_tm, 256, 128), (k_fm, k_tm, 256, 128),
                                                     (kb_fm, zsrc, 512, 128)]:
                            def tpx(e, ci=ci, half=half, srcb=srcb):
                                return e.transpose(ps_tp[:], srcb[:, half * Lseq + ci * 128: half * Lseq + ci * 128 + 128],
                                                   consts.ident[:])
                            prog.emit("tensor", tpx, waits=[(s_v, prog.n(s_v))], inc=s_mm)
                            prog.emit("vector", lambda e, ci=ci, half=half, dstb=dstb, do=do: e.tensor_copy(
                                out=dstb[:, ci * do + half * 128: ci * do + half * 128 + 128], in_=ps_tp[:]),
                                waits=[(s_mm, prog.n(s_mm))], inc=s_v)
                tp_done = prog.n(s_v)
                for which, tmb in [(0, q_tm), (1, k_tm)]:
                    for ci in range(NT):
                        prog.emit("scalar", lambda e, ci=ci, tmb=tmb: e.activation(
                            out=sqsc[:, 0:256], in_=tmb[:, ci * 256:(ci + 1) * 256], func=AF.Square,
                            accum_out=nsc[:, ci * 8 + which: ci * 8 + which + 1] if False else None),
                            waits=[(s_v, tp_done), (s_s, prog.n(s_s)), (s_v, prog.n(s_v))], inc=s_s)
                        prog.emit("vector", lambda e, ci=ci, which=which: e.reduce_sum(
                            out=nsc[:, ci * 8 + which: ci * 8 + which + 1], in_=sqsc[:, 0:256], axis=AX.X),
                            waits=[(s_s, prog.n(s_s)), (s_v, prog.n(s_v))], inc=s_v)
                        prog.emit("vector", lambda e, ci=ci, which=which: e.tensor_scalar(
                            out=nsc[:, ci * 8 + 2 + which: ci * 8 + 3 + which],
                            in0=nsc[:, ci * 8 + which: ci * 8 + which + 1],
                            scalar1=1e-6, scalar2=None, op0=ALU.add),
                            waits=[(s_v, prog.n(s_v))], inc=s_v)
                        prog.emit("vector", lambda e, ci=ci, which=which: e.reciprocal(
                            out=nsc[:, ci * 8 + 4 + which: ci * 8 + 5 + which],
                            in_=nsc[:, ci * 8 + 2 + which: ci * 8 + 3 + which]),
                            waits=[(s_v, prog.n(s_v))], inc=s_v)
                        prog.emit("scalar", lambda e, ci=ci, which=which: e.activation(
                            out=nsc[:, ci * 8 + 6 + which: ci * 8 + 7 + which],
                            in_=nsc[:, ci * 8 + 4 + which: ci * 8 + 5 + which], func=AF.Sqrt),
                            waits=[(s_v, prog.n(s_v)), (s_s, prog.n(s_s))], inc=s_s)
                rs_done = prog.n(s_s)
                for ci in range(NT):
                    prog.emit("vector", lambda e, ci=ci: e.tensor_scalar(
                        out=q_tm[:, ci * 256:(ci + 1) * 256], in0=q_tm[:, ci * 256:(ci + 1) * 256],
                        scalar1=nsc[:, ci * 8 + 6: ci * 8 + 7], scalar2=None, op0=ALU.mult),
                        waits=[(s_s, rs_done), (s_v, prog.n(s_v))], inc=s_v)
                    prog.emit("vector", lambda e, ci=ci: e.tensor_scalar(
                        out=k_tm[:, ci * 256:(ci + 1) * 256], in0=k_tm[:, ci * 256:(ci + 1) * 256],
                        scalar1=nsc[:, ci * 8 + 7: ci * 8 + 8], scalar2=None, op0=ALU.mult),
                        waits=[(s_v, prog.n(s_v))], inc=s_v)
                    prog.emit("vector", lambda e, ci=ci, h=h: e.tensor_scalar(
                        out=zsrc[:, ci * 512: ci * 512 + 256], in0=zsrc[:, ci * 512: ci * 512 + 256],
                        scalar1=bcol[:, ci * 4 + h: ci * 4 + h + 1], scalar2=None, op0=ALU.mult),
                        waits=[(s_v, prog.n(s_v))], inc=s_v)
                    prog.emit("vector", lambda e, ci=ci, h=h: e.tensor_scalar(
                        out=zsrc[:, ci * 512 + 256: ci * 512 + 512], in0=k_tm[:, ci * 256:(ci + 1) * 256],
                        scalar1=bcol[:, ci * 4 + h: ci * 4 + h + 1], scalar2=None, op0=ALU.mult),
                        waits=[(s_v, prog.n(s_v))], inc=s_v)
                scale_done = prog.n(s_v)
                for ci in range(NT):
                    for half in range(2):
                        for (srcb, dstb) in [(q_tm, q_fm), (k_tm, k_fm)]:
                            def tpb(e, ci=ci, half=half, srcb=srcb):
                                return e.transpose(ps_tp[:], srcb[:, ci * 256 + half * 128: ci * 256 + half * 128 + 128],
                                                   consts.ident[:])
                            prog.emit("tensor", tpb, waits=[(s_v, prog.n(s_v)), (s_v, scale_done)], inc=s_mm)
                            prog.emit("vector", lambda e, ci=ci, half=half, dstb=dstb: e.tensor_copy(
                                out=dstb[:, half * Lseq + ci * 128: half * Lseq + ci * 128 + 128], in_=ps_tp[:]),
                                waits=[(s_mm, prog.n(s_mm))], inc=s_v)
                        def tpkb(e, ci=ci, half=half):
                            return e.transpose(ps_tp[:], zsrc[:, ci * 512 + 256 + half * 128: ci * 512 + 256 + half * 128 + 128],
                                               consts.ident[:])
                        prog.emit("tensor", tpkb, waits=[(s_v, prog.n(s_v)), (s_v, scale_done)], inc=s_mm)
                        prog.emit("vector", lambda e, ci=ci, half=half: e.tensor_copy(
                            out=kb_fm[:, half * Lseq + ci * 128: half * Lseq + ci * 128 + 128], in_=ps_tp[:]),
                            waits=[(s_mm, prog.n(s_mm))], inc=s_v)
                fm_done = prog.n(s_v)
                prog.emit("vector", lambda e: e.memset(S_sb[:], 0.0),
                          waits=[(s_v, fm_done)], inc=s_v)
                prog.emit("vector", lambda e: e.memset(S_f32[:], 0.0),
                          waits=[(s_v, prog.n(s_v))], inc=s_v)
                lvl = getattr(prog, "l0_p3_lvl", 9)
                for ci in range(NT):
                    if lvl < 2:
                        break
                    cb = ci * 128
                    def amm2(e, cb=cb):
                        e.matmul(ps_A[:], kb_fm[:, cb: cb + 128], k_fm[:, cb: cb + 128],
                                 start=True, stop=False)
                        return e.matmul(ps_A[:], kb_fm[:, Lseq + cb: Lseq + cb + 128],
                                        k_fm[:, Lseq + cb: Lseq + cb + 128], start=False, stop=True)
                    prog.emit("tensor", amm2, waits=[(s_v, fm_done), (s_v, prog.n(s_v))], inc=s_mm)
                    prog.emit("vector", lambda e: e.tensor_tensor(out=pw[:, 0:128], in0=ps_A[:],
                                                                  in1=mslf[:], op=ALU.mult),
                              waits=[(s_mm, prog.n(s_mm))], inc=s_v)
                    a_done = prog.n(s_v)
                    def atmm(e, cb=cb):
                        e.matmul(ps_A[:], k_fm[:, cb: cb + 128], kb_fm[:, cb: cb + 128],
                                 start=True, stop=False)
                        return e.matmul(ps_A[:], k_fm[:, Lseq + cb: Lseq + cb + 128],
                                        kb_fm[:, Lseq + cb: Lseq + cb + 128], start=False, stop=True)
                    prog.emit("tensor", atmm, waits=[(s_v, a_done)], inc=s_mm)
                    prog.emit("vector", lambda e: e.tensor_tensor(out=pwT[:, 0:128], in0=ps_A[:],
                                                                  in1=msuf[:], op=ALU.mult),
                              waits=[(s_mm, prog.n(s_mm))], inc=s_v)
                    def ltm(e, cb=cb):
                        e.matmul(ps_u[:, 0:128], k_fm[:, cb: cb + 128], q_fm[:, cb: cb + 128],
                                 start=True, stop=False)
                        return e.matmul(ps_u[:, 0:128], k_fm[:, Lseq + cb: Lseq + cb + 128],
                                        q_fm[:, Lseq + cb: Lseq + cb + 128], start=False, stop=True)
                    prog.emit("tensor", ltm, waits=[(s_v, fm_done), (s_v, prog.n(s_v))], inc=s_mm)
                    prog.emit("vector", lambda e, ci=ci: e.tensor_tensor(
                        out=lT_all[:, ci * 128:(ci + 1) * 128], in0=ps_u[:, 0:128], in1=muif[:],
                        op=ALU.mult),
                        waits=[(s_mm, prog.n(s_mm))], inc=s_v)
                    if lvl < 3:
                        continue
                    for lv in range(1, 7):
                        po = (lv - 1) * 128
                        no = lv * 128
                        def sqm(e, po=po):
                            e.matmul(ps_sq[:], pwT[:, po:po + 128], pw[:, po:po + 128],
                                     start=True, stop=True)
                            return e.matmul(ps_z[:, 0:128], pw[:, po:po + 128], pwT[:, po:po + 128],
                                            start=True, stop=True)
                        prog.emit("tensor", sqm, waits=[(s_v, prog.n(s_v)), (s_s, prog.n(s_s))], inc=s_mm)
                        sq_done = prog.n(s_mm)
                        prog.emit("vector", lambda e, no=no: e.tensor_copy(out=pw[:, no:no + 128],
                                                                           in_=ps_sq[:]),
                                  waits=[(s_mm, sq_done)], inc=s_v)
                        prog.emit("scalar", lambda e, no=no: e.copy(out=pwT[:, no:no + 128],
                                                                    in_=ps_z[:, 0:128]),
                                  waits=[(s_mm, sq_done)], inc=s_s)
                    pows_v = prog.n(s_v)
                    pows_s = prog.n(s_s)
                    if lvl < 4:
                        continue
                    for si, lv in enumerate(range(6, -1, -1)):
                        po = lv * 128
                        zin = zsrc[:, ci * 512:(ci + 1) * 512] if si == 0 else zbuf[(si - 1) % 2][:]
                        zout = (zbuf[si % 2][:] if si < 6 else zsrc[:, ci * 512:(ci + 1) * 512])
                        def zap(e, po=po, zin=zin):
                            e.matmul(ps_z[:], pwT[:, po:po + 128], zin, start=True, stop=False)
                            return e.matmul(ps_z[:], consts.ident[:], zin, start=False, stop=True)
                        prog.emit("tensor", zap,
                                  waits=[(s_v, pows_v), (s_s, prog.n(s_s)), (s_v, prog.n(s_v))],
                                  inc=s_mm)
                        prog.emit("scalar", lambda e, zout=zout: e.copy(out=zout, in_=ps_z[:]),
                                  waits=[(s_mm, prog.n(s_mm))], inc=s_s)
                    z_done_s = prog.n(s_s)
                    if lvl < 5:
                        continue
                    for half in range(2):
                        def tpw(e, ci=ci, half=half):
                            return e.transpose(ps_tp[:], zsrc[:, ci * 512 + 256 + half * 128: ci * 512 + 256 + half * 128 + 128],
                                               consts.ident[:])
                        prog.emit("tensor", tpw, waits=[(s_s, z_done_s), (s_v, prog.n(s_v))], inc=s_mm)
                        prog.emit("vector", lambda e, half=half, cb=cb: e.tensor_copy(
                            out=kb_fm[:, half * Lseq + cb: half * Lseq + cb + 128], in_=ps_tp[:]),
                            waits=[(s_mm, prog.n(s_mm))], inc=s_v)
                A_done_v = prog.n(s_v)
                A_done_s = prog.n(s_s)
                for ci in range(NT):
                    if lvl < 5:
                        break
                    cb = ci * 128
                    def uam(e, cb=cb):
                        r = None
                        for kc in range(2):
                            r = e.matmul(ps_u[:], kb_fm[:, kc * Lseq + cb: kc * Lseq + cb + 128],
                                         S_sb[:, kc * 256:(kc + 1) * 256],
                                         start=(kc == 0), stop=(kc == 1))
                        return r
                    prog.emit("tensor", uam, waits=[(s_v, A_done_v), (s_s, A_done_s),
                                                    (s_v, prog.n(s_v))], inc=s_mm)
                    prog.emit("vector", lambda e, ci=ci: e.tensor_sub(
                        out=uadj[:], in0=zsrc[:, ci * 512: ci * 512 + 256], in1=ps_u[:]),
                        waits=[(s_mm, prog.n(s_mm))], inc=s_v)
                    ua_done = prog.n(s_v)
                    def omm(e, cb=cb, ci=ci):
                        e.matmul(ps_o[:], q_fm[:, cb: cb + 128], S_sb[:, 0:256], start=True, stop=False)
                        e.matmul(ps_o[:], q_fm[:, Lseq + cb: Lseq + cb + 128], S_sb[:, 256:512],
                                 start=False, stop=False)
                        return e.matmul(ps_o[:], lT_all[:, ci * 128:(ci + 1) * 128], uadj[:],
                                        start=False, stop=True)
                    prog.emit("tensor", omm, waits=[(s_v, ua_done), (s_s, prog.n(s_s))], inc=s_mm)
                    prog.emit("scalar", lambda e, ci=ci: e.copy(out=dh_tm[:, ci * 256:(ci + 1) * 256],
                                                                in_=ps_o[:]),
                              waits=[(s_mm, prog.n(s_mm)), (s_scr, prog.n(s_scr))], inc=s_s)
                    def sup(e, ci=ci):
                        e.matmul(ps_S0[:], k_tm[:, ci * 256: ci * 256 + 128], uadj[:],
                                 start=True, stop=True)
                        return e.matmul(ps_S1[:], k_tm[:, ci * 256 + 128: ci * 256 + 256], uadj[:],
                                        start=True, stop=True)
                    prog.emit("tensor", sup, waits=[(s_v, ua_done), (s_v, prog.n(s_v))], inc=s_mm)
                    sup_done = prog.n(s_mm)
                    if ci < NT - 1:
                        prog.emit("vector", lambda e: e.tensor_tensor(
                            out=S_f32[:, 0:256], in0=S_f32[:, 0:256], in1=ps_S0[:], op=ALU.add),
                            waits=[(s_mm, sup_done)], inc=s_v)
                        prog.emit("vector", lambda e: e.tensor_tensor(
                            out=S_f32[:, 256:512], in0=S_f32[:, 256:512], in1=ps_S1[:], op=ALU.add),
                            waits=[(s_mm, sup_done)], inc=s_v)
                        prog.emit("vector", lambda e: e.tensor_copy(out=S_sb[:], in_=S_f32[:]),
                                  waits=[(s_v, prog.n(s_v))], inc=s_v)
                dh_done_s = prog.n(s_s)
                if lvl < 5:
                    continue
                prog.emit("sync", lambda e, sq=sq, h=h: e.dma_start(
                    out=dtm_scr[sq][:, :].rearrange("p (t c) -> p t c", c=1024)[:, :, h * 256:(h + 1) * 256],
                    in_=dh_tm[:].rearrange("p (t c) -> p t c", c=256)),
                    waits=[(s_s, dh_done_s)], inc=s_scr, inc_by=16)
                for ci in range(NT):
                    for half in range(2):
                        def tpd(e, ci=ci, half=half):
                            return e.transpose(ps_tp[:], dh_tm[:, ci * 256 + half * 128: ci * 256 + half * 128 + 128],
                                               consts.ident[:])
                        prog.emit("tensor", tpd, waits=[(s_s, dh_done_s), (s_v, prog.n(s_v))], inc=s_mm)
                        prog.emit("vector", lambda e, ci=ci, half=half: e.tensor_copy(
                            out=dh_fm[:, half * Lseq + ci * 128: half * Lseq + ci * 128 + 128], in_=ps_tp[:]),
                            waits=[(s_mm, prog.n(s_mm))], inc=s_v)
                prog.emit("sync", lambda e, sq=sq, ch0=ch0: e.dma_start(
                    out=dfm_scr[sq][:, ch0 * Lseq:(ch0 + 2) * Lseq].rearrange("p (c w) -> p c w", c=2),
                    in_=dh_fm[:].rearrange("p (c w) -> p c w", c=2)),
                    waits=[(s_v, prog.n(s_v))], inc=s_scr, inc_by=16)
            prog.fence(ALLSEMS)

    
    def _pass4(sq):
        sb0 = sq * Lseq
        with prog.scope() as es:
            wf1b = [prog.sb_s(es, f"a0_wf1_{i}", [128, NC_D * 2048], BF16) for i in range(4)]
            wf2b = prog.sb_s(es, "a0_wf2b", [128, 16 * 16], BF16)
            bf2b = prog.sb_s(es, "a0_bf2b", [1, 16], BF16)
            bf1c = prog.sb_s(es, "a0_bf1c", [128, 16], F32)
            g4 = [prog.sb_s(es, f"a0_g4_{i}", [128, NC_D * 256], BF16) for i in range(4)]
            h_fm = prog.sb_s(es, "a0_h_fm", [128, 16 * 256], BF16)
            brtm = prog.sb_s(es, "a0_brtm", [128, 4 * 1024], BF16)
            o_tm = prog.sb_s(es, "a0_o_tm", [128, 1024], F32)
            l0sq = prog.sb_s(es, "a0_l0sq", [128, 256], F32)
            prog.l0_bfm = prog.sb_s(es, "a0_bfm", [128, NC_D * 128], BF16)
            prog.l0_sq = l0sq
            ps_tp = prog.ps_s(es, "a0_ps_tp4", [128, 128], BF16)
            ps_h = [prog.ps_s(es, f"a0_ps_h4_{i}", [128, 256], F32) for i in range(2)]
            ps_b = prog.ps_s(es, "a0_ps_b4", [128, 16], F32)
            for i in range(4):
                prog.emit("sync", lambda e, i=i: e.dma_start(
                    out=wf1b[i][:].rearrange("p (c n) -> p c n", c=NC_D),
                    in_=WB["Wf1"][i * 1024:(i + 1) * 1024, :].rearrange("(c p) n -> p c n", p=128)),
                    waits=[(s_cvt, cvt_done)], inc=s_lw, inc_by=16)
            prog.emit("sync", lambda e: e.dma_start(
                out=wf2b[:].rearrange("p (c n) -> p c n", c=16),
                in_=WB["Wf2"].rearrange("(c p) n -> p c n", p=128)),
                waits=[(s_cvt, cvt_done)], inc=s_lw, inc_by=16)
            prog.emit("sync", lambda e: e.dma_start(out=bf2b[:], in_=WB["bf2row"]), inc=s_lw, inc_by=16)
            prog.emit("sync", lambda e: e.dma_start(out=bf1c[:], in_=W["bf1c"]), inc=s_lw, inc_by=16)
            wf_ready = prog.n(s_lw)
            prog.barrier("tensor", [(s_lw, wf_ready)])
            for t4 in range(NT4):
                tb = t4 * 256
                for sp in range(2):
                    emit_x_ptile_prep(prog, x_dram, sb0 + tb + sp * 128,
                                      x_tm, x_tmb, g4[0], sp * 128, 256,
                                      ps_tp, s_lx, s_v, s_mm)
                for gi, srcsc in [(1, firs_scr), (2, firl_scr), (3, dfm_scr)]:
                    prog.emit("sync", lambda e, sq=sq, tb=tb, gi=gi, srcsc=srcsc: e.dma_start(
                        out=g4[gi][:].rearrange("p (c n) -> p c n", c=NC_D),
                        in_=srcsc[sq][:, :].rearrange("p (c w) -> p c w", c=NC_D)[:, :, tb:tb + 256]),
                        waits=[(s_mm, prog.n(s_mm)), (s_v, prog.n(s_v))], inc=s_lsc, inc_by=16)
                g_ld = prog.n(s_lsc)
                g_v = prog.n(s_v)
                prog.barrier("tensor", [(s_lsc, g_ld), (s_v, g_v)])
                for oc in range(16):
                    pb = oc % 2
                    def f1mm(e, oc=oc, pb=pb):
                        r = None
                        for gi in range(4):
                            for kc in range(NC_D):
                                r = e.matmul(ps_h[pb][:],
                                             wf1b[gi][:, kc * 2048 + oc * 128: kc * 2048 + oc * 128 + 128],
                                             g4[gi][:, kc * 256:(kc + 1) * 256],
                                             start=(gi == 0 and kc == 0),
                                             stop=(gi == 3 and kc == NC_D - 1))
                        return r
                    prog.emit("tensor", f1mm, waits=[(s_s, prog.n(s_s))], inc=s_mm)
                    prog.emit("scalar", lambda e, oc=oc, pb=pb: e.activation(
                        out=h_fm[:, oc * 256:(oc + 1) * 256], in_=ps_h[pb][:], func=AF.Gelu,
                        bias=bf1c[:, oc:oc + 1]),
                        waits=[(s_mm, prog.n(s_mm))], inc=s_s)
                h_done = prog.n(s_s)
                for sp in range(2):
                    pti = t4 * 2 + sp
                    def f2mm(e, sp=sp):
                        for kc in range(16):
                            e.matmul(ps_b[:], h_fm[:, kc * 256 + sp * 128: kc * 256 + sp * 128 + 128],
                                     wf2b[:, kc * 16:(kc + 1) * 16], start=(kc == 0), stop=False)
                        return e.matmul(ps_b[:], consts.ones1[:], bf2b[:], start=False, stop=True)
                    prog.emit("tensor", f2mm, waits=[(s_s, h_done), (s_v, prog.n(s_v)),
                                                     (s_s, prog.n(s_s))], inc=s_mm)
                    prog.emit("vector", lambda e, pti=pti: e.tensor_copy(
                        out=logit_sb[:, pti * 16:(pti + 1) * 16], in_=ps_b[:]),
                        waits=[(s_mm, prog.n(s_mm))], inc=s_v)
            for pti in range(NT):
                emit_l0_gate_mix(prog, C, sq, pti, logit_sb, rtau, gcol, brtm, o_tm,
                                 nwb, firs_scr, firl_scr, vs_scr, dtm_scr,
                                 onorm_scr, ps_tp, ps_b,
                                 s_lsc, s_v, s_s, s_mm, s_scr, consts)
            prog.fence(ALLSEMS)

    
    def _pass5(sq):
        sb0 = sq * Lseq
        with prog.scope() as es:
            wob = prog.sb_s(es, "a0_wob", [128, NC_D * 1024], BF16)
            o_fm = prog.sb_s(es, "a0_o_fm", [128, NC_D * 512], BF16)
            og_tm = prog.sb_s(es, "a0_og_tm", [128, 4 * 1024], BF16)
            acc = prog.sb_s(es, "a0_acc", [128, 4 * D], F32)
            lnsq = prog.sb_s(es, "a0_lnsq", [128, 2 * D], F32)
            lnst = prog.sb_s(es, "a0_lnst", [128, 8], F32)
            ps_tp = prog.ps_s(es, "a0_ps_tp5", [128, 128], BF16)
            ps_h = [prog.ps_s(es, f"a0_ps_h5_{i}", [128, 512], F32) for i in range(2)]
            prog.emit("sync", lambda e: e.dma_start(
                out=wob[:].rearrange("p (c n) -> p c n", c=NC_D),
                in_=WB["Wo"].rearrange("(c p) n -> p c n", p=128)),
                waits=[(s_cvt, cvt_done)], inc=s_lw, inc_by=16)
            wo_ready = prog.n(s_lw)
            prog.barrier("tensor", [(s_lw, wo_ready)])
            for tt in range(NTT):
                prog.emit("sync", lambda e, sq=sq, tt=tt: e.dma_start(
                    out=og_tm[:].rearrange("p (t d) -> p t d", t=4),
                    in_=onorm_scr[sq][:, tt * 4 * 1024:(tt + 1) * 4 * 1024].rearrange("p (t d) -> p t d", t=4)),
                    waits=[(s_v, prog.n(s_v)), (s_mm, prog.n(s_mm))], inc=s_lsc, inc_by=16)
                on_ld = prog.n(s_lsc)
                prog.barrier("tensor", [(s_lsc, on_ld)])
                for sp in range(4):
                    for c in range(NC_D):
                        def tpo(e, sp=sp, c=c):
                            return e.transpose(ps_tp[:], og_tm[:, sp * 1024 + c * 128: sp * 1024 + c * 128 + 128],
                                               consts.ident[:])
                        prog.emit("tensor", tpo, waits=[(s_v, prog.n(s_v))], inc=s_mm)
                        prog.emit("vector", lambda e, sp=sp, c=c: e.tensor_copy(
                            out=o_fm[:, c * 512 + sp * 128: c * 512 + sp * 128 + 128], in_=ps_tp[:]),
                            waits=[(s_mm, prog.n(s_mm))], inc=s_v)
                ofm_done = prog.n(s_v)
                for sp in range(4):
                    for dn in range(2):
                        pb = (sp * 2 + dn) % 2
                        def wom(e, sp=sp, dn=dn, pb=pb):
                            r = None
                            for kc in range(NC_D):
                                r = e.matmul(ps_h[pb][:],
                                             o_fm[:, kc * 512 + sp * 128: kc * 512 + sp * 128 + 128],
                                             wob[:, kc * 1024 + dn * 512: kc * 1024 + dn * 512 + 512],
                                             start=(kc == 0), stop=(kc == NC_D - 1))
                            return r
                        prog.emit("tensor", wom, waits=[(s_v, ofm_done), (s_v, prog.n(s_v))], inc=s_mm)
                        prog.emit("vector", lambda e, sp=sp, dn=dn, pb=pb: e.tensor_copy(
                            out=acc[:, sp * D + dn * 512: sp * D + dn * 512 + 512], in_=ps_h[pb][:]),
                            waits=[(s_mm, prog.n(s_mm))], inc=s_v)
                for sp in range(4):
                    t0 = sb0 + tt * 512 + sp * 128
                    prog.emit("sync", lambda e, t0=t0: e.dma_start(
                        out=x_tm[:, 0:D], in_=x_dram[t0:t0 + 128, :]),
                        waits=[(s_v, prog.n(s_v))], inc=s_lx, inc_by=16)
                    ld = prog.n(s_lx)
                    emit_ln_epilogue(prog, x_tm[:, 0:D], acc[:, sp * D:(sp + 1) * D],
                                     y_dram[t0:t0 + 128, :], {"sq": lnsq, "st": lnst},
                                     s_v, s_s, s_st, waits_v=[(s_lx, ld), (s_v, prog.n(s_v)),
                                                              (s_s, prog.n(s_s)),
                                                              (s_st, prog.n(s_st))])
            prog.fence(ALLSEMS)

    passes = getattr(prog, "l0_passes", (1, 2, 3, 4, 5))
    for sq in range(NSEQ):
        if 1 in passes:
            _pass1(sq)
        if 2 in passes:
            _pass2(sq)
        if 3 in passes:
            _pass3(sq)
        if 4 in passes:
            _pass4(sq)
        if 5 in passes:
            _pass5(sq)
    prog.barrier("sync", [(s_st, prog.n(s_st))])
    prog.block_end(ALLSEMS)


def emit_l0_gate_mix(prog, C, sq, pti, logit_sb, rtau, gcol, brtm, o_tm, nwb,
                     firs_scr, firl_scr, vs_scr, dtm_scr, onorm_scr, ps_tp, ps_b,
                     s_lsc, s_v, s_s, s_mm, s_scr, consts):
    Lseq = C["L"]
    H = 4
    lg = logit_sb[:, pti * 16:(pti + 1) * 16]
    lg4 = lambda j: logit_sb[:].rearrange("p (t h j) -> p t h j", t=Lseq // 128, j=4)[:, pti, :, j]
    prog.emit("scalar", lambda e: e.activation(out=gcol[:, 12:16], in_=lg4(3), func=AF.Sigmoid),
              waits=[(s_v, prog.n(s_v)), (s_s, prog.n(s_s))], inc=s_s)
    pr_done = prog.n(s_s)
    prog.emit("vector", lambda e: e.tensor_scalar(
        out=gcol[:, 8:12], in0=gcol[:, 12:16], scalar1=0.09, scalar2=0.9,
        op0=ALU.mult, op1=ALU.add),
        waits=[(s_s, pr_done), (s_v, prog.n(s_v))], inc=s_v)
    prog.emit("vector", lambda e: e.tensor_tensor(
        out=gcol[:, 12:16], in0=gcol[:, 8:12], in1=gcol[:, 12:16], op=ALU.mult),
        waits=[(s_v, prog.n(s_v))], inc=s_v)
    sc = prog.l0_sc
    prog.emit("vector", lambda e: e.tensor_tensor(out=sc[:, 0:16], in0=lg, in1=rtau[:], op=ALU.mult),
              waits=[(s_v, prog.n(s_v))], inc=s_v)
    r3 = lambda t, j: t[:].rearrange("p (h j) -> p h j", j=4)[:, :, j]
    prog.emit("vector", lambda e: e.tensor_tensor(out=sc[:, 16:20], in0=r3(sc[:, 0:16], 0),
                                                  in1=r3(sc[:, 0:16], 1), op=ALU.max),
              waits=[(s_v, prog.n(s_v))], inc=s_v)
    prog.emit("vector", lambda e: e.tensor_tensor(out=sc[:, 16:20], in0=sc[:, 16:20],
                                                  in1=r3(sc[:, 0:16], 2), op=ALU.max),
              waits=[(s_v, prog.n(s_v))], inc=s_v)
    prog.emit("vector", lambda e: e.tensor_copy(
        out=sc[:, 20:36].rearrange("p (h j) -> p h j", j=4)[:, :, 0:3],
        in_=sc[:, 16:20].rearrange("p (h j) -> p h j", j=1).broadcast_to((128, 4, 3))),
        waits=[(s_v, prog.n(s_v))], inc=s_v)
    prog.emit("vector", lambda e: e.memset(sc[:, 20:36].rearrange("p (h j) -> p h j", j=4)[:, :, 3:4], 0.0),
              waits=[(s_v, prog.n(s_v))], inc=s_v)
    prog.emit("vector", lambda e: e.tensor_sub(out=sc[:, 0:16], in0=sc[:, 0:16], in1=sc[:, 20:36]),
              waits=[(s_v, prog.n(s_v))], inc=s_v)
    prog.emit("scalar", lambda e: e.activation(out=sc[:, 0:16], in_=sc[:, 0:16], func=AF.Exp),
              waits=[(s_v, prog.n(s_v)), (s_s, prog.n(s_s))], inc=s_s)
    e_done = prog.n(s_s)
    prog.emit("vector", lambda e: e.tensor_tensor(out=sc[:, 16:20], in0=r3(sc[:, 0:16], 0),
                                                  in1=r3(sc[:, 0:16], 1), op=ALU.add),
              waits=[(s_s, e_done), (s_v, prog.n(s_v))], inc=s_v)
    prog.emit("vector", lambda e: e.tensor_tensor(out=sc[:, 16:20], in0=sc[:, 16:20],
                                                  in1=r3(sc[:, 0:16], 2), op=ALU.add),
              waits=[(s_v, prog.n(s_v))], inc=s_v)
    prog.emit("vector", lambda e: e.reciprocal(out=sc[:, 16:20], in_=sc[:, 16:20]),
              waits=[(s_v, prog.n(s_v))], inc=s_v)
    prog.emit("vector", lambda e: e.tensor_scalar(
        out=sc[:, 36:40], in0=gcol[:, 12:16], scalar1=-1.0, scalar2=1.0, op0=ALU.mult, op1=ALU.add),
        waits=[(s_v, prog.n(s_v))], inc=s_v)
    prog.emit("vector", lambda e: e.tensor_tensor(out=sc[:, 16:20], in0=sc[:, 16:20],
                                                  in1=sc[:, 36:40], op=ALU.mult),
              waits=[(s_v, prog.n(s_v))], inc=s_v)
    for j in range(3):
        prog.emit("vector", lambda e, j=j: e.tensor_tensor(
            out=gcol[:, j * 4:(j + 1) * 4], in0=r3(sc[:, 0:16], j), in1=sc[:, 16:20], op=ALU.mult),
            waits=[(s_v, prog.n(s_v))], inc=s_v)
    g_done = prog.n(s_v)
    for bi, srcsc in enumerate([firs_scr, firl_scr, vs_scr]):
        prog.emit("sync", lambda e, sq=sq, srcsc=srcsc, pti=pti: e.dma_start(
            out=prog.l0_bfm[:].rearrange("p (c n) -> p c n", c=NC_D),
            in_=srcsc[sq][:, :].rearrange("p (c w) -> p c w", c=NC_D)[:, :, pti * 128:(pti + 1) * 128]),
            waits=[(s_v, prog.n(s_v)), (s_mm, prog.n(s_mm))], inc=s_lsc, inc_by=16)
        bld = prog.n(s_lsc)
        prog.barrier("tensor", [(s_lsc, bld)])
        for c in range(NC_D):
            def tpb(e, c=c):
                return e.transpose(ps_tp[:], prog.l0_bfm[:, c * 128:(c + 1) * 128], consts.ident[:])
            prog.emit("tensor", tpb, waits=[(s_v, prog.n(s_v))], inc=s_mm)
            prog.emit("vector", lambda e, bi=bi, c=c: e.tensor_copy(
                out=brtm[:, bi * 1024 + c * 128: bi * 1024 + c * 128 + 128], in_=ps_tp[:]),
                waits=[(s_mm, prog.n(s_mm)), (s_scr, prog.n(s_scr))], inc=s_v)
    prog.emit("sync", lambda e, sq=sq, pti=pti: e.dma_start(
        out=brtm[:, 3 * 1024: 4 * 1024], in_=dtm_scr[sq][:, pti * 1024:(pti + 1) * 1024]),
        waits=[(s_v, prog.n(s_v)), (s_scr, prog.n(s_scr))], inc=s_lsc, inc_by=16)
    br_ld = prog.n(s_lsc)
    br_done = prog.n(s_v)
    for h in range(H):
        ho = h * 256
        prog.emit("vector", lambda e, h=h, ho=ho: e.tensor_scalar(
            out=o_tm[:, ho:ho + 256], in0=brtm[:, 0 * 1024 + ho: 0 * 1024 + ho + 256],
            scalar1=gcol[:, 0 + h:1 + h], scalar2=None, op0=ALU.mult),
            waits=[(s_v, br_done), (s_lsc, br_ld), (s_v, prog.n(s_v))], inc=s_v)
        for bi, coff in [(1, 4), (3, 8), (2, 12)]:
            prog.emit("vector", lambda e, h=h, ho=ho, bi=bi, coff=coff: e.scalar_tensor_tensor(
                out=o_tm[:, ho:ho + 256], in0=brtm[:, bi * 1024 + ho: bi * 1024 + ho + 256],
                scalar=gcol[:, coff + h: coff + h + 1], in1=o_tm[:, ho:ho + 256],
                op0=ALU.mult, op1=ALU.add),
                waits=[(s_v, prog.n(s_v))], inc=s_v)
    mix_done = prog.n(s_v)
    for h in range(H):
        ho = h * 256
        prog.emit("scalar", lambda e, ho=ho: e.activation(out=prog.l0_sq[:, 0:256],
                                                          in_=o_tm[:, ho:ho + 256], func=AF.Square,
                                                          accum_out=prog.l0_sc[:, 20:21]),
                  waits=[(s_v, mix_done), (s_s, prog.n(s_s)), (s_v, prog.n(s_v))], inc=s_s)
        prog.emit("vector", lambda e: e.tensor_scalar(
            out=prog.l0_sc[:, 21:22], in0=prog.l0_sc[:, 20:21], scalar1=1.0 / 256, scalar2=1e-5,
            op0=ALU.mult, op1=ALU.add),
            waits=[(s_s, prog.n(s_s)), (s_v, prog.n(s_v))], inc=s_v)
        prog.emit("vector", lambda e: e.reciprocal(out=prog.l0_sc[:, 22:23], in_=prog.l0_sc[:, 21:22]),
                  waits=[(s_v, prog.n(s_v))], inc=s_v)
        prog.emit("scalar", lambda e: e.activation(out=prog.l0_sc[:, 23:24], in_=prog.l0_sc[:, 22:23],
                                                   func=AF.Sqrt),
                  waits=[(s_v, prog.n(s_v)), (s_s, prog.n(s_s))], inc=s_s)
        prog.emit("vector", lambda e, ho=ho: e.tensor_scalar(
            out=o_tm[:, ho:ho + 256], in0=o_tm[:, ho:ho + 256],
            scalar1=prog.l0_sc[:, 23:24], scalar2=None, op0=ALU.mult),
            waits=[(s_s, prog.n(s_s)), (s_v, prog.n(s_v))], inc=s_v)
        prog.emit("vector", lambda e, ho=ho: e.tensor_tensor(
            out=o_tm[:, ho:ho + 256], in0=o_tm[:, ho:ho + 256], in1=nwb[:], op=ALU.mult),
            waits=[(s_v, prog.n(s_v))], inc=s_v)
    on_done = prog.n(s_v)
    prog.emit("vector", lambda e: e.tensor_copy(out=brtm[:, 0:1024], in_=o_tm[:]),
              waits=[(s_v, on_done)], inc=s_v)
    prog.emit("sync", lambda e, sq=sq, pti=pti: e.dma_start(
        out=onorm_scr[sq][:, pti * 1024:(pti + 1) * 1024], in_=brtm[:, 0:1024]),
        waits=[(s_v, prog.n(s_v))], inc=s_scr, inc_by=16)




L3ATT_SPEC = {"wkvd": (1024, 128), "wkvu": (128, 2048), "wq": (1024, 1024),
              "wout": (1024, 1024)}


def prep_l3att_weights(inputs):
    cc = np.ascontiguousarray
    return {"wkvd": cc(inputs["L3_attn_w_kv_down"]), "wkvu": cc(inputs["L3_attn_w_kv_up"]),
            "wq": cc(inputs["L3_attn_w_q"]), "wout": cc(inputs["L3_attn_w_out"])}


def host_consts_att():
    hs = np.zeros((128, 8, 16), np.float32)
    for c in range(8):
        for j in range(16):
            if j == 2 * c:
                hs[0:64, c, j] = 1.0
            elif j == 2 * c + 1:
                hs[64:128, c, j] = 1.0
    return {"c_headsel": hs.reshape(128, 128), "c_headsel2": hs.reshape(128, 128)}


def emit_l3att(prog, C, x_dram, y_dram, W):
    T, Lseq = C["T"], C["L"]
    NSEQ = T // Lseq
    NT = Lseq // 128
    NTT = Lseq // 512
    NH = 16
    SCALE = 1.0 / 8.0
    consts = prog.consts
    nc = prog.nc

    s_lx = prog.sem("a3_lx"); s_lw = prog.sem("a3_lw")
    s_v = prog.sem("a3_v"); s_s = prog.sem("a3_s"); s_g = prog.sem("a3_g")
    s_mm = prog.sem("a3_mm"); s_st = prog.sem("a3_st")
    s_cvt = prog.sem("a3_cvt")
    ALLSEMS = [s_lx, s_lw, s_v, s_s, s_g, s_mm, s_st, s_cvt]
    prog.block_begin()

    WB, cvt_done = emit_weight_convert(prog, "l3a", W, ["wkvd", "wkvu", "wq", "wout"], s_cvt)

    hselb = prog.sb("a3_hselb", [128, 128], BF16)
    hself = prog.sb("a3_hself", [128, 128], F32)
    hs_in = nc.dram_tensor("c_headsel", [128, 128], F32, kind="ExternalInput").ap()
    s_lh = prog.sem("a3_lh")
    prog.emit("sync", lambda e: e.dma_start(out=hself[:], in_=hs_in), inc=s_lh, inc_by=16)
    prog.emit("vector", lambda e: e.tensor_copy(out=hselb[:], in_=hself[:]),
              waits=[(s_lh, prog.n(s_lh)), (s_v, prog.n(s_v))], inc=s_v)

    x_tm = prog.sb("a3_x_tm", [128, D], F32)
    x_tmb = prog.sb("a3_x_tmb", [128, D], BF16)
    with prog.scope() as es:
        x_fm = prog.sb_s(es, "a3_x_fm", [128, NC_D * 512], BF16)
        k_fm = prog.sb_s(es, "a3_k_fm", [128, NC_D * Lseq], BF16)
        v_tm = prog.sb_s(es, "a3_v_tm", [128, NT * 1024], BF16)
        q_fm = prog.sb_s(es, "a3_q_fm", [128, NC_D * Lseq], BF16)
        latb = prog.sb_s(es, "a3_latb", [128, 512], BF16)
        sq2 = prog.sb_s(es, "a3_sq2", [128, 512], BF16)
        wdb = prog.sb_s(es, "a3_wdb", [128, NC_D * 128], BF16)
        wub = prog.sb_s(es, "a3_wub", [128, 2048], BF16)
        wqb = prog.sb_s(es, "a3_wqb", [128, NC_D * 1024], BF16)
        wob = prog.sb_s(es, "a3_wob", [128, NC_D * 1024], BF16)
        nrm = prog.sb_s(es, "a3_nrm", [128, 64], F32)
        mrow = prog.sb_s(es, "a3_mrow", [1, 16], F32)
        mcols = prog.sb_s(es, "a3_mcols", [128, NTT * 16], F32)
        pt_sb = prog.sb_s(es, "a3_pt", [128, 1024], BF16)
        recip = prog.sb_s(es, "a3_recip", [1, 512], F32)
        rb = prog.sb_s(es, "a3_rb", [128, 512], F32)
        attn_fm = prog.sb_s(es, "a3_attn", [128, NC_D * 512], BF16)
        acc = prog.sb_s(es, "a3_acc", [128, 4 * D], F32)
        lnsq = prog.sb_s(es, "a3_lnsq", [128, 2 * D], F32)
        lnst = prog.sb_s(es, "a3_lnst", [128, 8], F32)
        ps_tp = prog.ps_s(es, "a3_ps_tp", [128, 128], BF16)
        ps_p = [prog.ps_s(es, f"a3_ps_p{i}", [128, 512], F32) for i in range(2)]
        ps_n = prog.ps_s(es, "a3_ps_n", [128, 512], F32)
        ps_st = prog.ps_s(es, "a3_ps_st", [128, 1024], F32)
        ps_pv = prog.ps_s(es, "a3_ps_pv", [128, 512], F32)
        ps_sm = prog.ps_s(es, "a3_ps_sm", [128, 512], F32)

        prog.emit("sync", lambda e: e.dma_start(
            out=wdb[:].rearrange("p (c n) -> p c n", c=NC_D),
            in_=WB["wkvd"].rearrange("(c p) n -> p c n", p=128)),
            waits=[(s_cvt, cvt_done)], inc=s_lw, inc_by=16)
        prog.emit("sync", lambda e: e.dma_start(out=wub[:], in_=WB["wkvu"]), inc=s_lw, inc_by=16)
        prog.emit("sync", lambda e: e.dma_start(
            out=wqb[:].rearrange("p (c n) -> p c n", c=NC_D),
            in_=WB["wq"].rearrange("(c p) n -> p c n", p=128)), inc=s_lw, inc_by=16)
        prog.emit("sync", lambda e: e.dma_start(
            out=wob[:].rearrange("p (c n) -> p c n", c=NC_D),
            in_=WB["wout"].rearrange("(c p) n -> p c n", p=128)), inc=s_lw, inc_by=16)
        w_ready = prog.n(s_lw)
        prog.barrier("tensor", [(s_lw, w_ready)])

        def _seq(sq):
            sb0 = sq * Lseq
            for tt in range(NTT):
                for sp in range(4):
                    emit_x_ptile_prep(prog, x_dram, sb0 + tt * 512 + sp * 128,
                                      x_tm, x_tmb, x_fm, sp * 128, 512,
                                      ps_tp, s_lx, s_v, s_mm)
                xfm_ready = prog.n(s_v)
                def latmm(e):
                    r = None
                    for kc in range(NC_D):
                        r = e.matmul(ps_p[0][:], wdb[:, kc * 128:(kc + 1) * 128],
                                     x_fm[:, kc * 512:(kc + 1) * 512],
                                     start=(kc == 0), stop=(kc == NC_D - 1))
                    return r
                prog.emit("tensor", latmm, waits=[(s_v, xfm_ready), (s_s, prog.n(s_s))], inc=s_mm)
                prog.emit("scalar", lambda e: e.copy(out=latb[:], in_=ps_p[0][:]),
                          waits=[(s_mm, prog.n(s_mm))], inc=s_s)
                lat_done = prog.n(s_s)
                for oc in range(16):
                    pb = oc % 2
                    def kvmm(e, oc=oc, pb=pb):
                        return e.matmul(ps_p[pb][:], wub[:, oc * 128:(oc + 1) * 128],
                                        latb[:], start=True, stop=True)
                    prog.emit("tensor", kvmm, waits=[(s_s, lat_done), (s_s, prog.n(s_s)),
                                                     (s_v, prog.n(s_v))], inc=s_mm)
                    mmn = prog.n(s_mm)
                    if oc < 8:
                        prog.emit("scalar", lambda e, oc=oc, tt=tt, pb=pb: e.copy(
                            out=k_fm[:, oc * Lseq + tt * 512: oc * Lseq + tt * 512 + 512],
                            in_=ps_p[pb][:]),
                            waits=[(s_mm, mmn)], inc=s_s)
                    else:
                        prog.emit("scalar", lambda e, pb=pb: e.copy(out=sq2[:], in_=ps_p[pb][:]),
                                  waits=[(s_mm, mmn), (s_v, prog.n(s_v))], inc=s_s)
                        vc_done = prog.n(s_s)
                        for spp in range(4):
                            def tpv(e, spp=spp):
                                return e.transpose(ps_tp[:], sq2[:, spp * 128:(spp + 1) * 128],
                                                   consts.ident[:])
                            prog.emit("tensor", tpv, waits=[(s_s, vc_done), (s_v, prog.n(s_v))], inc=s_mm)
                            vo = oc - 8
                            prog.emit("vector", lambda e, tt=tt, spp=spp, vo=vo: e.tensor_copy(
                                out=v_tm[:, (tt * 4 + spp) * 1024 + vo * 128: (tt * 4 + spp) * 1024 + vo * 128 + 128],
                                in_=ps_tp[:]),
                                waits=[(s_mm, prog.n(s_mm))], inc=s_v)
                for oc in range(NC_D):
                    pb = oc % 2
                    def qmm(e, oc=oc, pb=pb):
                        r = None
                        for kc in range(NC_D):
                            r = e.matmul(ps_p[pb][:], wqb[:, kc * 1024 + oc * 128: kc * 1024 + oc * 128 + 128],
                                         x_fm[:, kc * 512:(kc + 1) * 512],
                                         start=(kc == 0), stop=(kc == NC_D - 1))
                        return r
                    prog.emit("tensor", qmm, waits=[(s_v, xfm_ready), (s_s, prog.n(s_s))], inc=s_mm)
                    prog.emit("scalar", lambda e, oc=oc, tt=tt, pb=pb: e.copy(
                        out=q_fm[:, oc * Lseq + tt * 512: oc * Lseq + tt * 512 + 512],
                        in_=ps_p[pb][:]),
                        waits=[(s_mm, prog.n(s_mm))], inc=s_s)
                proj_done = prog.n(s_s)
                for which, fmb, col0 in [(0, q_fm, 0), (1, k_fm, 32)]:
                    def n2mm(e, which=which, fmb=fmb, tt=tt):
                        r = None
                        for c in range(NC_D):
                            pass
                        return r
                    for c in range(NC_D):
                        prog.emit("scalar", lambda e, c=c, fmb=fmb, tt=tt: e.activation(
                            out=sq2[:], in_=fmb[:, c * Lseq + tt * 512: c * Lseq + tt * 512 + 512],
                            func=AF.Square),
                            waits=[(s_s, proj_done), (s_s, prog.n(s_s)), (s_mm, prog.n(s_mm))], inc=s_s)
                        sqd = prog.n(s_s)
                        def nmm(e, c=c):
                            return e.matmul(ps_n[:16, :], hselb[:, c * 16:(c + 1) * 16], sq2[:],
                                            start=(c == 0), stop=(c == NC_D - 1))
                        prog.emit("tensor", nmm, waits=[(s_s, sqd), (s_v, prog.n(s_v))], inc=s_mm)
                    nd = prog.n(s_mm)
                    prog.emit("vector", lambda e, tt=tt, col0=col0: e.reduce_max(
                        out=nrm[0:16, col0 + tt: col0 + tt + 1], in_=ps_n[0:16, :], axis=AX.X),
                        waits=[(s_mm, nd), (s_v, prog.n(s_v))], inc=s_v)
            n_done = prog.n(s_v)
            prog.emit("vector", lambda e: e.reduce_max(out=nrm[0:16, 36:37], in_=nrm[0:16, 32:32 + NTT],
                                                       axis=AX.X),
                      waits=[(s_v, n_done)], inc=s_v)
            prog.emit("scalar", lambda e: e.activation(out=nrm[0:16, 37:38], in_=nrm[0:16, 36:37],
                                                       func=AF.Sqrt),
                      waits=[(s_v, prog.n(s_v)), (s_s, prog.n(s_s))], inc=s_s)
            for tt in range(NTT):
                prog.emit("scalar", lambda e, tt=tt: e.activation(out=nrm[0:16, 40 + tt:41 + tt],
                                                                  in_=nrm[0:16, tt:tt + 1], func=AF.Sqrt),
                          waits=[(s_s, prog.n(s_s)), (s_v, prog.n(s_v))], inc=s_s)
                prog.emit("vector", lambda e, tt=tt: e.tensor_tensor(
                    out=nrm[0:16, 44 + tt:45 + tt], in0=nrm[0:16, 40 + tt:41 + tt],
                    in1=nrm[0:16, 37:38], op=ALU.mult),
                    waits=[(s_s, prog.n(s_s)), (s_v, prog.n(s_v))], inc=s_v)
                prog.emit("vector", lambda e, tt=tt: e.tensor_scalar(
                    out=nrm[0:16, 44 + tt:45 + tt], in0=nrm[0:16, 44 + tt:45 + tt],
                    scalar1=-SCALE, scalar2=None, op0=ALU.mult),
                    waits=[(s_v, prog.n(s_v))], inc=s_v)
            negm_done = prog.n(s_v)
            prog.emit("vector", lambda e: e.tensor_copy(out=rb[0:16, 0:16], in_=hself[0:16, 0:16] if False else consts.identf[0:16, 0:16]),
                      waits=[(s_v, negm_done)], inc=s_v)
            idok = prog.n(s_v)
            for tt in range(NTT):
                prog.emit("tensor", lambda e, tt=tt: e.matmul(
                    ps_n[0:1, 0:16], nrm[0:16, 44 + tt:45 + tt], rb[0:16, 0:16],
                    start=True, stop=True),
                    waits=[(s_v, idok), (s_v, prog.n(s_v)), (s_mm, prog.n(s_mm))], inc=s_mm)
                prog.emit("vector", lambda e: e.tensor_copy(out=mrow[:], in_=ps_n[0:1, 0:16]),
                          waits=[(s_mm, prog.n(s_mm))], inc=s_v)
                mr_done = prog.n(s_v)
                def mb(e, tt=tt):
                    return e.matmul(ps_n[:, 0:16], consts.ones1f[:], mrow[:], start=True, stop=True)
                prog.emit("tensor", mb, waits=[(s_v, mr_done)], inc=s_mm)
                prog.emit("vector", lambda e, tt=tt: e.tensor_copy(
                    out=mcols[:, tt * 16:(tt + 1) * 16], in_=ps_n[:, 0:16]),
                    waits=[(s_mm, prog.n(s_mm))], inc=s_v)
            m_done = prog.n(s_v)
            for qt in range(NTT):
                for h in range(NH):
                    c = h // 2
                    b0 = (h % 2) * 64
                    nkt = 4 * (qt + 1)
                    exp_done_pair = prog.n(s_s)
                    cons_done_pair = prog.n(s_mm)
                    npair = nkt // 2
                    for j in range(npair):
                        for u in range(2):
                            kt = 2 * j + u
                            def stm(e, c=c, b0=b0, kt=kt, qt=qt, u=u):
                                return e.matmul(ps_st[:, u * 512:(u + 1) * 512],
                                                k_fm[b0:b0 + 64, c * Lseq + kt * 128: c * Lseq + kt * 128 + 128],
                                                q_fm[b0:b0 + 64, c * Lseq + qt * 512: c * Lseq + qt * 512 + 512],
                                                start=True, stop=True)
                            prog.emit("tensor", stm,
                                      waits=([(s_v, m_done), (s_s, exp_done_pair)] if u == 0 else []),
                                      inc=s_mm)
                        st_done = prog.n(s_mm)
                        prog.emit("scalar", lambda e, h=h, qt=qt: e.activation(
                            out=pt_sb[:], in_=ps_st[:], func=AF.Exp, scale=SCALE,
                            bias=mcols[:, qt * 16 + h: qt * 16 + h + 1]),
                            waits=[(s_mm, st_done), (s_mm, cons_done_pair),
                                   (s_g, prog.n(s_g))], inc=s_s)
                        exp_done_pair = prog.n(s_s)
                        any_mask = False
                        for u in range(2):
                            kt = 2 * j + u
                            if kt >= 4 * qt:
                                any_mask = True
                                prog.emit("gpsimd", lambda e, qt=qt, kt=kt, u=u: e.affine_select(
                                    out=pt_sb[:, u * 512:(u + 1) * 512],
                                    in_=pt_sb[:, u * 512:(u + 1) * 512], pattern=[[1, 512]],
                                    compare_op=ALU.is_ge, fill=0.0,
                                    base=qt * 512 - kt * 128, channel_multiplier=-1),
                                    waits=[(s_s, exp_done_pair)], inc=s_g)
                        pt_ready = [(s_g, prog.n(s_g))] if any_mask else [(s_s, exp_done_pair)]
                        for u in range(2):
                            kt = 2 * j + u
                            def pvm(e, kt=kt, h=h, u=u, first=(kt == 0), last=(kt == nkt - 1)):
                                return e.matmul(ps_pv[0:64, :],
                                                v_tm[:, kt * 1024 + h * 64: kt * 1024 + h * 64 + 64],
                                                pt_sb[:, u * 512:(u + 1) * 512],
                                                start=first, stop=last)
                            prog.emit("tensor", pvm,
                                      waits=(pt_ready if u == 0 else []) +
                                            ([(s_v, prog.n(s_v))] if kt == 0 else []),
                                      inc=s_mm)
                            def smm(e, kt=kt, u=u, first=(kt == 0), last=(kt == nkt - 1)):
                                return e.matmul(ps_sm[0:1, :], consts.onescol[:],
                                                pt_sb[:, u * 512:(u + 1) * 512],
                                                start=first, stop=last)
                            prog.emit("tensor", smm,
                                      waits=([(s_v, prog.n(s_v))] if kt == 0 else []), inc=s_mm)
                        cons_done_pair = prog.n(s_mm)
                    pv_done = prog.n(s_mm)
                    prog.emit("vector", lambda e: e.reciprocal(out=recip[:], in_=ps_sm[0:1, :]),
                              waits=[(s_mm, pv_done), (s_v, prog.n(s_v))], inc=s_v)
                    rc_done = prog.n(s_v)
                    def rbm(e):
                        return e.matmul(ps_sm[0:64, :], consts.ones1f[:, 0:64], recip[:],
                                        start=True, stop=True)
                    prog.emit("tensor", rbm, waits=[(s_v, rc_done)], inc=s_mm)
                    prog.emit("vector", lambda e: e.tensor_copy(out=rb[0:64, :], in_=ps_sm[0:64, :]),
                              waits=[(s_mm, prog.n(s_mm))], inc=s_v)
                    rb_done = prog.n(s_v)
                    prog.emit("vector", lambda e, c=c, b0=b0: e.tensor_tensor(
                        out=attn_fm[b0:b0 + 64, c * 512:(c + 1) * 512], in0=ps_pv[0:64, :],
                        in1=rb[0:64, :], op=ALU.mult),
                        waits=[(s_v, rb_done)], inc=s_v)
                at_done = prog.n(s_v)
                for sp in range(4):
                    for dn in range(2):
                        pb = (sp * 2 + dn) % 2
                        def wom(e, sp=sp, dn=dn, pb=pb):
                            r = None
                            for kc in range(NC_D):
                                r = e.matmul(ps_p[pb][:],
                                             attn_fm[:, kc * 512 + sp * 128: kc * 512 + sp * 128 + 128],
                                             wob[:, kc * 1024 + dn * 512: kc * 1024 + dn * 512 + 512],
                                             start=(kc == 0), stop=(kc == NC_D - 1))
                            return r
                        prog.emit("tensor", wom, waits=[(s_v, at_done), (s_v, prog.n(s_v))], inc=s_mm)
                        prog.emit("vector", lambda e, sp=sp, dn=dn, pb=pb: e.tensor_copy(
                            out=acc[:, sp * D + dn * 512: sp * D + dn * 512 + 512], in_=ps_p[pb][:]),
                            waits=[(s_mm, prog.n(s_mm))], inc=s_v)
                for sp in range(4):
                    t0 = sb0 + qt * 512 + sp * 128
                    prog.emit("sync", lambda e, t0=t0: e.dma_start(
                        out=x_tm[:, 0:D], in_=x_dram[t0:t0 + 128, :]),
                        waits=[(s_v, prog.n(s_v))], inc=s_lx, inc_by=16)
                    ld = prog.n(s_lx)
                    emit_ln_epilogue(prog, x_tm[:, 0:D], acc[:, sp * D:(sp + 1) * D],
                                     y_dram[t0:t0 + 128, :], {"sq": lnsq, "st": lnst},
                                     s_v, s_s, s_st, waits_v=[(s_lx, ld), (s_v, prog.n(s_v)),
                                                              (s_s, prog.n(s_s)),
                                                              (s_st, prog.n(s_st))])
        for sq in range(NSEQ):
            _seq(sq)
        prog.fence(ALLSEMS)
    prog.barrier("sync", [(s_st, prog.n(s_st))])
    prog.block_end(ALLSEMS + [s_lh])




L2ATT_SPEC = {"wqT": (1024, 1024), "wkT": (1024, 256), "wvT": (1024, 256),
              "woT": (1024, 1024), "qAT": (1024, 8), "qB2T": (8, 1024),
              "kAT": (1024, 8), "kB2T": (8, 256), "vAT": (1024, 8), "vB2T": (8, 256),
              "oAT": (1024, 8), "oB2T": (8, 1024), "rope": (2048, 64)}


def prep_l2att_weights(inputs, Lseq=2048):
    cc = np.ascontiguousarray
    HD = 64
    inv = 1.0 / (10000.0 ** (np.arange(0, HD, 2, dtype=np.float32) / HD))
    fr = np.arange(Lseq, dtype=np.float32)[:, None] * inv[None, :]
    rope = np.concatenate([np.cos(fr), np.sin(fr)], axis=1).astype(np.float32)
    return {
        "wqT": cc(inputs["L2_attn_Wq"].T), "wkT": cc(inputs["L2_attn_Wk"].T),
        "wvT": cc(inputs["L2_attn_Wv"].T), "woT": cc(inputs["L2_attn_Wo"].T),
        "qAT": cc(inputs["L2_attn_qA"].T), "qB2T": cc(2.0 * inputs["L2_attn_qB"].T),
        "kAT": cc(inputs["L2_attn_kA"].T), "kB2T": cc(2.0 * inputs["L2_attn_kB"].T),
        "vAT": cc(inputs["L2_attn_vA"].T), "vB2T": cc(2.0 * inputs["L2_attn_vB"].T),
        "oAT": cc(inputs["L2_attn_oA"].T), "oB2T": cc(2.0 * inputs["L2_attn_oB"].T),
        "rope": rope,
    }


def emit_l2att(prog, C, x_dram, y_dram, W):
    T, Lseq = C["T"], C["L"]
    NSEQ = T // Lseq
    NT = Lseq // 128
    NTT = Lseq // 512
    NH, NKV = 16, 4
    SCALE = 1.0 / 8.0
    consts = prog.consts
    nc = prog.nc

    s_lx = prog.sem("a2_lx"); s_lw = prog.sem("a2_lw"); s_lr = prog.sem("a2_lr")
    s_v = prog.sem("a2_v"); s_s = prog.sem("a2_s"); s_g = prog.sem("a2_g")
    s_mm = prog.sem("a2_mm"); s_st = prog.sem("a2_st")
    s_cvt = prog.sem("a2_cvt")
    ALLSEMS = [s_lx, s_lw, s_lr, s_v, s_s, s_g, s_mm, s_st, s_cvt]
    prog.block_begin()

    WB, cvt_done = emit_weight_convert(prog, "l2a", W,
                                       ["wqT", "wkT", "wvT", "woT", "qAT", "qB2T",
                                        "kAT", "kB2T", "vAT", "vB2T", "oAT", "oB2T"], s_cvt)
    rope_in = nc.dram_tensor("l2a_rope", [2048, 64], F32, kind="ExternalInput").ap()

    x_tm = prog.sb("a2_x_tm", [128, D], F32)
    x_tmb = prog.sb("a2_x_tmb", [128, D], BF16)
    with prog.scope() as es:
        x_fm = prog.sb_s(es, "a2_x_fm", [128, NC_D * 512], BF16)
        q_fm = prog.sb_s(es, "a2_q_fm", [128, NC_D * Lseq], BF16)
        k_fm = prog.sb_s(es, "a2_k_fm", [128, 2 * Lseq], BF16)
        kdup = prog.sb_s(es, "a2_kdup", [128, 2 * Lseq], BF16)
        v_tm = prog.sb_s(es, "a2_v_tm", [128, NT * 256], BF16)
        ro = prog.sb_s(es, "a2_ro", [128, 64], F32)
        qro = prog.sb_s(es, "a2_qro", [128, 1024], F32)
        tmp1 = prog.sb_s(es, "a2_tmp1", [128, 512], F32)
        tmp2 = prog.sb_s(es, "a2_tmp2", [128, 512], F32)
        ptile = prog.sb_s(es, "a2_ptile", [128, 1024], BF16)
        sq2 = prog.sb_s(es, "a2_sq2", [128, 512], BF16)
        hselb = prog.sb_s(es, "a2_hselb", [128, 128], BF16)
        hself = prog.sb_s(es, "a2_hself", [128, 128], F32)
        wqTb = prog.sb_s(es, "a2_wqT", [128, NC_D * 1024], BF16)
        wkTb = prog.sb_s(es, "a2_wkT", [128, NC_D * 256], BF16)
        wvTb = prog.sb_s(es, "a2_wvT", [128, NC_D * 256], BF16)
        woTb = prog.sb_s(es, "a2_woT", [128, NC_D * 1024], BF16)
        lA = {}
        lB = {}
        for nm, outd in [("q", 1024), ("k", 256), ("v", 256)]:
            lA[nm] = prog.sb_s(es, f"a2_{nm}AT", [128, NC_D * 8], BF16)
            lB[nm] = prog.sb_s(es, f"a2_{nm}B2T", [8, outd], BF16)
        lA["o"] = prog.sb_s(es, "a2_oAT", [128, NC_D * 8], BF16)
        lB["o"] = prog.sb_s(es, "a2_oB2T", [8, 1024], BF16)
        zlo = prog.sb_s(es, "a2_zlo", [8, 512], BF16)
        nrm = prog.sb_s(es, "a2_nrm", [128, 64], F32)
        mrow = prog.sb_s(es, "a2_mrow", [1, 16], F32)
        mcols = prog.sb_s(es, "a2_mcols", [128, NTT * 16], F32)
        pt_sb = prog.sb_s(es, "a2_pt", [128, 1024], BF16)
        recip = prog.sb_s(es, "a2_recip", [1, 512], F32)
        rbb = prog.sb_s(es, "a2_rb", [128, 512], F32)
        attn_fm = prog.sb_s(es, "a2_attn", [128, NC_D * 512], BF16)
        acc = prog.sb_s(es, "a2_acc", [128, 4 * D], F32)
        lnsq = prog.sb_s(es, "a2_lnsq", [128, 2 * D], F32)
        lnst = prog.sb_s(es, "a2_lnst", [128, 8], F32)
        ps_tp = prog.ps_s(es, "a2_ps_tp", [128, 128], BF16)
        ps_p = [prog.ps_s(es, f"a2_ps_p{i}", [128, 512], F32) for i in range(2)]
        ps_z = prog.ps_s(es, "a2_ps_z", [128, 512], F32)
        ps_st = prog.ps_s(es, "a2_ps_st", [128, 1024], F32)
        ps_pv = prog.ps_s(es, "a2_ps_pv", [128, 512], F32)
        ps_sm = prog.ps_s(es, "a2_ps_sm", [128, 512], F32)

        hs_in2 = nc.dram_tensor("c_headsel2", [128, 128], F32, kind="ExternalInput").ap()
        s_lh = prog.sem("a2_lh")
        prog.emit("sync", lambda e: e.dma_start(out=hself[:], in_=hs_in2), inc=s_lh, inc_by=16)
        prog.emit("vector", lambda e: e.tensor_copy(out=hselb[:], in_=hself[:]),
                  waits=[(s_lh, prog.n(s_lh)), (s_v, prog.n(s_v))], inc=s_v)
        for key, buf, nchunks in [("wqT", wqTb, NC_D), ("wkT", wkTb, NC_D), ("wvT", wvTb, NC_D),
                                  ("woT", woTb, NC_D)]:
            prog.emit("sync", lambda e, key=key, buf=buf, nchunks=nchunks: e.dma_start(
                out=buf[:].rearrange("p (c n) -> p c n", c=nchunks),
                in_=WB[key].rearrange("(c p) n -> p c n", p=128)),
                waits=[(s_cvt, cvt_done)], inc=s_lw, inc_by=16)
        for nm in ["q", "k", "v", "o"]:
            prog.emit("sync", lambda e, nm=nm: e.dma_start(
                out=lA[nm][:].rearrange("p (c n) -> p c n", c=NC_D),
                in_=WB[nm + "AT"].rearrange("(c p) n -> p c n", p=128)),
                inc=s_lw, inc_by=16)
            prog.emit("sync", lambda e, nm=nm: e.dma_start(out=lB[nm][:], in_=WB[nm + "B2T"]),
                      inc=s_lw, inc_by=16)
        w_ready = prog.n(s_lw)
        prog.barrier("tensor", [(s_lw, w_ready)])

        def _seq(sq):
            sb0 = sq * Lseq
            for pt in range(NT):
                t0 = sb0 + pt * 128
                emit_x_ptile_prep(prog, x_dram, t0, x_tm, x_tmb, x_fm, 0, 128,
                                  ps_tp, s_lx, s_v, s_mm)
                xfm_ready = prog.n(s_v)
                prog.emit("sync", lambda e, pt=pt: e.dma_start(out=ro[:], in_=rope_in[pt * 128:(pt + 1) * 128, :]),
                          waits=[(s_v, prog.n(s_v))], inc=s_lr, inc_by=16)
                ro_ld = prog.n(s_lr)
                for nm in ["q", "k", "v"]:
                    def zmm(e, nm=nm):
                        r = None
                        for c in range(NC_D):
                            r = e.matmul(ps_z[0:8, 0:128], lA[nm][:, c * 8:(c + 1) * 8],
                                         x_fm[:, c * 128:(c + 1) * 128],
                                         start=(c == 0), stop=(c == NC_D - 1))
                        return r
                    prog.emit("tensor", zmm, waits=[(s_v, xfm_ready), (s_s, prog.n(s_s)),
                                                    (s_v, prog.n(s_v))], inc=s_mm)
                    prog.emit("scalar", lambda e, nm=nm: e.copy(
                        out=zlo[0:8, {"q": 0, "k": 128, "v": 256}[nm]:{"q": 0, "k": 128, "v": 256}[nm] + 128],
                        in_=ps_z[0:8, 0:128]),
                        waits=[(s_mm, prog.n(s_mm))], inc=s_s)
                z_done = prog.n(s_s)
                for nm, wb_, outw, zoff in [("q", wqTb, 1024, 0), ("k", wkTb, 256, 128), ("v", wvTb, 256, 256)]:
                    for dn in range(outw // 512 if outw >= 512 else 1):
                        wnn = min(512, outw)
                        pb = dn % 2
                        def prmm(e, nm=nm, wb_=wb_, dn=dn, wnn=wnn, pb=pb, zoff=zoff):
                            for kc in range(NC_D):
                                e.matmul(ps_p[pb][:, 0:wnn],
                                         x_fm[:, kc * 128:(kc + 1) * 128],
                                         wb_[:, kc * (1024 if nm == "q" else 256) + dn * 512: kc * (1024 if nm == "q" else 256) + dn * 512 + wnn],
                                         start=(kc == 0), stop=False)
                            return e.matmul(ps_p[pb][:, 0:wnn], zlo[0:8, zoff:zoff + 128],
                                            lB[nm][:, dn * 512: dn * 512 + wnn],
                                            start=False, stop=True)
                        prog.emit("tensor", prmm, waits=[(s_s, z_done), (s_v, prog.n(s_v)),
                                                         (s_s, prog.n(s_s))], inc=s_mm)
                        mmn = prog.n(s_mm)
                        if nm == "v":
                            prog.emit("scalar", lambda e, pt=pt: e.copy(
                                out=v_tm[:, pt * 256: pt * 256 + 256], in_=ps_p[0][:, 0:256]),
                                waits=[(s_mm, mmn), (s_v, prog.n(s_v)), (s_mm, prog.n(s_mm))], inc=s_s)
                        else:
                            prog.emit("vector", lambda e, nm=nm, dn=dn, pb=pb, wnn=wnn: e.tensor_copy(
                                out=(qro[:, dn * 512: dn * 512 + wnn] if nm == "q" else qro[:, 0:256]),
                                in_=ps_p[pb][:, 0:wnn]),
                                waits=[(s_mm, mmn), (s_lr, ro_ld), (s_v, prog.n(s_v))], inc=s_v)
                    if nm in ("q", "k"):
                        nh = 16 if nm == "q" else 4
                        wl = nh * 64
                        cosb = ro[:, 0:32].unsqueeze(1).broadcast_to((128, nh, 32))
                        sinb = ro[:, 32:64].unsqueeze(1).broadcast_to((128, nh, 32))
                        prog.emit("vector", lambda e, nh=nh, wl=wl, cosb=cosb: e.tensor_tensor(
                            out=tmp1[:, 0:nh * 32].rearrange("p (h j) -> p h j", h=nh),
                            in0=qro[:, 0:wl].rearrange("p (h j) -> p h j", h=nh)[:, :, 0:64:2],
                            in1=cosb, op=ALU.mult),
                            waits=[(s_v, prog.n(s_v)), (s_lr, ro_ld)], inc=s_v)
                        prog.emit("vector", lambda e, nh=nh, wl=wl, sinb=sinb: e.tensor_tensor(
                            out=tmp2[:, 0:nh * 32].rearrange("p (h j) -> p h j", h=nh),
                            in0=qro[:, 0:wl].rearrange("p (h j) -> p h j", h=nh)[:, :, 1:64:2],
                            in1=sinb, op=ALU.mult),
                            waits=[(s_v, prog.n(s_v))], inc=s_v)
                        prog.emit("vector", lambda e, nh=nh: e.tensor_sub(
                            out=ptile[:, 0:nh * 64].rearrange("p (h j) -> p h j", h=nh)[:, :, 0:32],
                            in0=tmp1[:, 0:nh * 32].rearrange("p (h j) -> p h j", h=nh),
                            in1=tmp2[:, 0:nh * 32].rearrange("p (h j) -> p h j", h=nh)),
                            waits=[(s_v, prog.n(s_v))], inc=s_v)
                        prog.emit("vector", lambda e, nh=nh, wl=wl, sinb=sinb: e.tensor_tensor(
                            out=tmp1[:, 0:nh * 32].rearrange("p (h j) -> p h j", h=nh),
                            in0=qro[:, 0:wl].rearrange("p (h j) -> p h j", h=nh)[:, :, 0:64:2],
                            in1=sinb, op=ALU.mult),
                            waits=[(s_v, prog.n(s_v))], inc=s_v)
                        prog.emit("vector", lambda e, nh=nh, wl=wl, cosb=cosb: e.tensor_tensor(
                            out=tmp2[:, 0:nh * 32].rearrange("p (h j) -> p h j", h=nh),
                            in0=qro[:, 0:wl].rearrange("p (h j) -> p h j", h=nh)[:, :, 1:64:2],
                            in1=cosb, op=ALU.mult),
                            waits=[(s_v, prog.n(s_v))], inc=s_v)
                        prog.emit("vector", lambda e, nh=nh: e.tensor_add(
                            out=ptile[:, 0:nh * 64].rearrange("p (h j) -> p h j", h=nh)[:, :, 32:64],
                            in0=tmp1[:, 0:nh * 32].rearrange("p (h j) -> p h j", h=nh),
                            in1=tmp2[:, 0:nh * 32].rearrange("p (h j) -> p h j", h=nh)),
                            waits=[(s_v, prog.n(s_v))], inc=s_v)
                        rope_done = prog.n(s_v)
                        nch = nh * 64 // 128
                        for c in range(nch):
                            def tpr(e, c=c):
                                return e.transpose(ps_tp[:], ptile[:, c * 128:(c + 1) * 128],
                                                   consts.ident[:])
                            prog.emit("tensor", tpr, waits=[(s_v, rope_done), (s_v, prog.n(s_v))], inc=s_mm)
                            dstb = q_fm if nm == "q" else k_fm
                            prog.emit("vector", lambda e, c=c, pt=pt, dstb=dstb: e.tensor_copy(
                                out=dstb[:, c * Lseq + pt * 128: c * Lseq + pt * 128 + 128], in_=ps_tp[:]),
                                waits=[(s_mm, prog.n(s_mm))], inc=s_v)
            prj_done = prog.n(s_v)
            prog.emit("sync", lambda e: e.dma_start(out=kdup[64:128, :], in_=k_fm[0:64, :]),
                      waits=[(s_v, prj_done), (s_mm, prog.n(s_mm))], inc=s_lr, inc_by=16)
            prog.emit("sync", lambda e: e.dma_start(out=kdup[0:64, :], in_=k_fm[64:128, :]),
                      inc=s_lr, inc_by=16)
            kdup_ld = prog.n(s_lr)
            prog.barrier("tensor", [(s_lr, kdup_ld)])
            for tt in range(NTT):
                for which, fmb, nch, col0 in [(0, q_fm, 8, 0), (1, k_fm, 2, 32)]:
                    for c in range(nch):
                        prog.emit("scalar", lambda e, c=c, fmb=fmb, tt=tt: e.activation(
                            out=sq2[:], in_=fmb[:, c * Lseq + tt * 512: c * Lseq + tt * 512 + 512],
                            func=AF.Square),
                            waits=[(s_v, prj_done), (s_s, prog.n(s_s)), (s_mm, prog.n(s_mm))], inc=s_s)
                        sqd = prog.n(s_s)
                        def nmm(e, c=c, nch=nch):
                            return e.matmul(ps_st[0:16, 0:512], hselb[:, c * 16:(c + 1) * 16], sq2[:],
                                            start=(c == 0), stop=(c == nch - 1))
                        prog.emit("tensor", nmm, waits=[(s_s, sqd), (s_v, prog.n(s_v))], inc=s_mm)
                    nd = prog.n(s_mm)
                    prog.emit("vector", lambda e, tt=tt, col0=col0: e.reduce_max(
                        out=nrm[0:16, col0 + tt: col0 + tt + 1], in_=ps_st[0:16, 0:512], axis=AX.X),
                        waits=[(s_mm, nd), (s_v, prog.n(s_v))], inc=s_v)
            n_done = prog.n(s_v)
            prog.emit("vector", lambda e: e.reduce_max(out=nrm[0:16, 36:37], in_=nrm[0:16, 32:32 + NTT],
                                                       axis=AX.X),
                      waits=[(s_v, n_done)], inc=s_v)
            prog.emit("scalar", lambda e: e.activation(out=nrm[0:16, 37:38], in_=nrm[0:16, 36:37],
                                                       func=AF.Sqrt),
                      waits=[(s_v, prog.n(s_v)), (s_s, prog.n(s_s))], inc=s_s)
            for tt in range(NTT):
                prog.emit("scalar", lambda e, tt=tt: e.activation(out=nrm[0:16, 40 + tt:41 + tt],
                                                                  in_=nrm[0:16, tt:tt + 1], func=AF.Sqrt),
                          waits=[(s_s, prog.n(s_s)), (s_v, prog.n(s_v))], inc=s_s)
            sq_done = prog.n(s_s)
            prog.emit("vector", lambda e: e.tensor_copy(out=rbb[0:16, 0:16], in_=consts.identf[0:16, 0:16]),
                      waits=[(s_s, sq_done), (s_v, prog.n(s_v))], inc=s_v)
            idok = prog.n(s_v)
            prog.emit("tensor", lambda e: e.matmul(ps_st[0:1, 0:16], nrm[0:16, 37:38], rbb[0:16, 0:16],
                                                   start=True, stop=True),
                      waits=[(s_v, idok), (s_mm, prog.n(s_mm))], inc=s_mm)
            prog.emit("vector", lambda e: e.tensor_copy(out=mrow[:], in_=ps_st[0:1, 0:16]),
                      waits=[(s_mm, prog.n(s_mm))], inc=s_v)
            prog.emit("vector", lambda e: e.reduce_max(out=mrow[0:1, 0:1] if False else nrm[0:1, 38:39],
                                                       in_=mrow[0:1, 0:4], axis=AX.X),
                      waits=[(s_v, prog.n(s_v))], inc=s_v)
            prog.emit("tensor", lambda e: e.matmul(ps_st[0:16, 0:1], consts.ones1f[:, 0:16],
                                                   nrm[0:1, 38:39], start=True, stop=True),
                      waits=[(s_v, prog.n(s_v)), (s_mm, prog.n(s_mm))], inc=s_mm)
            prog.emit("vector", lambda e: e.tensor_copy(out=nrm[0:16, 39:40], in_=ps_st[0:16, 0:1]),
                      waits=[(s_mm, prog.n(s_mm))], inc=s_v)
            kn_done = prog.n(s_v)
            for tt in range(NTT):
                prog.emit("vector", lambda e, tt=tt: e.tensor_tensor(
                    out=nrm[0:16, 44 + tt:45 + tt], in0=nrm[0:16, 40 + tt:41 + tt],
                    in1=nrm[0:16, 39:40], op=ALU.mult),
                    waits=[(s_v, kn_done), (s_v, prog.n(s_v))], inc=s_v)
                prog.emit("vector", lambda e, tt=tt: e.tensor_scalar(
                    out=nrm[0:16, 44 + tt:45 + tt], in0=nrm[0:16, 44 + tt:45 + tt],
                    scalar1=-SCALE, scalar2=None, op0=ALU.mult),
                    waits=[(s_v, prog.n(s_v))], inc=s_v)
                prog.emit("tensor", lambda e, tt=tt: e.matmul(
                    ps_st[0:1, 0:16], nrm[0:16, 44 + tt:45 + tt], rbb[0:16, 0:16],
                    start=True, stop=True),
                    waits=[(s_v, prog.n(s_v)), (s_mm, prog.n(s_mm))], inc=s_mm)
                prog.emit("vector", lambda e: e.tensor_copy(out=mrow[:], in_=ps_st[0:1, 0:16]),
                          waits=[(s_mm, prog.n(s_mm))], inc=s_v)
                mr_done = prog.n(s_v)
                prog.emit("tensor", lambda e: e.matmul(ps_st[:, 0:16], consts.ones1f[:], mrow[:],
                                                       start=True, stop=True),
                          waits=[(s_v, mr_done)], inc=s_mm)
                prog.emit("vector", lambda e, tt=tt: e.tensor_copy(
                    out=mcols[:, tt * 16:(tt + 1) * 16], in_=ps_st[:, 0:16]),
                    waits=[(s_mm, prog.n(s_mm))], inc=s_v)
            m_done = prog.n(s_v)
            for qt in range(NTT):
                for h in range(NH):
                    kvh = h // 4
                    kc_ = kvh // 2
                    kb0 = (kvh % 2) * 64
                    qc = h // 2
                    qb0 = (h % 2) * 64
                    ksrc = k_fm if (kvh % 2) == (h % 2) else kdup
                    exp_done_pair = prog.n(s_s)
                    cons_done_pair = prog.n(s_mm)
                    for j in range(NT // 2):
                        for u in range(2):
                            kt = 2 * j + u
                            def stm(e, ksrc=ksrc, kc_=kc_, qc=qc, qb0=qb0, kt=kt, qt=qt, u=u):
                                return e.matmul(ps_st[:, u * 512:(u + 1) * 512],
                                                ksrc[qb0:qb0 + 64, kc_ * Lseq + kt * 128: kc_ * Lseq + kt * 128 + 128],
                                                q_fm[qb0:qb0 + 64, qc * Lseq + qt * 512: qc * Lseq + qt * 512 + 512],
                                                start=True, stop=True)
                            prog.emit("tensor", stm,
                                      waits=([(s_v, m_done), (s_s, exp_done_pair)] if u == 0 else []),
                                      inc=s_mm)
                        st_done = prog.n(s_mm)
                        prog.emit("scalar", lambda e, h=h, qt=qt: e.activation(
                            out=pt_sb[:], in_=ps_st[:], func=AF.Exp, scale=SCALE,
                            bias=mcols[:, qt * 16 + h: qt * 16 + h + 1]),
                            waits=[(s_mm, st_done), (s_mm, cons_done_pair)], inc=s_s)
                        exp_done_pair = prog.n(s_s)
                        for u in range(2):
                            kt = 2 * j + u
                            def pvm(e, kt=kt, kvh=kvh, u=u, first=(kt == 0), last=(kt == NT - 1)):
                                return e.matmul(ps_pv[0:64, :],
                                                v_tm[:, kt * 256 + kvh * 64: kt * 256 + kvh * 64 + 64],
                                                pt_sb[:, u * 512:(u + 1) * 512],
                                                start=first, stop=last)
                            prog.emit("tensor", pvm,
                                      waits=([(s_s, exp_done_pair)] if u == 0 else []) +
                                            ([(s_v, prog.n(s_v))] if kt == 0 else []),
                                      inc=s_mm)
                            def smm(e, kt=kt, u=u, first=(kt == 0), last=(kt == NT - 1)):
                                return e.matmul(ps_sm[0:1, :], consts.onescol[:],
                                                pt_sb[:, u * 512:(u + 1) * 512],
                                                start=first, stop=last)
                            prog.emit("tensor", smm,
                                      waits=([(s_v, prog.n(s_v))] if kt == 0 else []), inc=s_mm)
                        cons_done_pair = prog.n(s_mm)
                    pv_done = prog.n(s_mm)
                    prog.emit("vector", lambda e: e.reciprocal(out=recip[:], in_=ps_sm[0:1, :]),
                              waits=[(s_mm, pv_done), (s_v, prog.n(s_v))], inc=s_v)
                    rc_done = prog.n(s_v)
                    def rbm(e):
                        return e.matmul(ps_sm[0:64, :], consts.ones1f[:, 0:64], recip[:],
                                        start=True, stop=True)
                    prog.emit("tensor", rbm, waits=[(s_v, rc_done)], inc=s_mm)
                    prog.emit("vector", lambda e: e.tensor_copy(out=rbb[0:64, :], in_=ps_sm[0:64, :]),
                              waits=[(s_mm, prog.n(s_mm))], inc=s_v)
                    rb_done = prog.n(s_v)
                    prog.emit("vector", lambda e, qc=qc, qb0=qb0: e.tensor_tensor(
                        out=attn_fm[qb0:qb0 + 64, qc * 512:(qc + 1) * 512], in0=ps_pv[0:64, :],
                        in1=rbb[0:64, :], op=ALU.mult),
                        waits=[(s_v, rb_done)], inc=s_v)
                at_done = prog.n(s_v)
                def zom(e):
                    r = None
                    for c in range(NC_D):
                        r = e.matmul(ps_z[0:8, :], lA["o"][:, c * 8:(c + 1) * 8],
                                     attn_fm[:, c * 512:(c + 1) * 512],
                                     start=(c == 0), stop=(c == NC_D - 1))
                    return r
                prog.emit("tensor", zom, waits=[(s_v, at_done), (s_s, prog.n(s_s))], inc=s_mm)
                prog.emit("scalar", lambda e: e.copy(out=zlo[0:8, 0:512], in_=ps_z[0:8, :]),
                          waits=[(s_mm, prog.n(s_mm))], inc=s_s)
                zo_done = prog.n(s_s)
                for sp in range(4):
                    for dn in range(2):
                        pb = (sp * 2 + dn) % 2
                        def wom(e, sp=sp, dn=dn, pb=pb):
                            for kc in range(NC_D):
                                e.matmul(ps_p[pb][:],
                                         attn_fm[:, kc * 512 + sp * 128: kc * 512 + sp * 128 + 128],
                                         woTb[:, kc * 1024 + dn * 512: kc * 1024 + dn * 512 + 512],
                                         start=(kc == 0), stop=False)
                            return e.matmul(ps_p[pb][:], zlo[0:8, sp * 128: sp * 128 + 128],
                                            lB["o"][:, dn * 512: dn * 512 + 512],
                                            start=False, stop=True)
                        prog.emit("tensor", wom, waits=[(s_s, zo_done), (s_v, prog.n(s_v))], inc=s_mm)
                        prog.emit("vector", lambda e, sp=sp, dn=dn, pb=pb: e.tensor_copy(
                            out=acc[:, sp * D + dn * 512: sp * D + dn * 512 + 512], in_=ps_p[pb][:]),
                            waits=[(s_mm, prog.n(s_mm))], inc=s_v)
                for sp in range(4):
                    t0 = sb0 + qt * 512 + sp * 128
                    prog.emit("sync", lambda e, t0=t0: e.dma_start(
                        out=x_tm[:, 0:D], in_=x_dram[t0:t0 + 128, :]),
                        waits=[(s_v, prog.n(s_v))], inc=s_lx, inc_by=16)
                    ld = prog.n(s_lx)
                    emit_ln_epilogue(prog, x_tm[:, 0:D], acc[:, sp * D:(sp + 1) * D],
                                     y_dram[t0:t0 + 128, :], {"sq": lnsq, "st": lnst},
                                     s_v, s_s, s_st, waits_v=[(s_lx, ld), (s_v, prog.n(s_v)),
                                                              (s_s, prog.n(s_s)),
                                                              (s_st, prog.n(s_st))])
        for sq in range(NSEQ):
            _seq(sq)
        prog.fence(ALLSEMS)
    prog.barrier("sync", [(s_st, prog.n(s_st))])
    prog.block_end(ALLSEMS + [s_lh])




L1ATT_SPEC = {"Wq": (1024, 1024), "Wk": (1024, 1024), "Wv": (1024, 1024),
              "Wo": (1024, 1024), "Wbfs": (1024, 48), "bfs_row": (1, 48),
              "Wtf1": (64, 32), "btf1_row": (32, 1), "Wtf2": (32, 1), "btf2_row": (1, 1)}


def prep_l1att_weights(inputs):
    cc = np.ascontiguousarray
    wbfs = np.concatenate([inputs["L1_attn_Wb"], inputs["L1_attn_Wfd"], inputs["L1_attn_Wsd"]], 1)
    brow = np.concatenate([np.zeros(16, np.float32), inputs["L1_attn_bfd"], inputs["L1_attn_bsd"]])
    return {"Wq": cc(inputs["L1_attn_Wq"]), "Wk": cc(inputs["L1_attn_Wk"]),
            "Wv": cc(inputs["L1_attn_Wv"]), "Wo": cc(inputs["L1_attn_Wo"]),
            "Wbfs": cc(wbfs), "bfs_row": cc(brow.reshape(1, 48)),
            "Wtf1": cc(inputs["L1_attn_Wtf1"]), "btf1_row": cc(inputs["L1_attn_btf1"].reshape(32, 1)),
            "Wtf2": cc(inputs["L1_attn_Wtf2"]), "btf2_row": cc(inputs["L1_attn_btf2"].reshape(1, 1))}


def emit_l1att(prog, C, x_dram, y_dram, W):
    T, Lseq = C["T"], C["L"]
    NSEQ = T // Lseq
    NT = Lseq // 128
    NTT = Lseq // 512
    NH = 16
    consts = prog.consts
    nc = prog.nc

    s_lx = prog.sem("a1_lx"); s_lw = prog.sem("a1_lw"); s_lsc = prog.sem("a1_lsc")
    s_v = prog.sem("a1_v"); s_s = prog.sem("a1_s"); s_g = prog.sem("a1_g")
    s_mm = prog.sem("a1_mm"); s_st = prog.sem("a1_st"); s_scr = prog.sem("a1_scr")
    s_cvt = prog.sem("a1_cvt")
    ALLSEMS = [s_lx, s_lw, s_lsc, s_v, s_s, s_g, s_mm, s_st, s_scr, s_cvt]
    prog.block_begin()

    WB, cvt_done = emit_weight_convert(prog, "l1a", W,
                                       ["Wq", "Wk", "Wv", "Wo", "Wbfs", "bfs_row",
                                        "Wtf1", "btf1_row", "Wtf2", "btf2_row"], s_cvt)
    qfm_scr = nc.dram_tensor("l1a_qfm_scr", [NSEQ, 128, NC_D * Lseq], BF16, kind="Internal").ap()

    x_tm = prog.sb("a1_x_tm", [128, D], F32)
    x_tmb = prog.sb("a1_x_tmb", [128, D], BF16)
    k_tm = prog.sb("a1_k_tm", [128, NT * 1024], BF16)
    v_tm = prog.sb("a1_v_tm", [128, NT * 1024], BF16)
    scols = prog.sb("a1_scols", [128, NT * 48], F32)
    acols = prog.sb("a1_acols", [128, NT * 16], F32)
    fswc = prog.sb("a1_fswc", [128, NT * 32], F32)
    M_sb = prog.sb("a1_M_sb", [128, NH * 128], BF16)
    dlog = prog.sb("a1_dlog", [32, Lseq], F32)
    dzero = prog.sb("a1_dzero", [32, Lseq], F32)
    dcs = prog.sb("a1_dcs", [32, Lseq], F32)
    dw = prog.sb("a1_dw", [32, Lseq], F32)
    dsum = prog.sb("a1_dsum", [32, 4], F32)

    def _seq(sq):
        sb0 = sq * Lseq
        with prog.scope() as es:
            x_fm = prog.sb_s(es, "a1_x_fm", [128, NC_D * 128], BF16)
            q_tm = prog.sb_s(es, "a1_q_tm", [128, NT * 1024], BF16)
            wqb = prog.sb_s(es, "a1_wqb", [128, NC_D * 1024], BF16)
            wkb = prog.sb_s(es, "a1_wkb", [128, NC_D * 1024], BF16)
            wvb = prog.sb_s(es, "a1_wvb", [128, NC_D * 1024], BF16)
            wbfsb = prog.sb_s(es, "a1_wbfsb", [128, NC_D * 48], BF16)
            bfsb = prog.sb_s(es, "a1_bfsb", [1, 48], BF16)
            sq2 = prog.sb_s(es, "a1_sq2", [128, 1024], F32)
            nsc = prog.sb_s(es, "a1_nsc", [128, 64], F32)
            ps_tp = prog.ps_s(es, "a1_ps_tp", [128, 128], BF16)
            ps_tpf = prog.ps_s(es, "a1_ps_tpf", [128, 128], F32)
            ps_p = [prog.ps_s(es, f"a1_ps_p{i}", [128, 512], F32) for i in range(2)]
            ps_b = prog.ps_s(es, "a1_ps_b", [128, 48], F32)
            for key, buf in [("Wq", wqb), ("Wk", wkb), ("Wv", wvb)]:
                prog.emit("sync", lambda e, key=key, buf=buf: e.dma_start(
                    out=buf[:].rearrange("p (c n) -> p c n", c=NC_D),
                    in_=WB[key].rearrange("(c p) n -> p c n", p=128)),
                    waits=[(s_cvt, cvt_done)], inc=s_lw, inc_by=16)
            prog.emit("sync", lambda e: e.dma_start(
                out=wbfsb[:].rearrange("p (c n) -> p c n", c=NC_D),
                in_=WB["Wbfs"].rearrange("(c p) n -> p c n", p=128)), inc=s_lw, inc_by=16)
            prog.emit("sync", lambda e: e.dma_start(out=bfsb[:], in_=WB["bfs_row"]), inc=s_lw, inc_by=16)
            w_ready = prog.n(s_lw)
            prog.barrier("tensor", [(s_lw, w_ready)])
            for pt in range(NT):
                t0 = sb0 + pt * 128
                prog.barrier("vector", [(s_scr, prog.n(s_scr))])
                emit_x_ptile_prep(prog, x_dram, t0, x_tm, x_tmb, x_fm, 0, 128,
                                  ps_tp, s_lx, s_v, s_mm)
                xfm_ready = prog.n(s_v)
                for nm, wb_, dst in [("q", wqb, q_tm), ("k", wkb, k_tm), ("v", wvb, v_tm)]:
                    for dn in range(2):
                        pb = dn % 2
                        def prmm(e, wb_=wb_, dn=dn, pb=pb):
                            r = None
                            for kc in range(NC_D):
                                r = e.matmul(ps_p[pb][:],
                                             x_fm[:, kc * 128:(kc + 1) * 128],
                                             wb_[:, kc * 1024 + dn * 512: kc * 1024 + dn * 512 + 512],
                                             start=(kc == 0), stop=(kc == NC_D - 1))
                            return r
                        prog.emit("tensor", prmm, waits=[(s_v, xfm_ready), (s_s, prog.n(s_s))], inc=s_mm)
                        prog.emit("scalar", lambda e, pt=pt, dn=dn, pb=pb, dst=dst: e.copy(
                            out=dst[:, pt * 1024 + dn * 512: pt * 1024 + dn * 512 + 512],
                            in_=ps_p[pb][:]),
                            waits=[(s_mm, prog.n(s_mm)), (s_s, prog.n(s_s))], inc=s_s)
                def bmm(e):
                    for kc in range(NC_D):
                        e.matmul(ps_b[:], x_fm[:, kc * 128:(kc + 1) * 128],
                                 wbfsb[:, kc * 48:(kc + 1) * 48], start=(kc == 0), stop=False)
                    return e.matmul(ps_b[:], consts.ones1[:], bfsb[:], start=False, stop=True)
                prog.emit("tensor", bmm, waits=[(s_v, xfm_ready), (s_s, prog.n(s_s))], inc=s_mm)
                prog.emit("scalar", lambda e, pt=pt: e.activation(
                    out=scols[:, pt * 48:(pt + 1) * 48], in_=ps_b[:], func=AF.Sigmoid),
                    waits=[(s_mm, prog.n(s_mm)), (s_s, prog.n(s_s))], inc=s_s)
                sig_done = prog.n(s_s)
                prog.emit("scalar", lambda e, pt=pt: e.activation(
                    out=sq2[:, 0:32], in_=scols[:, pt * 48 + 16: pt * 48 + 48], func=AF.Ln),
                    waits=[(s_s, sig_done), (s_v, prog.n(s_v))], inc=s_s)
                ln_done = prog.n(s_s)
                def tpd(e):
                    return e.transpose(ps_tpf[0:32, :], sq2[:, 0:32], consts.identf[:])
                prog.emit("tensor", tpd, waits=[(s_s, ln_done), (s_v, prog.n(s_v))], inc=s_mm)
                prog.emit("vector", lambda e, pt=pt: e.tensor_copy(
                    out=dlog[:, pt * 128:(pt + 1) * 128], in_=ps_tpf[0:32, :]),
                    waits=[(s_mm, prog.n(s_mm))], inc=s_v)
                for which, dst in [(0, q_tm), (1, k_tm)]:
                    prog.emit("scalar", lambda e, pt=pt, dst=dst: e.activation(
                        out=sq2[:], in_=dst[:, pt * 1024:(pt + 1) * 1024], func=AF.Square),
                        waits=[(s_s, prog.n(s_s)), (s_v, prog.n(s_v))], inc=s_s)
                    sqd = prog.n(s_s)
                    prog.emit("vector", lambda e, which=which: e.tensor_reduce(
                        out=nsc[:, which * 16:(which + 1) * 16],
                        in_=sq2[:].rearrange("p (h j) -> p h j", h=NH), axis=AX.X, op=ALU.add),
                        waits=[(s_s, sqd), (s_v, prog.n(s_v))], inc=s_v)
                    prog.emit("scalar", lambda e, which=which: e.activation(
                        out=nsc[:, 32 + which * 16: 32 + (which + 1) * 16],
                        in_=nsc[:, which * 16:(which + 1) * 16], func=AF.Sqrt),
                        waits=[(s_v, prog.n(s_v)), (s_s, prog.n(s_s))], inc=s_s)
                    prog.emit("vector", lambda e, which=which: e.tensor_scalar(
                        out=nsc[:, 32 + which * 16: 32 + (which + 1) * 16],
                        in0=nsc[:, 32 + which * 16: 32 + (which + 1) * 16],
                        scalar1=1e-12, scalar2=None, op0=ALU.max),
                        waits=[(s_s, prog.n(s_s)), (s_v, prog.n(s_v))], inc=s_v)
                    prog.emit("vector", lambda e, which=which: e.reciprocal(
                        out=nsc[:, which * 16:(which + 1) * 16],
                        in_=nsc[:, 32 + which * 16: 32 + (which + 1) * 16]),
                        waits=[(s_v, prog.n(s_v))], inc=s_v)
                    rd = prog.n(s_v)
                    for h in range(NH):
                        prog.emit("vector", lambda e, pt=pt, h=h, which=which, dst=dst: e.tensor_scalar(
                            out=dst[:, pt * 1024 + h * 64: pt * 1024 + h * 64 + 64],
                            in0=dst[:, pt * 1024 + h * 64: pt * 1024 + h * 64 + 64],
                            scalar1=nsc[:, which * 16 + h: which * 16 + h + 1], scalar2=None,
                            op0=ALU.mult),
                            waits=[(s_v, rd), (s_v, prog.n(s_v)), (s_s, prog.n(s_s)),
                                   (s_mm, prog.n(s_mm))], inc=s_v)
                qk_done = prog.n(s_v)
                for c in range(NC_D):
                    def tpq(e, pt=pt, c=c):
                        return e.transpose(ps_tp[:], q_tm[:, pt * 1024 + c * 128: pt * 1024 + c * 128 + 128],
                                           consts.ident[:])
                    prog.emit("tensor", tpq, waits=[(s_v, qk_done), (s_v, prog.n(s_v))], inc=s_mm)
                    prog.emit("vector", lambda e, c=c: e.tensor_copy(
                        out=x_fm[:, c * 128:(c + 1) * 128], in_=ps_tp[:]),
                        waits=[(s_mm, prog.n(s_mm)), (s_scr, prog.n(s_scr))], inc=s_v)
                qfm_done = prog.n(s_v)
                prog.emit("sync", lambda e, sq=sq, pt=pt: e.dma_start(
                    out=qfm_scr[sq][:, :].rearrange("p (c w) -> p c w", c=NC_D)[:, :, pt * 128:(pt + 1) * 128],
                    in_=x_fm[:].rearrange("p (c n) -> p c n", c=NC_D)),
                    waits=[(s_v, qfm_done)], inc=s_scr, inc_by=16)
            prog.emit("vector", lambda e: e.memset(dzero[:], 0.0), waits=[(s_v, prog.n(s_v))], inc=s_v)
            prog.emit("vector", lambda e: e.tensor_tensor_scan(
                out=dcs[:], data0=dlog[:], data1=dzero[:], initial=0.0,
                op0=ALU.add, op1=ALU.add),
                waits=[(s_v, prog.n(s_v))], inc=s_v)
            cs_done = prog.n(s_v)
            prog.emit("vector", lambda e: e.tensor_copy(out=dsum[:, 0:1], in_=dcs[:, Lseq - 1:Lseq]),
                      waits=[(s_v, cs_done)], inc=s_v)
            prog.emit("scalar", lambda e: e.activation(out=dw[:], in_=dcs[:], func=AF.Exp,
                                                       scale=-1.0, bias=dsum[:, 0:1],
                                                       accum_out=dsum[:, 1:2]),
                      waits=[(s_v, prog.n(s_v)), (s_s, prog.n(s_s))], inc=s_s)
            prog.emit("vector", lambda e: e.tensor_scalar(out=dsum[:, 2:3], in0=dsum[:, 1:2],
                                                          scalar1=1e-6, scalar2=None, op0=ALU.add),
                      waits=[(s_s, prog.n(s_s)), (s_v, prog.n(s_v))], inc=s_v)
            prog.emit("vector", lambda e: e.reciprocal(out=dsum[:, 3:4], in_=dsum[:, 2:3]),
                      waits=[(s_v, prog.n(s_v))], inc=s_v)
            prog.emit("vector", lambda e: e.tensor_scalar(out=dw[:], in0=dw[:],
                                                          scalar1=dsum[:, 3:4], scalar2=None,
                                                          op0=ALU.mult),
                      waits=[(s_v, prog.n(s_v))], inc=s_v)
            dw_done = prog.n(s_v)
            for pt in range(NT):
                def tpw(e, pt=pt):
                    return e.transpose(ps_tpf[:, 0:32], dlog[:, pt * 128:(pt + 1) * 128]
                                       if False else dw[:, pt * 128:(pt + 1) * 128],
                                       consts.identf[0:32, 0:32])
                prog.emit("tensor", tpw, waits=[(s_v, dw_done), (s_v, prog.n(s_v))], inc=s_mm)
                prog.emit("vector", lambda e, pt=pt: e.tensor_copy(
                    out=fswc[:, pt * 32:(pt + 1) * 32], in_=ps_tpf[:, 0:32]),
                    waits=[(s_mm, prog.n(s_mm))], inc=s_v)
            prog.fence(ALLSEMS)

        with prog.scope() as es:
            kb = prog.sb_s(es, "a1_kb", [128, 64], BF16)
            kf = prog.sb_s(es, "a1_kf", [128, 64], BF16)
            ks = prog.sb_s(es, "a1_ks", [128, 64], BF16)
            vb = prog.sb_s(es, "a1_vb", [128, 64], BF16)
            kbfm = prog.sb_s(es, "a1_kbfm", [64, 512], BF16)
            hmtm = prog.sb_s(es, "a1_hmtm", [32, 512], BF16)
            hmfm = prog.sb_s(es, "a1_hmfm", [32, 128], BF16)
            colsc = prog.sb_s(es, "a1_colsc", [128, 8], F32)
            wtf1b = prog.sb_s(es, "a1_wtf1b", [64, 32], BF16)
            btf1b = prog.sb_s(es, "a1_btf1b", [32, 1], F32)
            wtf2b = prog.sb_s(es, "a1_wtf2b", [32, 1], BF16)
            btf2b = prog.sb_s(es, "a1_btf2b", [1, 1], BF16)
            ps_tp = prog.ps_s(es, "a1_ps_tpB", [128, 128], BF16)
            ps_Mf = prog.ps_s(es, "a1_ps_Mf", [128, 64], F32)
            ps_Ms = prog.ps_s(es, "a1_ps_Ms", [128, 64], F32)
            ps_hm = prog.ps_s(es, "a1_ps_hm", [32, 512], F32)
            ps_tf = prog.ps_s(es, "a1_ps_tf", [128, 1], F32)
            prog.emit("sync", lambda e: e.dma_start(out=wtf1b[:], in_=WB["Wtf1"]),
                      waits=[(s_cvt, cvt_done)], inc=s_lw, inc_by=16)
            prog.emit("sync", lambda e: e.dma_start(out=btf1b[:], in_=W["btf1_row"]), inc=s_lw, inc_by=16)
            prog.emit("sync", lambda e: e.dma_start(out=wtf2b[:], in_=WB["Wtf2"]), inc=s_lw, inc_by=16)
            prog.emit("sync", lambda e: e.dma_start(out=btf2b[:], in_=WB["btf2_row"]), inc=s_lw, inc_by=16)
            wtf_ready = prog.n(s_lw)
            prog.barrier("tensor", [(s_lw, wtf_ready)])
            for h in range(NH):
                for pt in range(NT):
                    prog.emit("vector", lambda e, h=h, pt=pt: e.tensor_scalar(
                        out=kb[:], in0=k_tm[:, pt * 1024 + h * 64: pt * 1024 + h * 64 + 64],
                        scalar1=scols[:, pt * 48 + h: pt * 48 + h + 1], scalar2=None, op0=ALU.mult),
                        waits=[(s_v, prog.n(s_v)), (s_mm, prog.n(s_mm)), (s_g, prog.n(s_g))], inc=s_v)
                    prog.emit("vector", lambda e, h=h, pt=pt: e.tensor_scalar(
                        out=vb[:], in0=v_tm[:, pt * 1024 + h * 64: pt * 1024 + h * 64 + 64],
                        scalar1=scols[:, pt * 48 + h: pt * 48 + h + 1], scalar2=None, op0=ALU.mult),
                        waits=[(s_v, prog.n(s_v)), (s_mm, prog.n(s_mm))], inc=s_v)
                    prog.emit("vector", lambda e, h=h, pt=pt: e.tensor_scalar(
                        out=kf[:], in0=kb[:],
                        scalar1=fswc[:, pt * 32 + h: pt * 32 + h + 1], scalar2=None, op0=ALU.mult),
                        waits=[(s_v, prog.n(s_v)), (s_mm, prog.n(s_mm))], inc=s_v)
                    prog.emit("vector", lambda e, h=h, pt=pt: e.tensor_scalar(
                        out=ks[:], in0=kb[:],
                        scalar1=fswc[:, pt * 32 + 16 + h: pt * 32 + 16 + h + 1], scalar2=None, op0=ALU.mult),
                        waits=[(s_v, prog.n(s_v)), (s_mm, prog.n(s_mm))], inc=s_v)
                    built = prog.n(s_v)
                    def mfm(e, pt=pt, first=(pt == 0), last=(pt == NT - 1)):
                        return e.matmul(ps_Mf[0:64, :], kf[:], vb[:], start=first, stop=last)
                    prog.emit("tensor", mfm, waits=[(s_v, built)], inc=s_mm)
                    def msm(e, pt=pt, first=(pt == 0), last=(pt == NT - 1)):
                        return e.matmul(ps_Ms[0:64, :], ks[:], vb[:], start=first, stop=last)
                    prog.emit("tensor", msm, waits=[(s_v, built)], inc=s_mm)
                    def tpkb(e, pt=pt):
                        return e.transpose(ps_tp[0:64, :], kb[:], consts.ident[:])
                    prog.emit("tensor", tpkb, waits=[(s_v, built), (s_v, prog.n(s_v))], inc=s_mm)
                    prog.emit("vector", lambda e, pt=pt: e.tensor_copy(
                        out=kbfm[:, (pt % 4) * 128:(pt % 4) * 128 + 128], in_=ps_tp[0:64, :]),
                        waits=[(s_mm, prog.n(s_mm))], inc=s_v)
                    if pt % 4 == 3:
                        kbfm_done = prog.n(s_v)
                        def hmm(e):
                            return e.matmul(ps_hm[:], wtf1b[:], kbfm[:], start=True, stop=True)
                        prog.emit("tensor", hmm, waits=[(s_v, kbfm_done), (s_s, prog.n(s_s))], inc=s_mm)
                        prog.emit("scalar", lambda e: e.activation(out=hmtm[:], in_=ps_hm[:],
                                                                   func=AF.Silu, bias=btf1b[:]),
                                  waits=[(s_mm, prog.n(s_mm)), (s_v, prog.n(s_v)),
                                         (s_s, prog.n(s_s))], inc=s_s)
                        hm_done = prog.n(s_s)
                        for u in range(4):
                            pt0 = pt - 3 + u
                            def tfm(e, u=u):
                                e.matmul(ps_tf[:], hmtm[:, u * 128:(u + 1) * 128], wtf2b[:],
                                         start=True, stop=False)
                                return e.matmul(ps_tf[:], consts.ones1[:], btf2b[:],
                                                start=False, stop=True)
                            prog.emit("tensor", tfm, waits=[(s_s, hm_done), (s_s, prog.n(s_s)),
                                                            (s_v, prog.n(s_v))], inc=s_mm)
                            prog.emit("scalar", lambda e: e.activation(out=colsc[:, 0:1], in_=ps_tf[:],
                                                                       func=AF.Sigmoid),
                                      waits=[(s_mm, prog.n(s_mm)), (s_v, prog.n(s_v)),
                                             (s_s, prog.n(s_s))], inc=s_s)
                            prog.emit("vector", lambda e: e.tensor_scalar(
                                out=colsc[:, 1:2], in0=colsc[:, 0:1], scalar1=0.01, scalar2=0.99,
                                op0=ALU.max, op1=ALU.min),
                                waits=[(s_s, prog.n(s_s)), (s_v, prog.n(s_v))], inc=s_v)
                            prog.emit("vector", lambda e, h=h, pt0=pt0: e.tensor_scalar(
                                out=acols[:, pt0 * 16 + h: pt0 * 16 + h + 1], in0=colsc[:, 1:2],
                                scalar1=0.3, scalar2=0.5, op0=ALU.mult, op1=ALU.add),
                                waits=[(s_v, prog.n(s_v))], inc=s_v)
                m_done = prog.n(s_mm)
                prog.emit("vector", lambda e, h=h: e.tensor_copy(
                    out=M_sb[0:64, h * 128: h * 128 + 64], in_=ps_Mf[0:64, :]),
                    waits=[(s_mm, m_done), (s_scr, prog.n(s_scr))], inc=s_v)
                prog.emit("vector", lambda e, h=h: e.tensor_copy(
                    out=M_sb[0:64, h * 128 + 64: h * 128 + 128], in_=ps_Ms[0:64, :]),
                    waits=[(s_mm, m_done)], inc=s_v)
                mev = prog.n(s_v)
                prog.emit("sync", lambda e, h=h: e.dma_start(
                    out=M_sb[64:128, h * 128:(h + 1) * 128], in_=M_sb[0:64, h * 128:(h + 1) * 128]),
                    waits=[(s_v, mev)], inc=s_scr, inc_by=16)
            prog.fence(ALLSEMS)

        with prog.scope() as es:
            q_fm = prog.sb_s(es, "a1_q_fm", [128, NC_D * Lseq], BF16)
            otm = prog.sb_s(es, "a1_otm", [128, 1024], BF16)
            offp = [prog.sb_s(es, f"a1_offp{i}", [128, 64], F32) for i in range(2)]
            o_fm = prog.sb_s(es, "a1_o_fm", [128, NC_D * 512], BF16)
            wob = prog.sb_s(es, "a1_wob", [128, NC_D * 1024], BF16)
            acc = prog.sb_s(es, "a1_acc", [128, 4 * D], F32)
            lnsq = prog.sb_s(es, "a1_lnsq", [128, 2 * D], F32)
            lnst = prog.sb_s(es, "a1_lnst", [128, 8], F32)
            ps_tp = prog.ps_s(es, "a1_ps_tpC", [128, 128], BF16)
            ps_of = [prog.ps_s(es, f"a1_ps_of{i}", [128, 64], F32) for i in range(2)]
            ps_os = [prog.ps_s(es, f"a1_ps_os{i}", [128, 64], F32) for i in range(2)]
            ps_p = [prog.ps_s(es, f"a1_ps_pC{i}", [128, 512], F32) for i in range(2)]
            prog.emit("sync", lambda e: e.dma_start(
                out=wob[:].rearrange("p (c n) -> p c n", c=NC_D),
                in_=WB["Wo"].rearrange("(c p) n -> p c n", p=128)),
                waits=[(s_cvt, cvt_done)], inc=s_lw, inc_by=16)
            prog.emit("sync", lambda e, sq=sq: e.dma_start(
                out=q_fm[:].rearrange("p (c w) -> p c w", c=NC_D),
                in_=qfm_scr[sq][:, :].rearrange("p (c w) -> p c w", c=NC_D)),
                inc=s_lw, inc_by=16)
            wo_ready = prog.n(s_lw)
            prog.barrier("tensor", [(s_lw, wo_ready)])
            for tt in range(NTT):
                for sp in range(4):
                    pt = tt * 4 + sp
                    blend_done = [prog.n(s_v)] * 2
                    for h in range(NH):
                        sl = h % 2
                        qc = h // 2
                        qb0 = (h % 2) * 64
                        def ofmm(e, qc=qc, qb0=qb0, pt=pt, h=h, sl=sl):
                            return e.matmul(ps_of[sl][:], q_fm[qb0:qb0 + 64, qc * Lseq + pt * 128: qc * Lseq + pt * 128 + 128],
                                            M_sb[qb0:qb0 + 64, h * 128: h * 128 + 64],
                                            start=True, stop=True)
                        prog.emit("tensor", ofmm, waits=[(s_v, blend_done[sl]), (s_s, prog.n(s_s))], inc=s_mm)
                        def osmm(e, qc=qc, qb0=qb0, pt=pt, h=h, sl=sl):
                            return e.matmul(ps_os[sl][:], q_fm[qb0:qb0 + 64, qc * Lseq + pt * 128: qc * Lseq + pt * 128 + 128],
                                            M_sb[qb0:qb0 + 64, h * 128 + 64: h * 128 + 128],
                                            start=True, stop=True)
                        prog.emit("tensor", osmm, waits=[], inc=s_mm)
                        oo_done = prog.n(s_mm)
                        prog.emit("vector", lambda e, sl=sl: e.tensor_copy(out=offp[sl][:], in_=ps_of[sl][:]),
                                  waits=[(s_mm, oo_done)], inc=s_v)
                        prog.emit("vector", lambda e, sl=sl: e.tensor_sub(out=offp[sl][:], in0=offp[sl][:],
                                                                          in1=ps_os[sl][:]),
                                  waits=[(s_v, prog.n(s_v))], inc=s_v)
                        prog.emit("vector", lambda e, pt=pt, h=h, sl=sl: e.tensor_scalar(
                            out=offp[sl][:], in0=offp[sl][:],
                            scalar1=acols[:, pt * 16 + h: pt * 16 + h + 1], scalar2=None, op0=ALU.mult),
                            waits=[(s_v, prog.n(s_v))], inc=s_v)
                        prog.emit("vector", lambda e, h=h, sl=sl: e.tensor_tensor(
                            out=otm[:, h * 64:(h + 1) * 64], in0=offp[sl][:], in1=ps_os[sl][:], op=ALU.add),
                            waits=[(s_v, prog.n(s_v))], inc=s_v)
                        blend_done[sl] = prog.n(s_v)
                    ot_done = prog.n(s_v)
                    for c in range(NC_D):
                        def tpo(e, c=c):
                            return e.transpose(ps_tp[:], otm[:, c * 128:(c + 1) * 128], consts.ident[:])
                        prog.emit("tensor", tpo, waits=[(s_v, ot_done), (s_v, prog.n(s_v))], inc=s_mm)
                        prog.emit("vector", lambda e, c=c, sp=sp: e.tensor_copy(
                            out=o_fm[:, c * 512 + sp * 128: c * 512 + sp * 128 + 128], in_=ps_tp[:]),
                            waits=[(s_mm, prog.n(s_mm))], inc=s_v)
                ofm_done = prog.n(s_v)
                for sp in range(4):
                    for dn in range(2):
                        pb = (sp * 2 + dn) % 2
                        def wom(e, sp=sp, dn=dn, pb=pb):
                            r = None
                            for kc in range(NC_D):
                                r = e.matmul(ps_p[pb][:],
                                             o_fm[:, kc * 512 + sp * 128: kc * 512 + sp * 128 + 128],
                                             wob[:, kc * 1024 + dn * 512: kc * 1024 + dn * 512 + 512],
                                             start=(kc == 0), stop=(kc == NC_D - 1))
                            return r
                        prog.emit("tensor", wom, waits=[(s_v, ofm_done), (s_v, prog.n(s_v))], inc=s_mm)
                        prog.emit("vector", lambda e, sp=sp, dn=dn, pb=pb: e.tensor_copy(
                            out=acc[:, sp * D + dn * 512: sp * D + dn * 512 + 512], in_=ps_p[pb][:]),
                            waits=[(s_mm, prog.n(s_mm))], inc=s_v)
                for sp in range(4):
                    t0 = sb0 + tt * 512 + sp * 128
                    prog.emit("sync", lambda e, t0=t0: e.dma_start(
                        out=x_tm[:, 0:D], in_=x_dram[t0:t0 + 128, :]),
                        waits=[(s_v, prog.n(s_v))], inc=s_lx, inc_by=16)
                    ld = prog.n(s_lx)
                    emit_ln_epilogue(prog, x_tm[:, 0:D], acc[:, sp * D:(sp + 1) * D],
                                     y_dram[t0:t0 + 128, :], {"sq": lnsq, "st": lnst},
                                     s_v, s_s, s_st, waits_v=[(s_lx, ld), (s_v, prog.n(s_v)),
                                                              (s_s, prog.n(s_s)),
                                                              (s_st, prog.n(s_st))])
            prog.fence(ALLSEMS)

    for sq in range(NSEQ):
        _seq(sq)
    prog.barrier("sync", [(s_st, prog.n(s_st))])
    prog.block_end(ALLSEMS)



L3FFN_SPEC = {"Wr": (1024, 8), "brrow": (1, 8), "W1": (8, 1024, 2048),
              "b1col": (8, 128, 16), "W2": (8, 2048, 1024), "b2row": (1, 8 * 1024)}


def prep_l3ffn_weights(inputs):
    return {
        "Wr": np.ascontiguousarray(inputs["L3_ffn_Wr"]),
        "brrow": np.ascontiguousarray(inputs["L3_ffn_br"]).reshape(1, 8),
        "W1": np.ascontiguousarray(inputs["L3_ffn_W1"]),
        "b1col": np.ascontiguousarray(
            inputs["L3_ffn_b1"].reshape(8, 16, 128).transpose(0, 2, 1)),
        "W2": np.ascontiguousarray(inputs["L3_ffn_W2"]),
        "b2row": np.ascontiguousarray(inputs["L3_ffn_b2"]).reshape(1, 8 * 1024),
    }


def declare_l3ffn_weights(nc, prefix="l3f"):
    aps = {}
    aps["Wr"] = nc.dram_tensor(f"{prefix}_Wr", [1024, 8], F32, kind="ExternalInput").ap()
    aps["brrow"] = nc.dram_tensor(f"{prefix}_br", [1, 8], F32, kind="ExternalInput").ap()
    aps["W1"] = nc.dram_tensor(f"{prefix}_W1", [8, 1024, 2048], F32, kind="ExternalInput").ap()
    aps["b1col"] = nc.dram_tensor(f"{prefix}_b1", [8, 128, 16], F32, kind="ExternalInput").ap()
    aps["W2"] = nc.dram_tensor(f"{prefix}_W2", [8, 2048, 1024], F32, kind="ExternalInput").ap()
    aps["b2row"] = nc.dram_tensor(f"{prefix}_b2", [1, 8 * 1024], F32, kind="ExternalInput").ap()
    return aps



def prep_all_weights(inputs):
    out = {}
    out.update({f"l0a_{k}": v for k, v in prep_l0att_weights(inputs).items()})
    out.update({f"l0f_{k}": v for k, v in prep_l0ffn_weights(inputs).items()})
    out.update({f"l1a_{k}": v for k, v in prep_l1att_weights(inputs).items()})
    out.update({f"l1f_{k}": v for k, v in prep_l1ffn_weights(inputs).items()})
    w2 = prep_l2att_weights(inputs, Lseq=2048)
    out.update({f"l2a_{k}": v for k, v in w2.items() if k != "rope"})
    out["l2a_rope"] = w2["rope"]
    out.update({f"l2f_{k}": v for k, v in prep_l2ffn_weights(inputs).items()})
    out.update({f"l3a_{k}": v for k, v in prep_l3att_weights(inputs).items()})
    out.update({f"l3f_{k}": v for k, v in prep_l3ffn_weights(inputs).items()})
    out.update(host_consts())
    out.update(host_consts_l0())
    out.update(host_consts_att())
    return out


def _prog_setup():
    nc = build_nc()
    prog = Prog(nc)
    s_cl = prog.sem("c_ld"); s_cv = prog.sem("c_v")
    prog.consts = Consts(prog, s_cl, s_cv)
    for e in Prog.ENGINES:
        prog.barrier(e, [(s_cv, prog.consts.ready_v)])
    return nc, prog


def build_module(C, n_cores=8):
    nc, prog = _prog_setup()
    T = C["T"]
    x_in = nc.dram_tensor("x", [T, D], F32, kind="ExternalInput").ap()
    y_out = nc.dram_tensor("y", [T, D], F32, kind="ExternalOutput").ap()
    bufs = [nc.dram_tensor(f"stream{i}", [T, D], F32, kind="Internal").ap()
            for i in range(7)]
    chain = [x_in] + bufs + [y_out]
    _emit_chain(nc, prog, C, chain, n_cores, "all")
    prog.finalize()
    return nc


def _emit_chain(nc, prog, C, chain, n_cores, l2f_mode):
    Wl0a = declare_weights(nc, "l0a", {k: s for k, s in L0ATT_SPEC.items() if k != "ltau"})
    Wl0f = declare_weights(nc, "l0f", L0FFN_SPEC)
    Wl1a = declare_weights(nc, "l1a", L1ATT_SPEC)
    Wl1f = declare_weights(nc, "l1f", L1FFN_SPEC)
    Wl2a = declare_weights(nc, "l2a", {k: s for k, s in L2ATT_SPEC.items() if k != "rope"})
    Wl2f = declare_weights(nc, "l2f", L2FFN_SPEC)
    Wl3a = declare_weights(nc, "l3a", L3ATT_SPEC)
    Wl3f = declare_weights(nc, "l3f", L3FFN_SPEC)
    emit_l0att(prog, C, chain[0], chain[1], Wl0a)
    emit_l0ffn(prog, C, chain[1], chain[2], Wl0f)
    emit_l1att(prog, C, chain[2], chain[3], Wl1a)
    emit_l1ffn(prog, C, chain[3], chain[4], Wl1f)
    emit_l2att(prog, C, chain[4], chain[5], Wl2a)
    emit_l2ffn(prog, C, chain[5], chain[6], Wl2f, n_cores=n_cores, mode=l2f_mode)
    emit_l3att(prog, C, chain[6], chain[7], Wl3a)
    emit_l3ffn(prog, C, chain[7], chain[8], Wl3f)


def build_module_A(C):
    nc, prog = _prog_setup()
    T = C["T"]
    x_in = nc.dram_tensor("x", [T, D], F32, kind="ExternalInput").ap()
    x5_out = nc.dram_tensor("y", [T, D], F32, kind="ExternalOutput").ap()
    bufs = [nc.dram_tensor(f"stream{i}", [T, D], F32, kind="Internal").ap()
            for i in range(4)]
    chain = [x_in] + bufs + [x5_out]
    Wl0a = declare_weights(nc, "l0a", {k: s for k, s in L0ATT_SPEC.items() if k != "ltau"})
    Wl0f = declare_weights(nc, "l0f", L0FFN_SPEC)
    Wl1a = declare_weights(nc, "l1a", L1ATT_SPEC)
    Wl1f = declare_weights(nc, "l1f", L1FFN_SPEC)
    Wl2a = declare_weights(nc, "l2a", {k: s for k, s in L2ATT_SPEC.items() if k != "rope"})
    Wl2f = declare_weights(nc, "l2f", {"gate_w": L2FFN_SPEC["gate_w"], "sim": L2FFN_SPEC["sim"]})
    emit_l0att(prog, C, chain[0], chain[1], Wl0a)
    emit_l0ffn(prog, C, chain[1], chain[2], Wl0f)
    emit_l1att(prog, C, chain[2], chain[3], Wl1a)
    emit_l1ffn(prog, C, chain[3], chain[4], Wl1f)
    emit_l2att(prog, C, chain[4], chain[5], Wl2a)
    emit_l2ffn(prog, C, chain[5], None, Wl2f, n_cores=1, mode="pre")
    prog.finalize()
    return nc


def build_module_B(C):
    nc, prog = _prog_setup()
    T = C["T"]
    x_in = nc.dram_tensor("x", [T, D], F32, kind="ExternalInput").ap()
    y_out = nc.dram_tensor("y", [T, D], F32, kind="ExternalOutput").ap()
    bufs = [nc.dram_tensor(f"stream{i}", [T, D], F32, kind="Internal").ap()
            for i in range(2)]
    chain = [x_in] + bufs + [y_out]
    Wl2f = declare_weights(nc, "l2f", {k: s for k, s in L2FFN_SPEC.items() if k != "gate_w"})
    Wl3a = declare_weights(nc, "l3a", L3ATT_SPEC)
    Wl3f = declare_weights(nc, "l3f", L3FFN_SPEC)
    emit_l2ffn(prog, C, chain[0], chain[1], Wl2f, n_cores=1, mode="post")
    emit_l3att(prog, C, chain[1], chain[2], Wl3a)
    emit_l3ffn(prog, C, chain[2], chain[3], Wl3f)
    prog.finalize()
    return nc



def kernel(**inputs):
    x = np.ascontiguousarray(inputs["L0_attn_hidden_states"], dtype=np.float32)
    B, L, _ = x.shape
    ncores = 8
    bper = B // ncores
    C = {"T": bper * L, "L": L, "ST": 1024}
    wmap = prep_all_weights(inputs)
    ncA = build_module_A(C)
    in_maps = []
    for ci in range(ncores):
        m = dict(wmap)
        m["x"] = np.ascontiguousarray(x[ci * bper:(ci + 1) * bper].reshape(bper * L, D))
        in_maps.append(m)
    resA = run_bass_kernel_spmd(ncA, in_maps, core_ids=list(range(ncores)))
    cnt_all = np.concatenate([resA.results[ci]["l2f_cnt_h"] for ci in range(ncores)], axis=0)
    ncB = build_module_B(C)
    in_mapsB = []
    for ci in range(ncores):
        m = dict(wmap)
        m["x"] = resA.results[ci]["y"]
        m["l2f_xfm_h"] = resA.results[ci]["l2f_xfm_h"]
        m["l2f_tstat_h"] = resA.results[ci]["l2f_tstat_h"]
        m["l2f_cnt_all"] = cnt_all
        in_mapsB.append(m)
    resB = run_bass_kernel_spmd(ncB, in_mapsB, core_ids=list(range(ncores)))
    shards = [resB.results[ci]["y"].reshape(bper, L, D) for ci in range(ncores)]
    return np.concatenate(shards, axis=0)